# version 25
# baseline (speedup 1.0000x reference)
import numpy as np
from contextlib import ExitStack

import concourse.bass as bass
import concourse.mybir as mybir
from concourse.bass_utils import run_bass_kernel_spmd
from bass_rust import AP as RawAP

F32 = mybir.dt.float32
BF16 = mybir.dt.bfloat16
OP = mybir.AluOpType
AF = mybir.ActivationFunctionType

PART = 128


def cdiv(a, b):
    return -(-a // b)


class Cfg:
    def __init__(self, H, I, T, n_cores, NCTRL, FT, AST, WQ, stop_after=None):
        self.H, self.I, self.T = H, I, T
        self.n_cores = n_cores
        self.NCTRL = NCTRL
        self.TOK = T // n_cores
        self.FT = FT
        self.AST = AST
        self.HB = H // PART
        self.IB = I // PART
        self.N = I * H
        self.Q = self.N // PART
        self.QC = self.Q // PART
        self.WQ = WQ
        self.NFT = self.TOK // FT
        self.NAST = self.TOK // AST
        self.QCs = self.QC // n_cores
        self.Qs = self.Q // n_cores
        self.SUBK = min(4, self.QCs // 4)
        self.stop_after = stop_after
        assert self.N % PART == 0 and self.Q % PART == 0
        assert self.N / (NCTRL - 1) >= PART, "a 128-tile may span >2 segments"
        assert T % n_cores == 0 and self.TOK % FT == 0 and self.TOK % AST == 0
        assert self.QC % 4 == 0 and self.Q % WQ == 0 and WQ % PART == 0
        assert self.QC % (4 * n_cores) == 0
        assert (self.QCs // 4) % self.SUBK == 0
        assert self.IB % n_cores == 0 and (self.IB // n_cores) % 2 == 0
        assert self.IB % 2 == 0


def full_cfg(stop_after=None):
    return Cfg(H=2048, I=8192, T=8192, n_cores=8, NCTRL=130156,
               FT=512, AST=128, WQ=512, stop_after=stop_after)


def host_grid(cfg):
    t = np.linspace(0.0, cfg.NCTRL - 1.0, cfg.N, dtype=np.float64)
    i = np.clip(np.floor(t).astype(np.int64), 0, cfg.NCTRL - 2)
    f = (t - i).astype(np.float32)
    f2 = f * f
    f3 = f2 * f
    it = i.reshape(cfg.Q, PART)
    jq = it[:, 0].copy()
    ok = (it == jq[:, None]) | (it == jq[:, None] + 1)
    assert ok.all(), "tile spans more than two segments"
    sq = (it == jq[:, None]).sum(1).astype(np.float32)
    return (f.reshape(cfg.Q, PART), f2.reshape(cfg.Q, PART),
            f3.reshape(cfg.Q, PART), jq, sq)


def host_pieces(jq):
    Q = len(jq)
    pieces = []
    q0 = 0
    for q in range(1, Q + 1):
        if q == Q or jq[q] != jq[q - 1] + 1:
            pieces.append((q0, q, int(jq[q0])))
            q0 = q
    return pieces


def piece_dmas(pieces, NCTRL, WQ):
    out = []
    for (q0, q1, j0) in pieces:
        q = q0
        while q < q1:
            win = q // WQ
            qe = min(q1, (win + 1) * WQ)
            L = qe - q
            j = j0 + (q - q0)
            if (j - 1) >= 0 and (j + L - 1 + 3) <= NCTRL - 1:
                out.append((win, q - win * WQ, L, ("full", j - 1)))
            else:
                rows = []
                for k in range(5):
                    runs = []
                    c = 0
                    while c < L:
                        idx = j + c + k - 1
                        if idx < 0:
                            ce = min(L, c + (0 - idx))
                            runs.append((c, ce - c, 0, 0))
                            c = ce
                        elif idx > NCTRL - 1:
                            runs.append((c, L - c, NCTRL - 1, 0))
                            c = L
                        else:
                            ce = min(L, c + ((NCTRL - 1) - idx) + 1)
                            runs.append((c, ce - c, idx, 1))
                            c = ce
                    rows.append((k, runs))
                out.append((win, q - win * WQ, L, ("rows", rows)))
            q = qe
    return out


def host_M():
    A = np.zeros((5, 4), dtype=np.float64)
    A[1, 0] = 1.0
    A[0, 1] = -0.5; A[2, 1] = 0.5
    A[0, 2] = 1.0; A[1, 2] = -2.5; A[2, 2] = 2.0; A[3, 2] = -0.5
    A[0, 3] = -0.5; A[1, 3] = 1.5; A[2, 3] = -1.5; A[3, 3] = 0.5
    B = np.zeros((5, 4), dtype=np.float64)
    B[1:5, :] = A[0:4, :]
    return np.concatenate([A, B - A], axis=1).astype(np.float32)


def down_perm(cfg):
    hb = np.arange(cfg.HB)
    ib = np.arange(cfg.IB)
    h_off = np.arange(PART)
    p = ((hb[None, :, None] * PART + h_off[None, None, :]) * cfg.IB
         + ib[:, None, None])
    return p.reshape(-1)


def tile_lhsT(Wt):
    K, M = Wt.shape
    a = Wt.reshape(K // PART, PART, M // PART, PART)
    a = a.transpose(2, 1, 0, 3)
    return np.ascontiguousarray(a.reshape(M // PART, PART, K))


ENGS = ("sync", "tensor", "vector", "scalar", "gpsimd")


class Chan:
    def __init__(self, stack, nc, name, dma=False):
        self.sem = stack.enter_context(nc.semaphore(name))
        self.count = 0
        self.dma = dma


class Emitter:
    def __init__(self, stack, nc):
        self.nc = nc
        self.ops = {e: [] for e in ENGS}
        self.seq = {e: 0 for e in ENGS}
        self.last_drain = {e: -1 for e in ENGS}
        self.waited = {e: {} for e in ENGS}
        self.chans = {}
        for name in ("ld_x", "ld_ftA", "ld_ftB", "ld_phA", "ld_phB",
                     "ld_pdA", "ld_pdB", "ld_phAs", "ld_phBs",
                     "ld_cdA", "ld_cdB",
                     "ld_cgA", "ld_cgB",
                     "ld_guA", "ld_guB", "ld_dw",
                     "ld_w0", "ld_w1", "ld_w2", "ld_w3", "ld_gth", "ld_ob",
                     "st_scr", "st_cimg", "st_prm", "st_carry", "st_y",
                     "st_sp",
                     "pe", "dv", "ac", "po", "cc"):
            self.chans[name] = Chan(stack, nc, name,
                                    dma=name.startswith(("ld_", "st_")))

    def ch(self, name):
        return self.chans[name]

    def now(self, chan_name):
        c = self.chans[chan_name]
        return ("sem", c, c.count)

    def need(self, eng, *tokens):
        drains = False
        waits = []
        for tok in tokens:
            if tok is None:
                continue
            if tok[0] == "sem":
                _, chan, val = tok
                if chan.dma and val > 0:
                    val = chan.count
                if val > 0 and self.waited[eng].get(id(chan), 0) < val:
                    self.waited[eng][id(chan)] = val
                    waits.append((chan.sem, val))
            elif tok[0] == "eng":
                _, e2, seq = tok
                assert e2 == eng, f"cross-engine eng token {e2}->{eng}"
                if seq > self.last_drain[eng]:
                    drains = True
            else:
                raise RuntimeError(f"bad token {tok}")
        if drains:
            self.last_drain[eng] = self.seq[eng]
            self.ops[eng].append(lambda E: E.drain())
            self.seq[eng] += 1
        for sem, val in waits:
            self.ops[eng].append(lambda E, s=sem, v=val: E.wait_ge(s, v))

    def drain(self, eng):
        self.last_drain[eng] = self.seq[eng]
        self.ops[eng].append(lambda E: E.drain())
        self.seq[eng] += 1

    def emit(self, eng, fn, inc=None, dinc=1):
        if inc is not None:
            chan = self.chans[inc]
            chan.count += dinc
            val = chan.count
            self.ops[eng].append(
                lambda E, f=fn, c=chan, n=dinc: f(E).then_inc(c.sem, n))
            self.seq[eng] += 1
            return ("sem", chan, val)
        self.ops[eng].append(lambda E, f=fn: f(E))
        self.seq[eng] += 1
        return ("eng", eng, self.seq[eng] - 1)


def build_program(cfg):
    nc = bass.Bass("TRN2", target_bir_lowering=False)
    H, I, TOK, HB, IB = cfg.H, cfg.I, cfg.TOK, cfg.HB, cfg.IB
    FT, AST, Q, QC, WQ, NCTRL = cfg.FT, cfg.AST, cfg.Q, cfg.QC, cfg.WQ, cfg.NCTRL
    n_cores = cfg.n_cores

    _, _, _, jq, _ = host_grid(cfg)
    pdmas = piece_dmas(host_pieces(jq), NCTRL, WQ)
    pdmas_by_win = {}
    for pd in pdmas:
        pdmas_by_win.setdefault(pd[0], []).append(pd)

    def din(name, shape):
        return nc.dram_tensor(name, list(shape), F32, kind="ExternalInput")

    x_in = din("x_in", (H, TOK))
    wiT = din("wiT", (HB, PART, HB * PART))
    wfT = din("wfT", (HB, PART, HB * PART))
    wgT = din("wgT", (HB, PART, HB * PART))
    woT = din("woT", (HB, PART, HB * PART))
    gnw_in = din("gnw", (PART, H // PART))
    cp_g = din("cp_g", (NCTRL,))
    cp_u = din("cp_u", (NCTRL,))
    cp_d = din("cp_d", (NCTRL,))
    ftab = din("ftab", (cfg.QCs // 4, PART, 512))
    f2tab = din("f2tab", (cfg.QCs // 4, PART, 512))
    f3tab = din("f3tab", (cfg.QCs // 4, PART, 512))
    fdtab = din("fdtab", (cfg.QCs // 4, PART, 512))
    f2dtab = din("f2dtab", (cfg.QCs // 4, PART, 512))
    f3dtab = din("f3dtab", (cfg.QCs // 4, PART, 512))
    stab = din("stab", (PART, cfg.QCs))
    sdtab = din("sdtab", (PART, cfg.QCs))
    iota_in = din("iota", (PART, PART))
    eye_in = din("eye", (PART, PART))
    ones_col = din("ones_col", (PART, 1))
    ones_row = din("ones_row", (1, PART))
    M_gu_in = din("M_gu", (10, 16))
    M_d_in = din("M_d", (5, 8))
    csel_in = din("csel", (PART, n_cores * HB))

    y_out = nc.dram_tensor("y_out", [H, TOK], F32, kind="ExternalOutput")

    _aspace = {"addr_space": "Shared"} if n_cores > 4 else {}
    gate_scr = nc.dram_tensor("gate_scr", [2 * QC * PART * PART], BF16,
                              **_aspace)
    up_scr = nc.dram_tensor("up_scr", [2 * QC * PART * PART], BF16,
                            **_aspace)
    down_scr = nc.dram_tensor("down_scr", [2 * QC * PART * PART], BF16,
                              **_aspace)
    Qs, QCs, SUBK = cfg.Qs, cfg.QCs, cfg.SUBK
    QsS = Qs // SUBK
    IBs = IB // n_cores
    gsh = nc.dram_tensor("gsh", [2 * Qs * PART], BF16)
    ush = nc.dram_tensor("ush", [2 * Qs * PART], BF16)
    dsh = nc.dram_tensor("dsh", [2 * Qs * PART], BF16)
    cimg_gu = nc.dram_tensor("cimg_gu", [Q * 16], F32)
    cimg_d2 = nc.dram_tensor("cimg_d2", [Q * 8], F32)
    cglo = nc.dram_tensor("cglo", [Qs * 16], F32)
    cdlo = nc.dram_tensor("cdlo", [Qs * 8], F32)
    cimg = nc.dram_tensor("cimg", [Q * 8], F32)
    carry_src = nc.dram_tensor("carry_src", [PART, 2 * HB], F32)
    carry_gth = nc.dram_tensor("carry_gth", [n_cores * PART, 2 * HB], F32)
    o_scr = nc.dram_tensor("o_scr", [PART, HB * TOK], F32)
    p_scr = nc.dram_tensor("p_scr", [PART, HB * TOK], F32)

    stack = ExitStack()
    with stack:
        def sb(name, p, n):
            return stack.enter_context(nc.sbuf_tensor(name, [p, n], F32))

        x_sb = sb("x_sb", PART, HB * TOK)
        iota_sb = sb("iota_sb", PART, PART)
        eye_sb = sb("eye_sb", PART, PART)
        onec_sb = sb("onec_sb", PART, 1)
        oner_sb = sb("oner_sb", 1, PART)
        gnw_sb = sb("gnw_sb", PART, HB)
        mgu_sb = sb("mgu_sb", 10, 16)
        md_sb = sb("md_sb", 5, 8)
        csel_sb = sb("csel_sb", PART, n_cores * HB)
        def sbb(name, p, n):
            return stack.enter_context(nc.sbuf_tensor(name, [p, n], BF16))

        lnx = sbb("lnx", PART, 2 * HB * FT)
        inter = sbb("inter", PART, 2 * 2 * FT)
        intu = sb("intu", PART, FT)
        lnu = intu[:, 0:FT]
        guwin = sbb("guwin", PART, 2 * 4 * HB * PART)
        arena = sb("arena", PART, max(4 * HB * AST, 2 * HB * PART))
        wext = (sb("wext", PART, 4 * HB * PART)
                if HB * FT < 4 * HB * PART else None)
        atile = arena[:, 0:HB * AST]
        vtile = arena[:, HB * AST:2 * HB * AST]
        otile = arena[:, 2 * HB * AST:3 * HB * AST]
        lnxa = arena[:, 3 * HB * AST:4 * HB * AST]
        dwin = arena[:, 0:2 * HB * PART].bitcast(BF16)
        phat = sb("phat", 10, 2 * WQ)
        phd = sb("phd", 5, 2 * WQ)
        ftileA = sb("ftileA", PART, 3 * 512)
        ftileB = sb("ftileB", PART, 3 * 512)
        sringA = sb("sringA", PART, 4)
        sringB = sb("sringB", PART, 4)
        htile = [[sb(f"htile{e}{k}", PART, 4 * PART) for k in range(2)]
                 for e in range(1)]
        accT = [[sb(f"acc{e}_{k}", PART, PART) for k in range(4)]
                for e in range(1)]
        wqT = [[sb(f"wq{e}_{k}", PART, PART) for k in range(6)]
               for e in range(1)]
        cgu = sb("cgu", PART, 2 * 64)
        cdn = sb("cdn", PART, 2 * 32)
        tstage = sbb("tstage", PART, 4 * 512)
        pstage = sb("pstage", PART, 64)
        pbuf = intu[:, FT - AST:FT]
        carry = sb("carry", PART, 2 * HB)
        carry2 = sb("carry2", PART, HB)
        cg_sb = sb("cg_sb", PART, n_cores * 2 * HB)
        sc_sb = sb("sc_sb", PART, 3 * HB)
        bcA = sb("bcA", PART, max(FT, AST))
        bcB = sb("bcB", PART, max(FT, AST))
        lnrowA = bcB[0:1, :]
        lnrowB = bcA[0:1, :]

        ps = [stack.enter_context(nc.psum_tensor(f"ps{k}", [PART, 512], F32))
              for k in range(8)]

        em = Emitter(stack, nc)
        PE, DV, AC = "pe", "dv", "ac"

        bank_rd = {k: None for k in range(8)}

        def dma(eng_name, out_ap, in_ap, ch):
            return em.emit(eng_name,
                           lambda E, o=out_ap, i=in_ap: E.dma_start(out=o, in_=i),
                           inc=ch, dinc=16)

        tok_x = dma("sync", x_sb[:, :],
                    RawAP(x_in, 0, [[TOK, PART], [PART * TOK, HB], [1, TOK]]),
                    "ld_x")
        dma("sync", iota_sb[:, :], iota_in.ap(), "ld_x")
        dma("sync", eye_sb[:, :], eye_in.ap(), "ld_x")
        dma("sync", onec_sb[:, :], ones_col.ap(), "ld_x")
        dma("sync", oner_sb[:, :], ones_row.ap(), "ld_x")
        dma("sync", gnw_sb[:, :], gnw_in.ap(), "ld_x")
        dma("sync", mgu_sb[:, :], M_gu_in.ap(), "ld_x")
        dma("sync", md_sb[:, :], M_d_in.ap(), "ld_x")
        tok_consts = dma("sync", csel_sb[:, :], csel_in.ap(), "ld_x")

        PIECE_Q = ("gpsimd", "sync")

        def emit_phat_window(win, dest_sb, half, cps, war_tok, ch):
            toks = []
            base = half * WQ
            for wi, cph in enumerate(cps):
                q = PIECE_Q[wi % len(PIECE_Q)]
                em.need(q, war_tok)
                r0 = wi * 5
                chq = ch + ("", "s")[wi % len(PIECE_Q)]
                tok = None
                for (_, col0, L, spec) in pdmas_by_win.get(win, []):
                    if spec[0] == "full":
                        sap = RawAP(cph, spec[1], [[1, 5], [1, L]])
                        tok = dma(q,
                                  dest_sb[r0:r0 + 5,
                                          base + col0:base + col0 + L],
                                  sap, chq)
                    else:
                        for (k, runs) in spec[1]:
                            for (c, ln, off, step) in runs:
                                dims = ([[step, ln], [1, 1]] if step == 0
                                        else [[step, ln]])
                                sap = RawAP(cph, off, dims)
                                tok = dma(
                                    q,
                                    dest_sb[r0 + k:r0 + k + 1,
                                            base + col0 + c:
                                            base + col0 + c + ln],
                                    sap, chq)
                toks.append(tok)
            return toks

        NW = Q // WQ
        CPW = WQ // PART
        BATCH = min(16, CPW)
        GS = (QCs // 4) // SUBK

        pp_done = [None]

        def emit_prepass(ph_sb, cps, chA, chB, msb, width, dst_dram, blocked,
                         interleave=False, other=None):
            ph_tok = [None, None]
            ph_rd = [None, None]
            for win in range(NW):
                half = win % 2
                ph_tok[half] = emit_phat_window(
                    win, ph_sb, half, cps, ph_rd[half],
                    chA if half == 0 else chB)
                for b0 in range(0, CPW, BATCH):
                    nb = min(BATCH, CPW - b0)
                    done = pp_done[0]
                    em.need("tensor", *ph_tok[half])
                    em.need("tensor", tok_consts, bank_rd[4])
                    for b in range(nb):
                        cq = b0 + b
                        lhs = ph_sb[:, half * WQ + cq * PART:
                                    half * WQ + (cq + 1) * PART]
                        em.emit("tensor",
                                lambda E, l=lhs, b_=b, w_=width, m_=msb:
                                E.matmul(ps[4][:, b_ * w_:(b_ + 1) * w_],
                                         l, m_[:, :], start=True, stop=True),
                                inc=PE if b == nb - 1 else None)
                    tok_mm = em.now(PE)
                    ph_rd[half] = tok_mm
                    em.need("scalar", tok_mm, done)
                    tok_ev = em.emit("scalar",
                                     lambda E, n_=nb * width: E.copy(
                                         out=pstage[:, 0:n_],
                                         in_=ps[4][:, 0:n_]),
                                     inc=AC)
                    bank_rd[4] = tok_ev
                    em.need("scalar", tok_ev)
                    cq0 = win * CPW + b0
                    if blocked:
                        r = cq0 // QCs
                        cqloc = cq0 % QCs
                        dst = RawAP(dst_dram,
                                    r * PART * QCs * width + cqloc * width,
                                    [[QCs * width, PART], [1, nb * width]])
                    else:
                        dst = RawAP(dst_dram, cq0 * width,
                                    [[QC * width, PART], [1, nb * width]])
                    pp_done[0] = dma("scalar", dst, pstage[:, 0:nb * width],
                                     "st_cimg")
                if interleave:
                    if other is not None:
                        next(other, None)
                    yield pp_done[0]
            if other is not None:
                for _ in other:
                    yield pp_done[0]
            yield pp_done[0]

        pp_gu = emit_prepass(phat, [cp_g, cp_u], "ld_phA", "ld_phB",
                             mgu_sb, 16, cimg_gu, True, interleave=True)
        pp_d = emit_prepass(phd, [cp_d], "ld_pdA", "ld_pdB",
                            md_sb, 8, cimg, False, interleave=True,
                            other=pp_gu)
        for _ in pp_d:
            pass
        pp1 = pp2 = pp_done[0]

        Pc = 128 // IB if IB <= 128 else 1
        assert IB <= 128 and 128 % IB == 0
        em.need("gpsimd", pp1, pp2)
        prm = None
        for ib in range(IB):
            r = ib // IBs
            dbase = r * PART * QCs * 8 + (ib % IBs) * HB * 8
            for B in range(Pc):
                dst = RawAP(cimg_d2, dbase + B * QCs * 8,
                            [[Pc * QCs * 8, IB], [1, HB * 8]])
                sa = RawAP(cimg, (ib + IB * B) * QC * 8,
                           [[8, IB], [IB * 8, HB], [1, 8]])
                prm = dma("gpsimd", dst, sa, "st_prm")
        em.need("gpsimd", prm)
        tok_rs_gu = em.emit("gpsimd", lambda E: E.collective_compute(
            "ReduceScatter", OP.add, replica_groups=[list(range(n_cores))],
            ins=[cimg_gu.ap().opt()], outs=[cglo.ap().opt()]), inc="cc")
        tok_rs_d = em.emit("gpsimd", lambda E: E.collective_compute(
            "ReduceScatter", OP.add, replica_groups=[list(range(n_cores))],
            ins=[cimg_d2.ap().opt()], outs=[cdlo.ap().opt()]), inc="cc")

        lnxa_rd = [None]

        def emit_ln(src_off, ntok, pair):
            pe_bar = em.now(PE) if pair else lnxa_rd[0]
            dv_bar = em.now(DV)
            st_bar = em.now("st_cimg")
            for c in range(HB):
                em.need("tensor", tok_x, tok_consts,
                        bank_rd[7] if c == 0 else None,
                        pe_bar if c == 0 else None)
                em.emit("tensor",
                        lambda E, c_=c, n_=ntok, o_=src_off: E.matmul(
                            ps[7][0:1, 0:n_], onec_sb[:, :],
                            x_sb[:, c_ * TOK + o_:c_ * TOK + o_ + n_],
                            start=(c_ == 0), stop=(c_ == HB - 1)),
                        inc=PE if c == HB - 1 else None)
            em.need("vector", em.now(PE))
            em.drain("vector")
            tok_mean = em.emit("vector", lambda E, n_=ntok: E.tensor_scalar(
                out=lnrowA[0:1, 0:n_], in0=ps[7][0:1, 0:n_],
                scalar1=1.0 / H, scalar2=None, op0=OP.mult), inc=DV)
            sq_war = None
            for c in range(HB):
                em.need("scalar", tok_x, sq_war,
                        dv_bar if c == 0 else None,
                        st_bar if c == 0 else None)
                t = em.emit("scalar",
                            lambda E, c_=c, n_=ntok, o_=src_off: E.activation(
                                out=intu[:, 0:n_],
                                in_=x_sb[:, c_ * TOK + o_:c_ * TOK + o_ + n_],
                                func=AF.Square),
                            inc=AC)
                em.need("tensor", t, tok_mean if c == 0 else None)
                sq_war = em.emit("tensor",
                                 lambda E, n_=ntok, first=(c == 0),
                                 last=(c == HB - 1): E.matmul(
                                     ps[7][0:1, 0:n_], onec_sb[:, :],
                                     intu[:, 0:n_], start=first, stop=last),
                                 inc=PE)
            em.need("vector", em.now(PE))
            em.emit("vector", lambda E, n_=ntok: E.tensor_scalar(
                out=lnrowB[0:1, 0:n_], in0=ps[7][0:1, 0:n_],
                scalar1=1.0 / H, scalar2=None, op0=OP.mult))
            em.drain("vector")
            em.emit("vector", lambda E, n_=ntok: E.tensor_tensor(
                out=intu[0:1, 0:n_], in0=lnrowA[0:1, 0:n_],
                in1=lnrowA[0:1, 0:n_], op=OP.mult))
            em.drain("vector")
            em.emit("vector", lambda E, n_=ntok: E.tensor_tensor(
                out=lnrowB[0:1, 0:n_], in0=lnrowB[0:1, 0:n_],
                in1=intu[0:1, 0:n_], op=OP.subtract))
            em.drain("vector")
            tok_var = em.emit("vector", lambda E, n_=ntok: E.tensor_scalar(
                out=lnrowB[0:1, 0:n_], in0=lnrowB[0:1, 0:n_],
                scalar1=1.0, scalar2=1e-5, op0=OP.mult, op1=OP.add), inc=DV)
            em.need("scalar", tok_var)
            tok_sq = em.emit("scalar", lambda E, n_=ntok: E.activation(
                out=lnrowB[0:1, 0:n_], in_=lnrowB[0:1, 0:n_], func=AF.Sqrt),
                inc=AC)
            em.need("vector", tok_sq)
            tok_a = em.emit("vector", lambda E, n_=ntok: E.reciprocal(
                out=lnrowB[0:1, 0:n_], in_=lnrowB[0:1, 0:n_]), inc=DV)
            em.drain("vector")
            em.emit("vector", lambda E, n_=ntok: E.tensor_tensor(
                out=lnrowA[0:1, 0:n_], in0=lnrowA[0:1, 0:n_],
                in1=lnrowB[0:1, 0:n_], op=OP.mult))
            em.drain("vector")
            tok_b = em.emit("vector", lambda E, n_=ntok: E.tensor_scalar(
                out=lnrowA[0:1, 0:n_], in0=lnrowA[0:1, 0:n_],
                scalar1=-1.0, scalar2=None, op0=OP.mult), inc=DV)
            em.need("tensor", tok_a, tok_b)
            em.emit("tensor", lambda E, n_=ntok: E.matmul(
                ps[7][:, 0:n_], oner_sb[:, :], lnrowB[0:1, 0:n_],
                start=True, stop=True), inc=PE)
            em.need("vector", em.now(PE))
            tok_evA = em.emit("vector", lambda E, n_=ntok: E.tensor_copy(
                out=bcA[:, 0:n_], in_=ps[7][:, 0:n_]), inc=DV)
            em.need("tensor", tok_evA)
            em.emit("tensor", lambda E, n_=ntok: E.matmul(
                ps[7][:, 0:n_], oner_sb[:, :], lnrowA[0:1, 0:n_],
                start=True, stop=True), inc=PE)
            em.need("vector", em.now(PE))
            tok_evB = em.emit("vector", lambda E, n_=ntok: E.tensor_copy(
                out=bcB[:, 0:n_], in_=ps[7][:, 0:n_]), inc=DV)
            bank_rd[7] = tok_evB
            last = None
            if pair:
                for c in range(HB):
                    em.need("vector", tok_evA, tok_evB, last)
                    em.emit("vector", lambda E, c_=c, n_=ntok, o_=src_off:
                            E.tensor_tensor(
                                out=lnu[:, 0:n_],
                                in0=x_sb[:, c_ * TOK + o_:c_ * TOK + o_ + n_],
                                in1=bcA[:, 0:n_], op=OP.mult))
                    em.drain("vector")
                    tok_u = em.emit("vector", lambda E, n_=ntok:
                                    E.tensor_tensor(
                                        out=lnu[:, 0:n_], in0=lnu[:, 0:n_],
                                        in1=bcB[:, 0:n_], op=OP.add), inc=DV)
                    em.need("scalar", tok_u)
                    tok_h = em.emit("scalar", lambda E, c_=c, n_=ntok:
                                    E.copy(out=lnx[:, c_ * n_:(c_ + 1) * n_],
                                           in_=lnu[:, 0:n_]), inc=AC)
                    em.need("vector", tok_h)
                    last = em.emit("vector", lambda E, c_=c, n_=ntok:
                                   E.tensor_tensor(
                                       out=lnx[:, HB * FT + c_ * n_:
                                               HB * FT + (c_ + 1) * n_],
                                       in0=lnu[:, 0:n_],
                                       in1=lnx[:, c_ * n_:(c_ + 1) * n_],
                                       op=OP.subtract), inc=DV)
            else:
                for c in range(HB):
                    em.need("vector", tok_evA, tok_evB)
                    em.emit("vector", lambda E, c_=c, n_=ntok, o_=src_off:
                            E.tensor_tensor(
                                out=lnxa[:, c_ * n_:(c_ + 1) * n_],
                                in0=x_sb[:, c_ * TOK + o_:c_ * TOK + o_ + n_],
                                in1=bcA[:, 0:n_], op=OP.mult))
                    em.drain("vector")
                    last = em.emit("vector", lambda E, c_=c, n_=ntok:
                                   E.tensor_tensor(
                                       out=lnxa[:, c_ * n_:(c_ + 1) * n_],
                                       in0=lnxa[:, c_ * n_:(c_ + 1) * n_],
                                       in1=bcB[:, 0:n_], op=OP.add),
                                   inc=DV if c == HB - 1 else None)
            return last

        def emit_hprep(eng, fsb, foff, srng, sidx, ht):
            em.emit(eng, lambda E: E.tensor_scalar(
                out=ht[:, 0:PART], in0=iota_sb[:, :],
                scalar1=srng[:, sidx:sidx + 1], scalar2=None, op0=OP.is_ge))

        def emit_hprep2(eng, fsb, foff, ht):
            f_ap = fsb[:, foff:foff + PART]
            f2_ap = fsb[:, 512 + foff:512 + foff + PART]
            f3_ap = fsb[:, 1024 + foff:1024 + foff + PART]
            em.emit(eng, lambda E: E.tensor_tensor(
                out=ht[:, PART:2 * PART], in0=ht[:, 0:PART],
                in1=f_ap, op=OP.mult))
            em.emit(eng, lambda E: E.tensor_tensor(
                out=ht[:, 2 * PART:3 * PART], in0=ht[:, 0:PART],
                in1=f2_ap, op=OP.mult))
            em.emit(eng, lambda E: E.tensor_tensor(
                out=ht[:, 3 * PART:4 * PART], in0=ht[:, 0:PART],
                in1=f3_ap, op=OP.mult))

        def emit_chain_group(eng, ch_inc, fsb, chains):
            def co(csb, coff, x):
                return csb[:, coff + x:coff + x + 1]

            for (foff, ht, csb, coff, acc, wq) in chains:
                f_ap = fsb[:, foff:foff + PART]
                em.emit(eng, lambda E, f=f_ap, c1=co(csb, coff, 1),
                        c0=co(csb, coff, 0), a=acc: E.tensor_scalar(
                            out=a[:, :], in0=f, scalar1=c1, scalar2=c0,
                            op0=OP.mult, op1=OP.add))
            for step, (sel, cx) in enumerate(
                    (("f2", 2), ("f3", 3), ("H", 4), ("fH", 5), ("f2H", 6))):
                em.drain(eng)
                for (foff, ht, csb, coff, acc, wq) in chains:
                    if sel == "f2":
                        tbl = fsb[:, 512 + foff:512 + foff + PART]
                    elif sel == "f3":
                        tbl = fsb[:, 1024 + foff:1024 + foff + PART]
                    elif sel == "H":
                        tbl = ht[:, 0:PART]
                    elif sel == "fH":
                        tbl = ht[:, PART:2 * PART]
                    else:
                        tbl = ht[:, 2 * PART:3 * PART]
                    em.emit(eng, lambda E, t=tbl, c_=co(csb, coff, cx),
                            a=acc: E.scalar_tensor_tensor(
                                out=a[:, :], in0=t, scalar=c_, in1=a[:, :],
                                op0=OP.mult, op1=OP.add))
            em.drain(eng)
            toks = []
            for (foff, ht, csb, coff, acc, wq) in chains:
                toks.append(em.emit(eng, lambda E, t=ht[:, 3 * PART:4 * PART],
                                    c_=co(csb, coff, 7), a=acc, w=wq:
                                    E.scalar_tensor_tensor(
                                        out=w[:, :], in0=t, scalar=c_,
                                        in1=a[:, :], op0=OP.mult, op1=OP.add),
                                    inc=ch_inc))
            return toks

        gu_win_state = {"win": -1, "tok": [None, None], "rd": [None, None]}
        stage_scr_gu = {}
        stage_scr_dn = {}
        ftile_rd = [None, None]
        cgu_rd = [None, None]
        cdn_rd = [None, None]
        wq_tr = [[None] * 6 for _ in range(1)]
        wq_ring = [0]
        tstage_rd = [None]

        def load_ftile(kind, grp, flip):
            buf = ftileA if flip == 0 else ftileB
            srng = sringA if flip == 0 else sringB
            srcs = ((ftab, f2tab, f3tab) if kind == "gu"
                    else (fdtab, f2dtab, f3dtab))
            stsrc = stab if kind == "gu" else sdtab
            ch = "ld_ftA" if flip == 0 else "ld_ftB"
            em.need("sync", ftile_rd[flip])
            for ti, tsrc in enumerate(srcs):
                src = RawAP(tsrc, (grp // 4) * PART * 512,
                            [[512, PART], [1, 512]])
                dma("sync", buf[:, ti * 512:(ti + 1) * 512], src, ch)
            tok = dma("sync", srng[:, 0:4],
                      RawAP(stsrc, grp, [[QCs, PART], [1, 4]]), ch)
            return buf, srng, tok

        def emit_gu_stage(s):
            ring = s % 2
            ch = "ld_cgA" if ring == 0 else "ld_cgB"
            em.need("scalar", tok_rs_gu, cgu_rd[ring])
            tok_cg = dma("scalar", cgu[:, ring * 64:(ring + 1) * 64],
                         RawAP(cglo, 4 * s * 16, [[QCs * 16, PART], [1, 64]]),
                         ch)
            flip = s % 2
            buf, srng, tok_ft = load_ftile("gu", 4 * s, flip)
            ei = 0
            eng = "vector"
            ch_inc = DV
            last_chain = None
            for pair in range(2):
                b0, b1 = 2 * pair, 2 * pair + 1
                r0 = wq_ring[ei]
                wq_ring[ei] = (r0 + 4) % 6
                em.need(eng, tok_ft, tok_cg,
                        *[wq_tr[ei][(r0 + k) % 6] for k in range(4)])
                em.drain(eng)
                emit_hprep(eng, buf, b0 * PART, srng, b0, htile[ei][0])
                emit_hprep(eng, buf, b1 * PART, srng, b1, htile[ei][1])
                em.drain(eng)
                emit_hprep2(eng, buf, b0 * PART, htile[ei][0])
                emit_hprep2(eng, buf, b1 * PART, htile[ei][1])
                em.drain(eng)
                chains = [
                    (b0 * PART, htile[ei][0], cgu, ring * 64 + b0 * 16 + 0,
                     accT[ei][0], wqT[ei][r0]),
                    (b0 * PART, htile[ei][0], cgu, ring * 64 + b0 * 16 + 8,
                     accT[ei][1], wqT[ei][(r0 + 1) % 6]),
                    (b1 * PART, htile[ei][1], cgu, ring * 64 + b1 * 16 + 0,
                     accT[ei][2], wqT[ei][(r0 + 2) % 6]),
                    (b1 * PART, htile[ei][1], cgu, ring * 64 + b1 * 16 + 8,
                     accT[ei][3], wqT[ei][(r0 + 3) % 6]),
                ]
                toks = emit_chain_group(eng, ch_inc, buf, chains)
                for ci, (b_, wi) in enumerate(((b0, 0), (b0, 1), (b1, 0), (b1, 1))):
                    bank_id = 5 if wi == 0 else 6
                    em.need("tensor", toks[ci],
                            bank_rd[bank_id] if (pair == 0 and b_ == b0) else None)
                    tr = em.emit(
                        "tensor", lambda E, w_=chains[ci][5], bi=bank_id, bb=b_:
                        E.transpose(ps[bi][:, bb * PART:(bb + 1) * PART],
                                    w_[:, :], eye_sb[:, :]), inc=PE)
                    wq_tr[ei][(r0 + ci) % 6] = tr
                last_chain = toks[-1]
            cgu_rd[ring] = last_chain
            ftile_rd[flip] = last_chain
            tok_tr = em.now(PE)
            em.need("scalar", tok_tr, tstage_rd[0])
            tok_h5 = em.emit("scalar", lambda E: E.copy(
                out=tstage[:, 0:512], in_=ps[5][:, :]), inc=AC)
            tok_h6 = em.emit("scalar", lambda E: E.copy(
                out=tstage[:, 1024:1536], in_=ps[6][:, :]), inc=AC)
            em.need("vector", tok_h5, tok_h6, tstage_rd[0])
            tok_l5 = em.emit("vector", lambda E: E.tensor_tensor(
                out=tstage[:, 512:1024], in0=ps[5][:, :],
                in1=tstage[:, 0:512], op=OP.subtract), inc=DV)
            tok_l6 = em.emit("vector", lambda E: E.tensor_tensor(
                out=tstage[:, 1536:2048], in0=ps[6][:, :],
                in1=tstage[:, 1024:1536], op=OP.subtract), inc=DV)
            bank_rd[5] = tok_l5
            bank_rd[6] = tok_l6
            em.need("scalar", tok_l5, tok_l6)
            k = s // GS
            off = k * 2 * PART * QsS + (4 * s * PART - k * QsS)
            for pl, (sh_, t0) in enumerate(((gsh, 0), (gsh, 512),
                                            (ush, 1024), (ush, 1536))):
                plane = pl % 2
                dst = RawAP(sh_, off + plane * PART * QsS,
                            [[QsS, PART], [1, 512]])
                dma("scalar", dst, tstage[:, t0:t0 + 512], "st_scr")
            em.need("scalar", em.now("st_scr"))
            tok = em.emit("scalar", lambda E: E.nop(), inc=AC)
            tstage_rd[0] = tok
            return tok

        def emit_dn_stage(s):
            ring = s % 2
            ch = "ld_cdA" if ring == 0 else "ld_cdB"
            em.need("scalar", tok_rs_d, cdn_rd[ring])
            tok_cd = dma("scalar", cdn[:, ring * 32:(ring + 1) * 32],
                         RawAP(cdlo, 4 * s * 8, [[QCs * 8, PART], [1, 32]]),
                         ch)
            flip = s % 2
            buf, srng, tok_ft = load_ftile("dn", 4 * s, flip)
            ei = 0
            eng = "vector"
            ch_inc = DV
            last_chain = None
            for pair in range(2):
                b0, b1 = 2 * pair, 2 * pair + 1
                r0 = wq_ring[ei]
                wq_ring[ei] = (r0 + 2) % 6
                em.need(eng, tok_ft, tok_cd,
                        wq_tr[ei][r0], wq_tr[ei][(r0 + 1) % 6])
                em.drain(eng)
                emit_hprep(eng, buf, b0 * PART, srng, b0, htile[ei][0])
                emit_hprep(eng, buf, b1 * PART, srng, b1, htile[ei][1])
                em.drain(eng)
                emit_hprep2(eng, buf, b0 * PART, htile[ei][0])
                emit_hprep2(eng, buf, b1 * PART, htile[ei][1])
                em.drain(eng)
                chains = [
                    (b0 * PART, htile[ei][0], cdn, ring * 32 + b0 * 8,
                     accT[ei][0], wqT[ei][r0]),
                    (b1 * PART, htile[ei][1], cdn, ring * 32 + b1 * 8,
                     accT[ei][1], wqT[ei][(r0 + 1) % 6]),
                ]
                toks = emit_chain_group(eng, ch_inc, buf, chains)
                for ci, bb in enumerate((b0, b1)):
                    em.need("tensor", toks[ci],
                            bank_rd[5] if (pair == 0 and ci == 0) else None)
                    wq_tr[ei][(r0 + ci) % 6] = em.emit(
                        "tensor", lambda E, w_=chains[ci][5], bb_=bb:
                        E.transpose(ps[5][:, bb_ * PART:(bb_ + 1) * PART],
                                    w_[:, :], eye_sb[:, :]), inc=PE)
                last_chain = toks[-1]
            cdn_rd[ring] = last_chain
            ftile_rd[flip] = last_chain
            em.need("scalar", em.now(PE), tstage_rd[0])
            tok_h = em.emit("scalar", lambda E: E.copy(
                out=tstage[:, 0:512], in_=ps[5][:, :]), inc=AC)
            em.need("vector", tok_h, tstage_rd[0])
            tok_l = em.emit("vector", lambda E: E.tensor_tensor(
                out=tstage[:, 512:1024], in0=ps[5][:, :],
                in1=tstage[:, 0:512], op=OP.subtract), inc=DV)
            bank_rd[5] = tok_l
            em.need("scalar", tok_l)
            k = s // GS
            off = k * 2 * PART * QsS + (4 * s * PART - k * QsS)
            for plane in range(2):
                dst = RawAP(dsh, off + plane * PART * QsS,
                            [[QsS, PART], [1, 512]])
                dma("scalar", dst, tstage[:, plane * 512:plane * 512 + 512],
                    "st_scr")
            em.need("scalar", em.now("st_scr"))
            tok = em.emit("scalar", lambda E: E.nop(), inc=AC)
            tstage_rd[0] = tok
            return tok

        ag_g = {}
        ag_u = {}
        ag_d = {}

        def emit_ag(k, shard, full, tokname, lad_tok):
            em.need("gpsimd", lad_tok)
            ins = RawAP(shard, k * 2 * PART * QsS, [[1, 2 * PART * QsS]])
            outs = RawAP(full, k * n_cores * 2 * PART * QsS,
                         [[1, n_cores * 2 * PART * QsS]])
            return em.emit("gpsimd", lambda E, i_=ins, o_=outs:
                           E.collective_compute(
                               "AllGather", OP.bypass,
                               replica_groups=[list(range(n_cores))],
                               ins=[i_.opt()], outs=[o_.opt()]), inc="cc")

        for s in range(QCs // 4):
            lad = emit_gu_stage(s)
            if (s + 1) % GS == 0:
                k = s // GS
                ag_g[k] = emit_ag(k, gsh, gate_scr, "g", lad)
                ag_u[k] = emit_ag(k, ush, up_scr, "u", lad)
        dn_todo = list(range(QCs // 4))

        def emit_dn_some(n):
            while n > 0 and dn_todo:
                s = dn_todo.pop(0)
                lad = emit_dn_stage(s)
                if (s + 1) % GS == 0:
                    k = s // GS
                    ag_d[k] = emit_ag(k, dsh, down_scr, "d", lad)
                n -= 1

        guwin_rd = [None, None]
        dwin_rd = [None]
        intu_rd = [None]
        inter_rd = [None, None]
        gu_ld = {}
        dn_ld = [None]
        x_wr = {}

        def emit_gu_load(ic):
            half = ic % 2
            qg = ic * HB * PART
            r = qg // Qs
            qloc = qg - r * Qs
            k = qloc // QsS
            q2 = qloc - k * QsS
            em.need("sync", ag_g[k], ag_u[k], guwin_rd[half])
            off = half * 4 * HB * PART
            base = k * n_cores * 2 * PART * QsS + r * 2 * PART * QsS + q2
            ch = "ld_guA" if half == 0 else "ld_guB"
            for pl, scr in enumerate((gate_scr, gate_scr, up_scr, up_scr)):
                plane = pl % 2
                sa = RawAP(scr, base + plane * PART * QsS,
                           [[QsS, PART], [1, HB * PART]])
                gu_ld[ic] = dma("sync",
                                guwin[:, off + pl * HB * PART:
                                      off + (pl + 1) * HB * PART], sa, ch)

        def emit_dn_load(g):
            r = (2 * g) // IBs
            ibloc = (2 * g) % IBs
            k = ibloc // (IBs // SUBK)
            q2 = (ibloc - k * (IBs // SUBK)) * HB * PART
            em.need("sync", ag_d[k], dwin_rd[0])
            base = k * n_cores * 2 * PART * QsS + r * 2 * PART * QsS + q2
            for plane in range(2):
                sa = RawAP(down_scr, base + plane * PART * QsS,
                           [[QsS, PART], [1, 2 * HB * PART]])
                dn_ld[0] = dma("sync",
                               dwin[:, plane * 2 * HB * PART:
                                    (plane + 1) * 2 * HB * PART], sa, "ld_dw")

        def emit_ic_unit(ic, lnx_tok):
            half = ic % 2
            off = half * 4 * HB * PART
            slot = ic % 2
            toks = {}
            for w in range(2):
                em.need("tensor", gu_ld[ic], lnx_tok, bank_rd[w])
                for kb in range(HB):
                    hi0 = off + 2 * w * HB * PART + kb
                    lo0 = hi0 + HB * PART
                    xh = lnx[:, kb * FT:(kb + 1) * FT]
                    xl = lnx[:, HB * FT + kb * FT:HB * FT + (kb + 1) * FT]
                    em.emit("tensor",
                            lambda E, s0=hi0, x_=xh, kb_=kb, w_=w: E.matmul(
                                ps[w_][:, 0:FT],
                                guwin[:, s0:s0 + (PART - 1) * HB + 1:HB],
                                x_, start=(kb_ == 0), stop=False))
                    em.emit("tensor",
                            lambda E, s0=hi0, x_=xl, w_=w: E.matmul(
                                ps[w_][:, 0:FT],
                                guwin[:, s0:s0 + (PART - 1) * HB + 1:HB],
                                x_, start=False, stop=False))
                    em.emit("tensor",
                            lambda E, s0=lo0, x_=xh, kb_=kb, w_=w: E.matmul(
                                ps[w_][:, 0:FT],
                                guwin[:, s0:s0 + (PART - 1) * HB + 1:HB],
                                x_, start=False, stop=(kb_ == HB - 1)),
                            inc=PE if kb == HB - 1 else None)
                toks[w] = em.now(PE)
            guwin_rd[half] = toks[1]
            sl = slot * 2 * FT
            em.need("scalar", toks[0], inter_rd[slot], intu_rd[0])
            tok_sg = em.emit("scalar", lambda E: E.activation(
                out=intu[:, 0:FT], in_=ps[0][:, 0:FT], func=AF.Silu),
                inc=AC)
            bank_rd[0] = tok_sg
            em.need("vector", tok_sg, toks[1])
            tok_mu = em.emit("vector", lambda E: E.tensor_tensor(
                out=intu[:, 0:FT], in0=intu[:, 0:FT],
                in1=ps[1][:, 0:FT], op=OP.mult), inc=DV)
            bank_rd[1] = tok_mu
            em.need("scalar", tok_mu)
            tok_h = em.emit("scalar", lambda E, s_=sl: E.copy(
                out=inter[:, s_:s_ + FT], in_=intu[:, 0:FT]), inc=AC)
            em.need("vector", tok_h)
            tok_in = em.emit("vector", lambda E, s_=sl: E.tensor_tensor(
                out=inter[:, s_ + FT:s_ + 2 * FT], in0=intu[:, 0:FT],
                in1=inter[:, s_:s_ + FT], op=OP.subtract), inc=DV)
            intu_rd[0] = tok_in
            return tok_in

        def emit_down_unit(tile, g, tok_in0, tok_in1, scale):
            last = None
            for hb in range(HB):
                bank = 2 + hb % 2
                em.need("tensor", dn_ld[0], tok_in0, tok_in1, bank_rd[bank])
                for k in range(2):
                    whi = (k * HB + hb) * PART
                    wlo = 2 * HB * PART + whi
                    xh = inter[:, k * 2 * FT:k * 2 * FT + FT]
                    xl = inter[:, k * 2 * FT + FT:k * 2 * FT + 2 * FT]
                    em.emit("tensor",
                            lambda E, w_=whi, x_=xh, k_=k, b=bank: E.matmul(
                                ps[b][:, 0:FT], dwin[:, w_:w_ + PART], x_,
                                start=(k_ == 0), stop=False))
                    em.emit("tensor",
                            lambda E, w_=whi, x_=xl, b=bank: E.matmul(
                                ps[b][:, 0:FT], dwin[:, w_:w_ + PART], x_,
                                start=False, stop=False))
                    em.emit("tensor",
                            lambda E, w_=wlo, x_=xh, k_=k, b=bank: E.matmul(
                                ps[b][:, 0:FT], dwin[:, w_:w_ + PART], x_,
                                start=False, stop=(k_ == 1)),
                            inc=PE if k == 1 else None)
                tok_mm = em.now(PE)
                em.need("vector", tok_mm, x_wr.get(hb))
                t = em.emit("vector", lambda E, hb_=hb, b=bank, t_=tile, s_=scale:
                            E.scalar_tensor_tensor(
                                out=x_sb[:, hb_ * TOK + t_ * FT:
                                         hb_ * TOK + t_ * FT + FT],
                                in0=ps[b][:, 0:FT], scalar=s_,
                                in1=x_sb[:, hb_ * TOK + t_ * FT:
                                         hb_ * TOK + t_ * FT + FT],
                                op0=OP.mult, op1=OP.add), inc=DV)
                x_wr[hb] = t
                bank_rd[bank] = t
                last = t
            dwin_rd[0] = em.now(PE)
            inter_rd[0] = dwin_rd[0]
            inter_rd[1] = dwin_rd[0]
            return last

        def emit_ffn(first=False):
            x_tok = None
            for tile in range(cfg.NFT):
                lnx_tok = emit_ln(tile * FT, FT, pair=True)
                for g in range(IB // 2):
                    emit_gu_load(2 * g)
                    emit_gu_load(2 * g + 1)
                    t0 = emit_ic_unit(2 * g, lnx_tok)
                    t1 = emit_ic_unit(2 * g + 1, lnx_tok)
                    if first and tile == 0:
                        need_done = cdiv((g % SUBK + 1) * (QCs // 4), SUBK)
                        emit_dn_some(max(need_done - (QCs // 4 - len(dn_todo)),
                                         2))
                    emit_dn_load(g)
                    x_tok = emit_down_unit(tile, g, t0, t1, 0.5)
            return x_tok

        x1_tok = emit_ffn(first=True)
        emit_dn_some(10 ** 9)

        def attn_proj(mb, wten, bank, lnx_tok, wreg, wreg_rd_tok, reg):
            em.need("sync", wreg_rd_tok)
            src = RawAP(wten, mb * HB * PART * PART,
                        [[HB * PART, PART], [1, HB * PART]])
            tw = dma("sync", wreg, src, "ld_w%d" % reg)
            em.need("tensor", tw, lnx_tok, bank_rd[bank])
            for kb in range(HB):
                em.emit("tensor",
                        lambda E, kb_=kb, r=wreg, b=bank, n_=AST: E.matmul(
                            ps[b][:, 0:n_],
                            r[:, kb_ * PART:(kb_ + 1) * PART],
                            lnxa[:, kb_ * AST:(kb_ + 1) * AST],
                            start=(kb_ == 0), stop=(kb_ == HB - 1)),
                        inc=PE if kb == HB - 1 else None)
            lnxa_rd[0] = em.now(PE)
            return lnxa_rd[0]

        def emit_av(mb, tok_i, tok_f):
            c = mb * AST
            em.need("scalar", tok_f, em.now(DV), em.now(PE))
            tok_a = em.emit("scalar", lambda E, c_=c: E.activation(
                out=atile[:, c_:c_ + AST], in_=ps[1][:, 0:AST],
                func=AF.Sigmoid), inc=AC)
            bank_rd[1] = tok_a
            em.need("scalar", tok_i)
            tok_si = em.emit("scalar", lambda E, c_=c: E.activation(
                out=vtile[:, c_:c_ + AST], in_=ps[0][:, 0:AST],
                func=AF.Silu), inc=AC)
            bank_rd[0] = tok_si
            em.need("vector", tok_a, tok_si)
            em.emit("vector", lambda E, c_=c: E.tensor_scalar(
                out=intu[:, 0:AST], in0=atile[:, c_:c_ + AST],
                scalar1=-1.0, scalar2=1.0, op0=OP.mult, op1=OP.add))
            em.drain("vector")
            return em.emit("vector", lambda E, c_=c: E.tensor_tensor(
                out=vtile[:, c_:c_ + AST], in0=vtile[:, c_:c_ + AST],
                in1=intu[:, 0:AST], op=OP.mult), inc=DV)

        def emit_attn():
            guw32 = guwin[:, :].bitcast(F32)
            WCOLS = HB * PART
            wext32 = (lnx[:, :].bitcast(F32) if HB * FT >= 4 * WCOLS
                      else wext)
            WREG = ([guw32[:, r * WCOLS:(r + 1) * WCOLS] for r in range(4)]
                    + [wext32[:, r * WCOLS:(r + 1) * WCOLS]
                       for r in range(4)])
            pe0 = em.now(PE)
            wreg_rd = [guwin_rd[0], guwin_rd[0], guwin_rd[1], guwin_rd[1],
                       pe0, pe0, pe0, pe0]
            for st in range(cfg.NAST):
                lnx_tok = emit_ln(st * AST, AST, pair=False)
                if st > 0:
                    em.need("scalar", em.now("st_sp"))
                for mb in range(HB):
                    r4 = mb % 4
                    tok_i = attn_proj(mb, wiT, 0, lnx_tok, WREG[r4],
                                      wreg_rd[r4], mb % 2)
                    wreg_rd[r4] = tok_i
                    tok_f = attn_proj(mb, wfT, 1, lnx_tok, WREG[4 + r4],
                                      wreg_rd[4 + r4], 2 + mb % 2)
                    wreg_rd[4 + r4] = tok_f
                    tok_v = emit_av(mb, tok_i, tok_f)
                    em.need("vector", tok_v)
                    c = mb * AST
                    init_o = 0.0 if st == 0 else sc_sb[:, mb:mb + 1]
                    t_o = em.emit("vector", lambda E, c_=c, i_=init_o:
                                  E.tensor_tensor_scan(
                                      out=otile[:, c_:c_ + AST],
                                      data0=atile[:, c_:c_ + AST],
                                      data1=vtile[:, c_:c_ + AST],
                                      initial=i_, op0=OP.mult, op1=OP.add),
                                  inc=DV)
                    em.need("vector", t_o)
                    t_oc = em.emit("vector", lambda E, c_=c, mb_=mb:
                                   E.tensor_copy(
                                       out=sc_sb[:, mb_:mb_ + 1],
                                       in_=otile[:, c_ + AST - 1:c_ + AST]),
                                   inc=DV)
                    init_p = 1.0 if st == 0 else sc_sb[:, HB + mb:HB + mb + 1]
                    em.need("vector", t_oc)
                    t_p = em.emit("vector", lambda E, c_=c, i_=init_p:
                                  E.tensor_tensor_scan(
                                      out=vtile[:, c_:c_ + AST],
                                      data0=atile[:, c_:c_ + AST],
                                      data1=atile[:, c_:c_ + AST],
                                      initial=i_, op0=OP.mult, op1=OP.bypass),
                                  inc=DV)
                    em.need("vector", t_p)
                    em.emit("vector", lambda E, c_=c, mb_=mb: E.tensor_copy(
                        out=sc_sb[:, HB + mb_:HB + mb_ + 1],
                        in_=vtile[:, c_ + AST - 1:c_ + AST]), inc=DV)
                em.need("gpsimd", em.now(DV))
                dma("gpsimd",
                    RawAP(o_scr, st * HB * AST,
                          [[HB * TOK, PART], [1, HB * AST]]),
                    otile[:, 0:HB * AST], "st_sp")
                dma("gpsimd",
                    RawAP(p_scr, st * HB * AST,
                          [[HB * TOK, PART], [1, HB * AST]]),
                    vtile[:, 0:HB * AST], "st_sp")

            em.need("vector", em.now(DV))
            tok_pack = em.emit("vector", lambda E: E.tensor_copy(
                out=carry[:, 0:2 * HB], in_=sc_sb[:, 0:2 * HB]), inc=DV)
            em.need("gpsimd", tok_pack)
            tok_cdma = dma("gpsimd", carry_src.ap(), carry[:, 0:2 * HB], "st_carry")
            em.need("gpsimd", tok_cdma)
            tok_cc = em.emit("gpsimd", lambda E: E.collective_compute(
                "AllGather", OP.bypass,
                replica_groups=[list(range(n_cores))],
                ins=[carry_src.ap().opt()],
                outs=[carry_gth.ap().opt()]), inc="cc")
            em.need("gpsimd", tok_cc)
            tok_gth = dma("gpsimd", cg_sb[:, 0:n_cores * 2 * HB],
                          RawAP(carry_gth, 0,
                                [[2 * HB, PART], [PART * 2 * HB, n_cores],
                                 [1, 2 * HB]]), "ld_gth")
            em.need("vector", tok_gth, tok_consts)
            em.emit("vector", lambda E: E.tensor_scalar(
                out=carry2[:, 0:HB], in0=cg_sb[:, 0:HB],
                scalar1=0.0, scalar2=None, op0=OP.mult))
            em.drain("vector")
            for k in range(n_cores):
                base = k * 2 * HB
                if k == 0:
                    em.emit("vector", lambda E, b=base: E.tensor_copy(
                        out=carry[:, 0:HB], in_=cg_sb[:, b:b + HB]))
                else:
                    em.emit("vector", lambda E, b=base: E.tensor_tensor(
                        out=carry[:, 0:HB], in0=carry[:, 0:HB],
                        in1=cg_sb[:, b + HB:b + 2 * HB], op=OP.mult))
                    em.drain("vector")
                    em.emit("vector", lambda E, b=base: E.tensor_tensor(
                        out=carry[:, 0:HB], in0=carry[:, 0:HB],
                        in1=cg_sb[:, b:b + HB], op=OP.add))
                em.drain("vector")
                em.emit("vector", lambda E, k_=k: E.tensor_tensor(
                    out=intu[:, 0:HB], in0=carry[:, 0:HB],
                    in1=csel_sb[:, k_ * HB:(k_ + 1) * HB], op=OP.mult))
                em.drain("vector")
                em.emit("vector", lambda E: E.tensor_tensor(
                    out=carry2[:, 0:HB], in0=carry2[:, 0:HB],
                    in1=intu[:, 0:HB], op=OP.add))
                em.drain("vector")
            tok_oin = em.emit("vector", lambda E: E.nop(), inc=DV)

            last_x = None
            prev_wo = None
            for st in range(cfg.NAST):
                lnx_tok = emit_ln(st * AST, AST, pair=False)
                em.need("gpsimd", prev_wo,
                        tok_oin if st == 0 else None,
                        em.now("st_sp") if st == 0 else None)
                dma("gpsimd", otile[:, 0:HB * AST],
                    RawAP(o_scr, st * HB * AST,
                          [[HB * TOK, PART], [1, HB * AST]]), "ld_ob")
                t_lp = dma("gpsimd", atile[:, 0:HB * AST],
                           RawAP(p_scr, st * HB * AST,
                                 [[HB * TOK, PART], [1, HB * AST]]), "ld_ob")
                rr_toks = []
                for mb in range(HB):
                    em.need("vector", t_lp, tok_oin, prev_wo)
                    t_r = em.emit("vector", lambda E, mb_=mb:
                                  E.scalar_tensor_tensor(
                                      out=otile[:, mb_ * AST:(mb_ + 1) * AST],
                                      in0=atile[:, mb_ * AST:(mb_ + 1) * AST],
                                      scalar=carry2[:, mb_:mb_ + 1],
                                      in1=otile[:, mb_ * AST:(mb_ + 1) * AST],
                                      op0=OP.mult, op1=OP.add), inc=DV)
                    rr_toks.append(t_r)
                pe_bar = em.now(PE)
                sq_war = None
                for mb in range(HB):
                    em.need("scalar", rr_toks[-1], sq_war)
                    t = em.emit("scalar", lambda E, mb_=mb: E.activation(
                        out=intu[:, 0:AST],
                        in_=otile[:, mb_ * AST:(mb_ + 1) * AST],
                        func=AF.Square), inc=AC)
                    em.need("tensor", t, bank_rd[7] if mb == 0 else None,
                            pe_bar if mb == 0 else None)
                    sq_war = em.emit("tensor", lambda E, mb_=mb: E.matmul(
                        ps[7][0:1, 0:AST], onec_sb[:, :], intu[:, 0:AST],
                        start=(mb_ == 0), stop=(mb_ == HB - 1)), inc=PE)
                em.need("vector", em.now(PE))
                em.emit("vector", lambda E: E.tensor_scalar(
                    out=lnrowA[0:1, 0:AST], in0=ps[7][0:1, 0:AST],
                    scalar1=1.0 / H, scalar2=1e-5, op0=OP.mult, op1=OP.add))
                em.drain("vector")
                tok_r_ = em.emit("vector", lambda E: E.nop(), inc=DV)
                em.need("scalar", tok_r_)
                tok_rs0 = em.emit("scalar", lambda E: E.activation(
                    out=lnrowA[0:1, 0:AST], in_=lnrowA[0:1, 0:AST],
                    func=AF.Sqrt), inc=AC)
                em.need("vector", tok_rs0)
                tok_rs = em.emit("vector", lambda E: E.reciprocal(
                    out=lnrowA[0:1, 0:AST], in_=lnrowA[0:1, 0:AST]), inc=DV)
                em.need("tensor", tok_rs)
                em.emit("tensor", lambda E: E.matmul(
                    ps[7][:, 0:AST], oner_sb[:, :], lnrowA[0:1, 0:AST],
                    start=True, stop=True), inc=PE)
                em.need("vector", em.now(PE))
                tok_R = em.emit("vector", lambda E: E.tensor_copy(
                    out=bcA[:, 0:AST], in_=ps[7][:, 0:AST]), inc=DV)
                bank_rd[7] = tok_R
                sg_toks = []
                for mb in range(HB):
                    r4 = mb % 4
                    tok_g = attn_proj(mb, wgT, 2, lnx_tok, WREG[r4],
                                      wreg_rd[r4], mb % 2)
                    wreg_rd[r4] = tok_g
                    em.need("scalar", tok_g, rr_toks[-1])
                    t_sg = em.emit("scalar", lambda E, c_=mb * AST:
                                   E.activation(
                                       out=vtile[:, c_:c_ + AST],
                                       in_=ps[2][:, 0:AST], func=AF.Silu),
                                   inc=AC)
                    bank_rd[2] = t_sg
                    sg_toks.append(t_sg)
                gtoks = []
                for mb in range(HB):
                    c = mb * AST
                    em.need("vector", tok_R, sg_toks[mb])
                    em.emit("vector", lambda E, mb_=mb: E.tensor_tensor(
                        out=otile[:, mb_ * AST:(mb_ + 1) * AST],
                        in0=otile[:, mb_ * AST:(mb_ + 1) * AST],
                        in1=bcA[:, 0:AST], op=OP.mult))
                    em.drain("vector")
                    t = em.emit("vector", lambda E, mb_=mb, c_=c:
                                E.scalar_tensor_tensor(
                                    out=otile[:, mb_ * AST:(mb_ + 1) * AST],
                                    in0=otile[:, mb_ * AST:(mb_ + 1) * AST],
                                    scalar=gnw_sb[:, mb_:mb_ + 1],
                                    in1=vtile[:, c_:c_ + AST],
                                    op0=OP.mult, op1=OP.mult), inc=DV)
                    gtoks.append(t)
                for mb in range(HB):
                    ri = 4 + mb % 4
                    em.need("sync", wreg_rd[ri])
                    src = RawAP(woT, mb * HB * PART * PART,
                                [[HB * PART, PART], [1, HB * PART]])
                    tk = dma("sync", WREG[ri], src, "ld_w%d" % (2 + mb % 2))
                    em.need("tensor", tk, gtoks[-1], bank_rd[3])
                    for kb in range(HB):
                        em.emit("tensor", lambda E, kb_=kb, r_=ri: E.matmul(
                            ps[3][:, 0:AST],
                            WREG[r_][:, kb_ * PART:(kb_ + 1) * PART],
                            otile[:, kb_ * AST:(kb_ + 1) * AST],
                            start=(kb_ == 0), stop=(kb_ == HB - 1)),
                            inc=PE if kb == HB - 1 else None)
                    tok_wo = em.now(PE)
                    wreg_rd[ri] = tok_wo
                    prev_wo = tok_wo
                    em.need("vector", tok_wo, x_wr.get(mb))
                    last_x = em.emit("vector", lambda E, mb_=mb, s_=st:
                                     E.tensor_tensor(
                                         out=x_sb[:, mb_ * TOK + s_ * AST:
                                                  mb_ * TOK + s_ * AST + AST],
                                         in0=x_sb[:, mb_ * TOK + s_ * AST:
                                                  mb_ * TOK + s_ * AST + AST],
                                         in1=ps[3][:, 0:AST], op=OP.add),
                                     inc=DV)
                    x_wr[mb] = last_x
                    bank_rd[3] = last_x
            guwin_rd[0] = guwin_rd[1] = em.now(PE)
            dwin_rd[0] = em.now(PE)
            em.need("vector", em.now(PE))
            em.need("scalar", em.now(PE))
            return last_x

        if cfg.stop_after == "ffn1":
            x_final = x1_tok
        else:
            x2_tok = emit_attn()
            if cfg.stop_after == "attn":
                x_final = x2_tok
            else:
                x_final = emit_ffn()

        em.need("gpsimd", x_final, em.now(DV))
        dst = RawAP(y_out, 0, [[TOK, PART], [PART * TOK, HB], [1, TOK]])
        tok_out = dma("gpsimd", dst, x_sb[:, 0:HB * TOK], "st_y")
        em.need("gpsimd", tok_out)

        with nc.Block() as block:
            @block.sync
            def _(E):
                for f in em.ops["sync"]:
                    f(E)

            @block.tensor
            def _(E):
                for f in em.ops["tensor"]:
                    f(E)

            @block.vector
            def _(E):
                for f in em.ops["vector"]:
                    f(E)

            @block.scalar
            def _(E):
                for f in em.ops["scalar"]:
                    f(E)

            @block.gpsimd
            def _(E):
                for f in em.ops["gpsimd"]:
                    f(E)

    return nc


def make_inputs(cfg, L0_attn_hidden_states, L0_attn_Wi, L0_attn_Wf,
                L0_attn_Wg, L0_attn_Wo, L0_attn_gnorm_w,
                L0_ffn_gate_cp, L0_ffn_up_cp, L0_ffn_down_cp):
    grid_f, grid_f2, grid_f3, jq, sq = host_grid(cfg)
    perm = down_perm(cfg)

    def tab4(a, core):
        sh = a[core * cfg.Qs:(core + 1) * cfg.Qs]
        g = sh.reshape(cfg.QCs // 4, 4, PART, PART).transpose(0, 2, 1, 3)
        return np.ascontiguousarray(g.reshape(cfg.QCs // 4, PART, 512))
    f32 = np.float32
    M = host_M() / cfg.n_cores
    assert cfg.n_cores in (1, 2, 4, 8, 16)
    M_gu = np.zeros((10, 16), f32)
    M_gu[0:5, 0:8] = M
    M_gu[5:10, 8:16] = M
    common = {
        "wiT": tile_lhsT(np.ascontiguousarray(np.asarray(L0_attn_Wi, f32).T)),
        "wfT": tile_lhsT(np.ascontiguousarray(np.asarray(L0_attn_Wf, f32).T)),
        "wgT": tile_lhsT(np.ascontiguousarray(np.asarray(L0_attn_Wg, f32).T)),
        "woT": tile_lhsT(np.ascontiguousarray(np.asarray(L0_attn_Wo, f32).T)),
        "gnw": np.ascontiguousarray(np.asarray(L0_attn_gnorm_w, f32).reshape(cfg.HB, PART).T),
        "cp_g": np.ascontiguousarray(L0_ffn_gate_cp, dtype=f32),
        "cp_u": np.ascontiguousarray(L0_ffn_up_cp, dtype=f32),
        "cp_d": np.ascontiguousarray(L0_ffn_down_cp, dtype=f32),
        "iota": np.ascontiguousarray(
            np.broadcast_to(np.arange(PART, dtype=f32), (PART, PART))),
        "eye": np.eye(PART, dtype=f32),
        "ones_col": np.ones((PART, 1), dtype=f32),
        "ones_row": np.ones((1, PART), dtype=f32),
        "M_gu": M_gu,
        "M_d": np.ascontiguousarray(M),
    }
    x = np.asarray(L0_attn_hidden_states, f32).reshape(cfg.T, cfg.H)
    in_maps = []
    for c in range(cfg.n_cores):
        sl = slice(c * cfg.TOK, (c + 1) * cfg.TOK)
        csel = np.zeros((PART, cfg.n_cores * cfg.HB), dtype=f32)
        if c > 0:
            csel[:, (c - 1) * cfg.HB:c * cfg.HB] = 1.0
        m = dict(common)
        m["x_in"] = np.ascontiguousarray(x[sl].T)
        m["csel"] = csel
        m["ftab"] = tab4(grid_f, c)
        m["f2tab"] = tab4(grid_f2, c)
        m["f3tab"] = tab4(grid_f3, c)
        m["fdtab"] = tab4(grid_f[perm], c)
        m["f2dtab"] = tab4(grid_f2[perm], c)
        m["f3dtab"] = tab4(grid_f3[perm], c)
        m["stab"] = np.ascontiguousarray(
            sq[c * cfg.Qs:(c + 1) * cfg.Qs].reshape(cfg.QCs, PART).T)
        m["sdtab"] = np.ascontiguousarray(
            sq[perm][c * cfg.Qs:(c + 1) * cfg.Qs].reshape(cfg.QCs, PART).T)
        in_maps.append(m)
    return in_maps


def run(cfg, **inputs):
    nc = build_program(cfg)
    in_maps = make_inputs(cfg, **inputs)
    res = run_bass_kernel_spmd(nc, in_maps, core_ids=list(range(cfg.n_cores)))
    shards = [np.asarray(r["y_out"]) for r in res.results]
    out = np.concatenate([s.T for s in shards], axis=0)
    return np.ascontiguousarray(out.reshape(1, cfg.T, cfg.H))


def kernel(**inputs):
    cfg = full_cfg()
    return run(cfg, **inputs)


# version 26
# speedup vs baseline: 1.0577x; 1.0577x over previous
import numpy as np
from contextlib import ExitStack

import concourse.bass as bass
import concourse.mybir as mybir
from concourse.bass_utils import run_bass_kernel_spmd
from bass_rust import AP as RawAP

F32 = mybir.dt.float32
BF16 = mybir.dt.bfloat16
OP = mybir.AluOpType
AF = mybir.ActivationFunctionType

PART = 128


def cdiv(a, b):
    return -(-a // b)


class Cfg:
    def __init__(self, H, I, T, n_cores, NCTRL, FT, AST, WQ, stop_after=None):
        self.H, self.I, self.T = H, I, T
        self.n_cores = n_cores
        self.NCTRL = NCTRL
        self.TOK = T // n_cores
        self.FT = FT
        self.AST = AST
        self.HB = H // PART
        self.IB = I // PART
        self.N = I * H
        self.Q = self.N // PART
        self.QC = self.Q // PART
        self.WQ = WQ
        self.NFT = self.TOK // FT
        self.NAST = self.TOK // AST
        self.QCs = self.QC // n_cores
        self.Qs = self.Q // n_cores
        self.SUBK = min(4, self.QCs // 4)
        self.stop_after = stop_after
        assert self.N % PART == 0 and self.Q % PART == 0
        assert self.N / (NCTRL - 1) >= PART, "a 128-tile may span >2 segments"
        assert T % n_cores == 0 and self.TOK % FT == 0 and self.TOK % AST == 0
        assert self.QC % 4 == 0 and self.Q % WQ == 0 and WQ % PART == 0
        assert self.QC % (4 * n_cores) == 0
        assert (self.QCs // 4) % self.SUBK == 0
        assert self.IB % n_cores == 0 and (self.IB // n_cores) % 2 == 0
        assert self.IB % 2 == 0


def full_cfg(stop_after=None):
    return Cfg(H=2048, I=8192, T=8192, n_cores=8, NCTRL=130156,
               FT=512, AST=128, WQ=512, stop_after=stop_after)


def host_grid(cfg):
    t = np.linspace(0.0, cfg.NCTRL - 1.0, cfg.N, dtype=np.float64)
    i = np.clip(np.floor(t).astype(np.int64), 0, cfg.NCTRL - 2)
    f = (t - i).astype(np.float32)
    f2 = f * f
    f3 = f2 * f
    it = i.reshape(cfg.Q, PART)
    jq = it[:, 0].copy()
    ok = (it == jq[:, None]) | (it == jq[:, None] + 1)
    assert ok.all(), "tile spans more than two segments"
    sq = (it == jq[:, None]).sum(1).astype(np.float32)
    return (f.reshape(cfg.Q, PART), f2.reshape(cfg.Q, PART),
            f3.reshape(cfg.Q, PART), jq, sq)


def host_pieces(jq):
    Q = len(jq)
    pieces = []
    q0 = 0
    for q in range(1, Q + 1):
        if q == Q or jq[q] != jq[q - 1] + 1:
            pieces.append((q0, q, int(jq[q0])))
            q0 = q
    return pieces


def piece_dmas(pieces, NCTRL, WQ):
    out = []
    for (q0, q1, j0) in pieces:
        q = q0
        while q < q1:
            win = q // WQ
            qe = min(q1, (win + 1) * WQ)
            L = qe - q
            j = j0 + (q - q0)
            if (j - 1) >= 0 and (j + L - 1 + 3) <= NCTRL - 1:
                out.append((win, q - win * WQ, L, ("full", j - 1)))
            else:
                rows = []
                for k in range(5):
                    runs = []
                    c = 0
                    while c < L:
                        idx = j + c + k - 1
                        if idx < 0:
                            ce = min(L, c + (0 - idx))
                            runs.append((c, ce - c, 0, 0))
                            c = ce
                        elif idx > NCTRL - 1:
                            runs.append((c, L - c, NCTRL - 1, 0))
                            c = L
                        else:
                            ce = min(L, c + ((NCTRL - 1) - idx) + 1)
                            runs.append((c, ce - c, idx, 1))
                            c = ce
                    rows.append((k, runs))
                out.append((win, q - win * WQ, L, ("rows", rows)))
            q = qe
    return out


def host_M():
    A = np.zeros((5, 4), dtype=np.float64)
    A[1, 0] = 1.0
    A[0, 1] = -0.5; A[2, 1] = 0.5
    A[0, 2] = 1.0; A[1, 2] = -2.5; A[2, 2] = 2.0; A[3, 2] = -0.5
    A[0, 3] = -0.5; A[1, 3] = 1.5; A[2, 3] = -1.5; A[3, 3] = 0.5
    B = np.zeros((5, 4), dtype=np.float64)
    B[1:5, :] = A[0:4, :]
    return np.concatenate([A, B - A], axis=1).astype(np.float32)


def down_perm(cfg):
    hb = np.arange(cfg.HB)
    ib = np.arange(cfg.IB)
    h_off = np.arange(PART)
    p = ((hb[None, :, None] * PART + h_off[None, None, :]) * cfg.IB
         + ib[:, None, None])
    return p.reshape(-1)


def tile_lhsT(Wt):
    K, M = Wt.shape
    a = Wt.reshape(K // PART, PART, M // PART, PART)
    a = a.transpose(2, 1, 0, 3)
    return np.ascontiguousarray(a.reshape(M // PART, PART, K))


ENGS = ("sync", "tensor", "vector", "scalar", "gpsimd")


class Chan:
    def __init__(self, stack, nc, name, dma=False):
        self.sem = stack.enter_context(nc.semaphore(name))
        self.count = 0
        self.dma = dma


class Emitter:
    def __init__(self, stack, nc):
        self.nc = nc
        self.ops = {e: [] for e in ENGS}
        self.seq = {e: 0 for e in ENGS}
        self.last_drain = {e: -1 for e in ENGS}
        self.waited = {e: {} for e in ENGS}
        self.chans = {}
        for name in ("ld_x", "ld_ftA", "ld_ftB", "ld_phA", "ld_phB",
                     "ld_pdA", "ld_pdB", "ld_phAs", "ld_phBs",
                     "ld_cdA", "ld_cdB",
                     "ld_cgA", "ld_cgB",
                     "ld_guA", "ld_guB", "ld_dw",
                     "ld_w0", "ld_w1", "ld_w2", "ld_w3", "ld_gth", "ld_ob",
                     "st_scr", "st_cimg", "st_prm", "st_carry", "st_y",
                     "st_sp",
                     "pe", "dv", "ac", "po", "cc"):
            self.chans[name] = Chan(stack, nc, name,
                                    dma=name.startswith(("ld_", "st_")))

    def ch(self, name):
        return self.chans[name]

    def now(self, chan_name):
        c = self.chans[chan_name]
        return ("sem", c, c.count)

    def need(self, eng, *tokens):
        drains = False
        waits = []
        for tok in tokens:
            if tok is None:
                continue
            if tok[0] == "sem":
                _, chan, val = tok
                if chan.dma and val > 0:
                    val = chan.count
                if val > 0 and self.waited[eng].get(id(chan), 0) < val:
                    self.waited[eng][id(chan)] = val
                    waits.append((chan.sem, val))
            elif tok[0] == "eng":
                _, e2, seq = tok
                assert e2 == eng, f"cross-engine eng token {e2}->{eng}"
                if seq > self.last_drain[eng]:
                    drains = True
            else:
                raise RuntimeError(f"bad token {tok}")
        if drains:
            self.last_drain[eng] = self.seq[eng]
            self.ops[eng].append(lambda E: E.drain())
            self.seq[eng] += 1
        for sem, val in waits:
            self.ops[eng].append(lambda E, s=sem, v=val: E.wait_ge(s, v))

    def drain(self, eng):
        self.last_drain[eng] = self.seq[eng]
        self.ops[eng].append(lambda E: E.drain())
        self.seq[eng] += 1

    def emit(self, eng, fn, inc=None, dinc=1):
        if inc is not None:
            chan = self.chans[inc]
            chan.count += dinc
            val = chan.count
            self.ops[eng].append(
                lambda E, f=fn, c=chan, n=dinc: f(E).then_inc(c.sem, n))
            self.seq[eng] += 1
            return ("sem", chan, val)
        self.ops[eng].append(lambda E, f=fn: f(E))
        self.seq[eng] += 1
        return ("eng", eng, self.seq[eng] - 1)


def build_program(cfg):
    nc = bass.Bass("TRN2", target_bir_lowering=False)
    H, I, TOK, HB, IB = cfg.H, cfg.I, cfg.TOK, cfg.HB, cfg.IB
    FT, AST, Q, QC, WQ, NCTRL = cfg.FT, cfg.AST, cfg.Q, cfg.QC, cfg.WQ, cfg.NCTRL
    n_cores = cfg.n_cores

    _, _, _, jq, _ = host_grid(cfg)
    pdmas = piece_dmas(host_pieces(jq), NCTRL, WQ)
    pdmas_by_win = {}
    for pd in pdmas:
        pdmas_by_win.setdefault(pd[0], []).append(pd)

    def din(name, shape):
        return nc.dram_tensor(name, list(shape), F32, kind="ExternalInput")

    x_in = din("x_in", (H, TOK))
    wiT = din("wiT", (HB, PART, HB * PART))
    wfT = din("wfT", (HB, PART, HB * PART))
    wgT = din("wgT", (HB, PART, HB * PART))
    woT = din("woT", (HB, PART, HB * PART))
    gnw_in = din("gnw", (PART, H // PART))
    cp_g = din("cp_g", (NCTRL,))
    cp_u = din("cp_u", (NCTRL,))
    cp_d = din("cp_d", (NCTRL,))
    ftab = din("ftab", (cfg.QCs // 4, PART, 512))
    f2tab = din("f2tab", (cfg.QCs // 4, PART, 512))
    f3tab = din("f3tab", (cfg.QCs // 4, PART, 512))
    fdtab = din("fdtab", (cfg.QCs // 4, PART, 512))
    f2dtab = din("f2dtab", (cfg.QCs // 4, PART, 512))
    f3dtab = din("f3dtab", (cfg.QCs // 4, PART, 512))
    stab = din("stab", (PART, cfg.QCs))
    sdtab = din("sdtab", (PART, cfg.QCs))
    iota_in = din("iota", (PART, PART))
    eye_in = din("eye", (PART, PART))
    ones_col = din("ones_col", (PART, 1))
    ones_row = din("ones_row", (1, PART))
    M_gu_in = din("M_gu", (10, 16))
    M_d_in = din("M_d", (5, 8))
    csel_in = din("csel", (PART, n_cores * HB))

    y_out = nc.dram_tensor("y_out", [H, TOK], F32, kind="ExternalOutput")

    _aspace = {"addr_space": "Shared"} if n_cores > 4 else {}
    gate_scr = nc.dram_tensor("gate_scr", [2 * QC * PART * PART], BF16,
                              **_aspace)
    up_scr = nc.dram_tensor("up_scr", [2 * QC * PART * PART], BF16,
                            **_aspace)
    down_scr = nc.dram_tensor("down_scr", [2 * QC * PART * PART], BF16,
                              **_aspace)
    Qs, QCs, SUBK = cfg.Qs, cfg.QCs, cfg.SUBK
    QsS = Qs // SUBK
    IBs = IB // n_cores
    gsh = nc.dram_tensor("gsh", [2 * Qs * PART], BF16)
    ush = nc.dram_tensor("ush", [2 * Qs * PART], BF16)
    dsh = nc.dram_tensor("dsh", [2 * Qs * PART], BF16)
    cimg_gu = nc.dram_tensor("cimg_gu", [Q * 16], F32)
    cimg_d2 = nc.dram_tensor("cimg_d2", [Q * 8], F32)
    cglo = nc.dram_tensor("cglo", [Qs * 16], F32)
    cdlo = nc.dram_tensor("cdlo", [Qs * 8], F32)
    cimg = nc.dram_tensor("cimg", [Q * 8], F32)
    carry_src = nc.dram_tensor("carry_src", [PART, 2 * HB], F32)
    carry_gth = nc.dram_tensor("carry_gth", [n_cores * PART, 2 * HB], F32)
    o_scr = nc.dram_tensor("o_scr", [PART, HB * TOK], F32)
    p_scr = nc.dram_tensor("p_scr", [PART, HB * TOK], F32)

    stack = ExitStack()
    with stack:
        def sb(name, p, n):
            return stack.enter_context(nc.sbuf_tensor(name, [p, n], F32))

        x_sb = sb("x_sb", PART, HB * TOK)
        iota_sb = sb("iota_sb", PART, PART)
        eye_sb = sb("eye_sb", PART, PART)
        onec_sb = sb("onec_sb", PART, 1)
        oner_sb = sb("oner_sb", 1, PART)
        gnw_sb = sb("gnw_sb", PART, HB)
        mgu_sb = sb("mgu_sb", 10, 16)
        md_sb = sb("md_sb", 5, 8)
        csel_sb = sb("csel_sb", PART, n_cores * HB)
        def sbb(name, p, n):
            return stack.enter_context(nc.sbuf_tensor(name, [p, n], BF16))

        lnx = sbb("lnx", PART, 2 * HB * FT)
        inter = sbb("inter", PART, 2 * 2 * FT)
        intu = sb("intu", PART, FT)
        lnu = intu[:, 0:FT]
        guwin = sbb("guwin", PART, 2 * 4 * HB * PART)
        arena = sb("arena", PART, max(4 * HB * AST, 2 * HB * PART))
        wext = (sb("wext", PART, 4 * HB * PART)
                if HB * FT < 4 * HB * PART else None)
        atile = arena[:, 0:HB * AST]
        vtile = arena[:, HB * AST:2 * HB * AST]
        otile = arena[:, 2 * HB * AST:3 * HB * AST]
        lnxa = arena[:, 3 * HB * AST:4 * HB * AST]
        dwin = arena[:, 0:2 * HB * PART].bitcast(BF16)
        phat = sb("phat", 10, 2 * WQ)
        phd = sb("phd", 5, 2 * WQ)
        ftileA = sb("ftileA", PART, 3 * 512)
        ftileB = sb("ftileB", PART, 3 * 512)
        sringA = sb("sringA", PART, 4)
        sringB = sb("sringB", PART, 4)
        htile = [[sb(f"htile{e}{k}", PART, 4 * PART) for k in range(2)]
                 for e in range(1)]
        accT = [[sb(f"acc{e}_{k}", PART, PART) for k in range(4)]
                for e in range(1)]
        wqT = [[sb(f"wq{e}_{k}", PART, PART) for k in range(6)]
               for e in range(1)]
        cgu = sb("cgu", PART, 2 * 64)
        cdn = sb("cdn", PART, 2 * 32)
        tstage = sbb("tstage", PART, 4 * 512)
        pstage = sb("pstage", PART, 64)
        pbuf = intu[:, FT - AST:FT]
        carry = sb("carry", PART, 2 * HB)
        carry2 = sb("carry2", PART, HB)
        cg_sb = sb("cg_sb", PART, n_cores * 2 * HB)
        sc_sb = sb("sc_sb", PART, 3 * HB)
        bcA = sb("bcA", PART, max(FT, AST))
        bcB = sb("bcB", PART, max(FT, AST))
        lnrowA = bcB[0:1, :]
        lnrowB = bcA[0:1, :]

        ps = [stack.enter_context(nc.psum_tensor(f"ps{k}", [PART, 512], F32))
              for k in range(8)]

        em = Emitter(stack, nc)
        PE, DV, AC = "pe", "dv", "ac"

        bank_rd = {k: None for k in range(8)}

        def dma(eng_name, out_ap, in_ap, ch):
            return em.emit(eng_name,
                           lambda E, o=out_ap, i=in_ap: E.dma_start(out=o, in_=i),
                           inc=ch, dinc=16)

        tok_x = dma("sync", x_sb[:, :],
                    RawAP(x_in, 0, [[TOK, PART], [PART * TOK, HB], [1, TOK]]),
                    "ld_x")
        dma("sync", iota_sb[:, :], iota_in.ap(), "ld_x")
        dma("sync", eye_sb[:, :], eye_in.ap(), "ld_x")
        dma("sync", onec_sb[:, :], ones_col.ap(), "ld_x")
        dma("sync", oner_sb[:, :], ones_row.ap(), "ld_x")
        dma("sync", gnw_sb[:, :], gnw_in.ap(), "ld_x")
        dma("sync", mgu_sb[:, :], M_gu_in.ap(), "ld_x")
        dma("sync", md_sb[:, :], M_d_in.ap(), "ld_x")
        tok_consts = dma("sync", csel_sb[:, :], csel_in.ap(), "ld_x")

        PIECE_Q = ("gpsimd", "sync")

        def emit_phat_window(win, dest_sb, half, cps, war_tok, ch):
            toks = []
            base = half * WQ
            for wi, cph in enumerate(cps):
                q = PIECE_Q[wi % len(PIECE_Q)]
                em.need(q, war_tok)
                r0 = wi * 5
                chq = ch + ("", "s")[wi % len(PIECE_Q)]
                tok = None
                for (_, col0, L, spec) in pdmas_by_win.get(win, []):
                    if spec[0] == "full":
                        sap = RawAP(cph, spec[1], [[1, 5], [1, L]])
                        tok = dma(q,
                                  dest_sb[r0:r0 + 5,
                                          base + col0:base + col0 + L],
                                  sap, chq)
                    else:
                        for (k, runs) in spec[1]:
                            for (c, ln, off, step) in runs:
                                dims = ([[step, ln], [1, 1]] if step == 0
                                        else [[step, ln]])
                                sap = RawAP(cph, off, dims)
                                tok = dma(
                                    q,
                                    dest_sb[r0 + k:r0 + k + 1,
                                            base + col0 + c:
                                            base + col0 + c + ln],
                                    sap, chq)
                toks.append(tok)
            return toks

        NW = Q // WQ
        CPW = WQ // PART
        BATCH = min(16, CPW)
        GS = (QCs // 4) // SUBK

        pp_done = [None]

        def emit_prepass(ph_sb, cps, chA, chB, msb, width, dst_dram, blocked,
                         interleave=False, other=None):
            ph_tok = [None, None]
            ph_rd = [None, None]
            for win in range(NW):
                half = win % 2
                ph_tok[half] = emit_phat_window(
                    win, ph_sb, half, cps, ph_rd[half],
                    chA if half == 0 else chB)
                for b0 in range(0, CPW, BATCH):
                    nb = min(BATCH, CPW - b0)
                    done = pp_done[0]
                    em.need("tensor", *ph_tok[half])
                    em.need("tensor", tok_consts, bank_rd[4])
                    for b in range(nb):
                        cq = b0 + b
                        lhs = ph_sb[:, half * WQ + cq * PART:
                                    half * WQ + (cq + 1) * PART]
                        em.emit("tensor",
                                lambda E, l=lhs, b_=b, w_=width, m_=msb:
                                E.matmul(ps[4][:, b_ * w_:(b_ + 1) * w_],
                                         l, m_[:, :], start=True, stop=True),
                                inc=PE if b == nb - 1 else None)
                    tok_mm = em.now(PE)
                    ph_rd[half] = tok_mm
                    em.need("scalar", tok_mm, done)
                    tok_ev = em.emit("scalar",
                                     lambda E, n_=nb * width: E.copy(
                                         out=pstage[:, 0:n_],
                                         in_=ps[4][:, 0:n_]),
                                     inc=AC)
                    bank_rd[4] = tok_ev
                    em.need("scalar", tok_ev)
                    cq0 = win * CPW + b0
                    if blocked:
                        r = cq0 // QCs
                        cqloc = cq0 % QCs
                        dst = RawAP(dst_dram,
                                    r * PART * QCs * width + cqloc * width,
                                    [[QCs * width, PART], [1, nb * width]])
                    else:
                        dst = RawAP(dst_dram, cq0 * width,
                                    [[QC * width, PART], [1, nb * width]])
                    pp_done[0] = dma("scalar", dst, pstage[:, 0:nb * width],
                                     "st_cimg")
                if interleave:
                    if other is not None:
                        next(other, None)
                    yield pp_done[0]
            if other is not None:
                for _ in other:
                    yield pp_done[0]
            yield pp_done[0]

        pp_gu = emit_prepass(phat, [cp_g, cp_u], "ld_phA", "ld_phB",
                             mgu_sb, 16, cimg_gu, True, interleave=True)
        pp_d = emit_prepass(phd, [cp_d], "ld_pdA", "ld_pdB",
                            md_sb, 8, cimg, False, interleave=True,
                            other=pp_gu)
        for _ in pp_d:
            pass
        pp1 = pp2 = pp_done[0]

        Pc = 128 // IB if IB <= 128 else 1
        assert IB <= 128 and 128 % IB == 0
        em.need("gpsimd", pp1, pp2)
        prm = None
        for ib in range(IB):
            r = ib // IBs
            dbase = r * PART * QCs * 8 + (ib % IBs) * HB * 8
            for B in range(Pc):
                dst = RawAP(cimg_d2, dbase + B * QCs * 8,
                            [[Pc * QCs * 8, IB], [1, HB * 8]])
                sa = RawAP(cimg, (ib + IB * B) * QC * 8,
                           [[8, IB], [IB * 8, HB], [1, 8]])
                prm = dma("gpsimd", dst, sa, "st_prm")
        em.need("gpsimd", prm)
        tok_rs_gu = em.emit("gpsimd", lambda E: E.collective_compute(
            "ReduceScatter", OP.add, replica_groups=[list(range(n_cores))],
            ins=[cimg_gu.ap().opt()], outs=[cglo.ap().opt()]), inc="cc")
        tok_rs_d = em.emit("gpsimd", lambda E: E.collective_compute(
            "ReduceScatter", OP.add, replica_groups=[list(range(n_cores))],
            ins=[cimg_d2.ap().opt()], outs=[cdlo.ap().opt()]), inc="cc")

        lnxa_rd = [None]

        def emit_ln(src_off, ntok, pair):
            pe_bar = em.now(PE) if pair else lnxa_rd[0]
            dv_bar = em.now(DV)
            st_bar = em.now("st_cimg")
            for c in range(HB):
                em.need("tensor", tok_x, tok_consts,
                        bank_rd[7] if c == 0 else None,
                        pe_bar if c == 0 else None)
                em.emit("tensor",
                        lambda E, c_=c, n_=ntok, o_=src_off: E.matmul(
                            ps[7][0:1, 0:n_], onec_sb[:, :],
                            x_sb[:, c_ * TOK + o_:c_ * TOK + o_ + n_],
                            start=(c_ == 0), stop=(c_ == HB - 1)),
                        inc=PE if c == HB - 1 else None)
            em.need("vector", em.now(PE))
            em.drain("vector")
            tok_mean = em.emit("vector", lambda E, n_=ntok: E.tensor_scalar(
                out=lnrowA[0:1, 0:n_], in0=ps[7][0:1, 0:n_],
                scalar1=1.0 / H, scalar2=None, op0=OP.mult), inc=DV)
            sq_war = None
            for c in range(HB):
                em.need("scalar", tok_x, sq_war,
                        dv_bar if c == 0 else None,
                        st_bar if c == 0 else None)
                t = em.emit("scalar",
                            lambda E, c_=c, n_=ntok, o_=src_off: E.activation(
                                out=intu[:, 0:n_],
                                in_=x_sb[:, c_ * TOK + o_:c_ * TOK + o_ + n_],
                                func=AF.Square),
                            inc=AC)
                em.need("tensor", t, tok_mean if c == 0 else None)
                sq_war = em.emit("tensor",
                                 lambda E, n_=ntok, first=(c == 0),
                                 last=(c == HB - 1): E.matmul(
                                     ps[7][0:1, 0:n_], onec_sb[:, :],
                                     intu[:, 0:n_], start=first, stop=last),
                                 inc=PE)
            em.need("vector", em.now(PE))
            em.emit("vector", lambda E, n_=ntok: E.tensor_scalar(
                out=lnrowB[0:1, 0:n_], in0=ps[7][0:1, 0:n_],
                scalar1=1.0 / H, scalar2=None, op0=OP.mult))
            em.drain("vector")
            em.emit("vector", lambda E, n_=ntok: E.tensor_tensor(
                out=intu[0:1, 0:n_], in0=lnrowA[0:1, 0:n_],
                in1=lnrowA[0:1, 0:n_], op=OP.mult))
            em.drain("vector")
            em.emit("vector", lambda E, n_=ntok: E.tensor_tensor(
                out=lnrowB[0:1, 0:n_], in0=lnrowB[0:1, 0:n_],
                in1=intu[0:1, 0:n_], op=OP.subtract))
            em.drain("vector")
            tok_var = em.emit("vector", lambda E, n_=ntok: E.tensor_scalar(
                out=lnrowB[0:1, 0:n_], in0=lnrowB[0:1, 0:n_],
                scalar1=1.0, scalar2=1e-5, op0=OP.mult, op1=OP.add), inc=DV)
            em.need("scalar", tok_var)
            tok_sq = em.emit("scalar", lambda E, n_=ntok: E.activation(
                out=lnrowB[0:1, 0:n_], in_=lnrowB[0:1, 0:n_], func=AF.Sqrt),
                inc=AC)
            em.need("vector", tok_sq)
            tok_a = em.emit("vector", lambda E, n_=ntok: E.reciprocal(
                out=lnrowB[0:1, 0:n_], in_=lnrowB[0:1, 0:n_]), inc=DV)
            em.drain("vector")
            em.emit("vector", lambda E, n_=ntok: E.tensor_tensor(
                out=lnrowA[0:1, 0:n_], in0=lnrowA[0:1, 0:n_],
                in1=lnrowB[0:1, 0:n_], op=OP.mult))
            em.drain("vector")
            tok_b = em.emit("vector", lambda E, n_=ntok: E.tensor_scalar(
                out=lnrowA[0:1, 0:n_], in0=lnrowA[0:1, 0:n_],
                scalar1=-1.0, scalar2=None, op0=OP.mult), inc=DV)
            em.need("tensor", tok_a, tok_b)
            em.emit("tensor", lambda E, n_=ntok: E.matmul(
                ps[7][:, 0:n_], oner_sb[:, :], lnrowB[0:1, 0:n_],
                start=True, stop=True), inc=PE)
            em.need("vector", em.now(PE))
            tok_evA = em.emit("vector", lambda E, n_=ntok: E.tensor_copy(
                out=bcA[:, 0:n_], in_=ps[7][:, 0:n_]), inc=DV)
            em.need("tensor", tok_evA)
            em.emit("tensor", lambda E, n_=ntok: E.matmul(
                ps[7][:, 0:n_], oner_sb[:, :], lnrowA[0:1, 0:n_],
                start=True, stop=True), inc=PE)
            em.need("vector", em.now(PE))
            tok_evB = em.emit("vector", lambda E, n_=ntok: E.tensor_copy(
                out=bcB[:, 0:n_], in_=ps[7][:, 0:n_]), inc=DV)
            bank_rd[7] = tok_evB
            last = None
            if pair:
                for c in range(HB):
                    em.need("vector", tok_evA, tok_evB, last)
                    em.emit("vector", lambda E, c_=c, n_=ntok, o_=src_off:
                            E.tensor_tensor(
                                out=lnu[:, 0:n_],
                                in0=x_sb[:, c_ * TOK + o_:c_ * TOK + o_ + n_],
                                in1=bcA[:, 0:n_], op=OP.mult))
                    em.drain("vector")
                    tok_u = em.emit("vector", lambda E, n_=ntok:
                                    E.tensor_tensor(
                                        out=lnu[:, 0:n_], in0=lnu[:, 0:n_],
                                        in1=bcB[:, 0:n_], op=OP.add), inc=DV)
                    em.need("scalar", tok_u)
                    tok_h = em.emit("scalar", lambda E, c_=c, n_=ntok:
                                    E.copy(out=lnx[:, c_ * n_:(c_ + 1) * n_],
                                           in_=lnu[:, 0:n_]), inc=AC)
                    em.need("vector", tok_h)
                    last = em.emit("vector", lambda E, c_=c, n_=ntok:
                                   E.tensor_tensor(
                                       out=lnx[:, HB * FT + c_ * n_:
                                               HB * FT + (c_ + 1) * n_],
                                       in0=lnu[:, 0:n_],
                                       in1=lnx[:, c_ * n_:(c_ + 1) * n_],
                                       op=OP.subtract), inc=DV)
            else:
                for c in range(HB):
                    em.need("vector", tok_evA, tok_evB)
                    em.emit("vector", lambda E, c_=c, n_=ntok, o_=src_off:
                            E.tensor_tensor(
                                out=lnxa[:, c_ * n_:(c_ + 1) * n_],
                                in0=x_sb[:, c_ * TOK + o_:c_ * TOK + o_ + n_],
                                in1=bcA[:, 0:n_], op=OP.mult))
                    em.drain("vector")
                    last = em.emit("vector", lambda E, c_=c, n_=ntok:
                                   E.tensor_tensor(
                                       out=lnxa[:, c_ * n_:(c_ + 1) * n_],
                                       in0=lnxa[:, c_ * n_:(c_ + 1) * n_],
                                       in1=bcB[:, 0:n_], op=OP.add),
                                   inc=DV if c == HB - 1 else None)
            return last

        def emit_hprep(eng, fsb, foff, srng, sidx, ht):
            em.emit(eng, lambda E: E.tensor_scalar(
                out=ht[:, 0:PART], in0=iota_sb[:, :],
                scalar1=srng[:, sidx:sidx + 1], scalar2=None, op0=OP.is_ge))

        def emit_hprep2(eng, fsb, foff, ht):
            f_ap = fsb[:, foff:foff + PART]
            f2_ap = fsb[:, 512 + foff:512 + foff + PART]
            f3_ap = fsb[:, 1024 + foff:1024 + foff + PART]
            em.emit(eng, lambda E: E.tensor_tensor(
                out=ht[:, PART:2 * PART], in0=ht[:, 0:PART],
                in1=f_ap, op=OP.mult))
            em.emit(eng, lambda E: E.tensor_tensor(
                out=ht[:, 2 * PART:3 * PART], in0=ht[:, 0:PART],
                in1=f2_ap, op=OP.mult))
            em.emit(eng, lambda E: E.tensor_tensor(
                out=ht[:, 3 * PART:4 * PART], in0=ht[:, 0:PART],
                in1=f3_ap, op=OP.mult))

        def emit_chain_group(eng, ch_inc, fsb, chains):
            def co(csb, coff, x):
                return csb[:, coff + x:coff + x + 1]

            for (foff, ht, csb, coff, acc, wq) in chains:
                f_ap = fsb[:, foff:foff + PART]
                em.emit(eng, lambda E, f=f_ap, c1=co(csb, coff, 1),
                        c0=co(csb, coff, 0), a=acc: E.tensor_scalar(
                            out=a[:, :], in0=f, scalar1=c1, scalar2=c0,
                            op0=OP.mult, op1=OP.add))
            for step, (sel, cx) in enumerate(
                    (("f2", 2), ("f3", 3), ("H", 4), ("fH", 5), ("f2H", 6))):
                em.drain(eng)
                for (foff, ht, csb, coff, acc, wq) in chains:
                    if sel == "f2":
                        tbl = fsb[:, 512 + foff:512 + foff + PART]
                    elif sel == "f3":
                        tbl = fsb[:, 1024 + foff:1024 + foff + PART]
                    elif sel == "H":
                        tbl = ht[:, 0:PART]
                    elif sel == "fH":
                        tbl = ht[:, PART:2 * PART]
                    else:
                        tbl = ht[:, 2 * PART:3 * PART]
                    em.emit(eng, lambda E, t=tbl, c_=co(csb, coff, cx),
                            a=acc: E.scalar_tensor_tensor(
                                out=a[:, :], in0=t, scalar=c_, in1=a[:, :],
                                op0=OP.mult, op1=OP.add))
            em.drain(eng)
            toks = []
            for (foff, ht, csb, coff, acc, wq) in chains:
                toks.append(em.emit(eng, lambda E, t=ht[:, 3 * PART:4 * PART],
                                    c_=co(csb, coff, 7), a=acc, w=wq:
                                    E.scalar_tensor_tensor(
                                        out=w[:, :], in0=t, scalar=c_,
                                        in1=a[:, :], op0=OP.mult, op1=OP.add),
                                    inc=ch_inc))
            return toks

        gu_win_state = {"win": -1, "tok": [None, None], "rd": [None, None]}
        stage_scr_gu = {}
        stage_scr_dn = {}
        ftile_rd = [None, None]
        cgu_rd = [None, None]
        cdn_rd = [None, None]
        wq_tr = [[None] * 6 for _ in range(1)]
        wq_ring = [0]
        tstage_rd = [None]

        def load_ftile(kind, grp, flip):
            buf = ftileA if flip == 0 else ftileB
            srng = sringA if flip == 0 else sringB
            srcs = ((ftab, f2tab, f3tab) if kind == "gu"
                    else (fdtab, f2dtab, f3dtab))
            stsrc = stab if kind == "gu" else sdtab
            ch = "ld_ftA" if flip == 0 else "ld_ftB"
            em.need("sync", ftile_rd[flip])
            for ti, tsrc in enumerate(srcs):
                src = RawAP(tsrc, (grp // 4) * PART * 512,
                            [[512, PART], [1, 512]])
                dma("sync", buf[:, ti * 512:(ti + 1) * 512], src, ch)
            tok = dma("sync", srng[:, 0:4],
                      RawAP(stsrc, grp, [[QCs, PART], [1, 4]]), ch)
            return buf, srng, tok

        def emit_gu_stage(s):
            ring = s % 2
            ch = "ld_cgA" if ring == 0 else "ld_cgB"
            em.need("scalar", tok_rs_gu, cgu_rd[ring])
            tok_cg = dma("scalar", cgu[:, ring * 64:(ring + 1) * 64],
                         RawAP(cglo, 4 * s * 16, [[QCs * 16, PART], [1, 64]]),
                         ch)
            flip = s % 2
            buf, srng, tok_ft = load_ftile("gu", 4 * s, flip)
            ei = 0
            eng = "vector"
            ch_inc = DV
            last_chain = None
            for pair in range(2):
                b0, b1 = 2 * pair, 2 * pair + 1
                r0 = wq_ring[ei]
                wq_ring[ei] = (r0 + 4) % 6
                em.need(eng, tok_ft, tok_cg,
                        *[wq_tr[ei][(r0 + k) % 6] for k in range(4)])
                em.drain(eng)
                emit_hprep(eng, buf, b0 * PART, srng, b0, htile[ei][0])
                emit_hprep(eng, buf, b1 * PART, srng, b1, htile[ei][1])
                em.drain(eng)
                emit_hprep2(eng, buf, b0 * PART, htile[ei][0])
                emit_hprep2(eng, buf, b1 * PART, htile[ei][1])
                em.drain(eng)
                chains = [
                    (b0 * PART, htile[ei][0], cgu, ring * 64 + b0 * 16 + 0,
                     accT[ei][0], wqT[ei][r0]),
                    (b0 * PART, htile[ei][0], cgu, ring * 64 + b0 * 16 + 8,
                     accT[ei][1], wqT[ei][(r0 + 1) % 6]),
                    (b1 * PART, htile[ei][1], cgu, ring * 64 + b1 * 16 + 0,
                     accT[ei][2], wqT[ei][(r0 + 2) % 6]),
                    (b1 * PART, htile[ei][1], cgu, ring * 64 + b1 * 16 + 8,
                     accT[ei][3], wqT[ei][(r0 + 3) % 6]),
                ]
                toks = emit_chain_group(eng, ch_inc, buf, chains)
                for ci, (b_, wi) in enumerate(((b0, 0), (b0, 1), (b1, 0), (b1, 1))):
                    bank_id = 5 if wi == 0 else 6
                    em.need("tensor", toks[ci],
                            bank_rd[bank_id] if (pair == 0 and b_ == b0) else None)
                    tr = em.emit(
                        "tensor", lambda E, w_=chains[ci][5], bi=bank_id, bb=b_:
                        E.transpose(ps[bi][:, bb * PART:(bb + 1) * PART],
                                    w_[:, :], eye_sb[:, :]), inc=PE)
                    wq_tr[ei][(r0 + ci) % 6] = tr
                last_chain = toks[-1]
            cgu_rd[ring] = last_chain
            ftile_rd[flip] = last_chain
            tok_tr = em.now(PE)
            em.need("scalar", tok_tr, tstage_rd[0])
            tok_h5 = em.emit("scalar", lambda E: E.copy(
                out=tstage[:, 0:512], in_=ps[5][:, :]), inc=AC)
            tok_h6 = em.emit("scalar", lambda E: E.copy(
                out=tstage[:, 1024:1536], in_=ps[6][:, :]), inc=AC)
            em.need("vector", tok_h5, tok_h6, tstage_rd[0])
            tok_l5 = em.emit("vector", lambda E: E.tensor_tensor(
                out=tstage[:, 512:1024], in0=ps[5][:, :],
                in1=tstage[:, 0:512], op=OP.subtract), inc=DV)
            tok_l6 = em.emit("vector", lambda E: E.tensor_tensor(
                out=tstage[:, 1536:2048], in0=ps[6][:, :],
                in1=tstage[:, 1024:1536], op=OP.subtract), inc=DV)
            bank_rd[5] = tok_l5
            bank_rd[6] = tok_l6
            em.need("scalar", tok_l5, tok_l6)
            k = s // GS
            off = k * 2 * PART * QsS + (4 * s * PART - k * QsS)
            for pl, (sh_, t0) in enumerate(((gsh, 0), (gsh, 512),
                                            (ush, 1024), (ush, 1536))):
                plane = pl % 2
                dst = RawAP(sh_, off + plane * PART * QsS,
                            [[QsS, PART], [1, 512]])
                dma("scalar", dst, tstage[:, t0:t0 + 512], "st_scr")
            em.need("scalar", em.now("st_scr"))
            tok = em.emit("scalar", lambda E: E.nop(), inc=AC)
            tstage_rd[0] = tok
            return tok

        def emit_dn_stage(s):
            ring = s % 2
            ch = "ld_cdA" if ring == 0 else "ld_cdB"
            em.need("scalar", tok_rs_d, cdn_rd[ring])
            tok_cd = dma("scalar", cdn[:, ring * 32:(ring + 1) * 32],
                         RawAP(cdlo, 4 * s * 8, [[QCs * 8, PART], [1, 32]]),
                         ch)
            flip = s % 2
            buf, srng, tok_ft = load_ftile("dn", 4 * s, flip)
            ei = 0
            eng = "vector"
            ch_inc = DV
            last_chain = None
            for pair in range(2):
                b0, b1 = 2 * pair, 2 * pair + 1
                r0 = wq_ring[ei]
                wq_ring[ei] = (r0 + 2) % 6
                em.need(eng, tok_ft, tok_cd,
                        wq_tr[ei][r0], wq_tr[ei][(r0 + 1) % 6])
                em.drain(eng)
                emit_hprep(eng, buf, b0 * PART, srng, b0, htile[ei][0])
                emit_hprep(eng, buf, b1 * PART, srng, b1, htile[ei][1])
                em.drain(eng)
                emit_hprep2(eng, buf, b0 * PART, htile[ei][0])
                emit_hprep2(eng, buf, b1 * PART, htile[ei][1])
                em.drain(eng)
                chains = [
                    (b0 * PART, htile[ei][0], cdn, ring * 32 + b0 * 8,
                     accT[ei][0], wqT[ei][r0]),
                    (b1 * PART, htile[ei][1], cdn, ring * 32 + b1 * 8,
                     accT[ei][1], wqT[ei][(r0 + 1) % 6]),
                ]
                toks = emit_chain_group(eng, ch_inc, buf, chains)
                for ci, bb in enumerate((b0, b1)):
                    em.need("tensor", toks[ci],
                            bank_rd[5] if (pair == 0 and ci == 0) else None)
                    wq_tr[ei][(r0 + ci) % 6] = em.emit(
                        "tensor", lambda E, w_=chains[ci][5], bb_=bb:
                        E.transpose(ps[5][:, bb_ * PART:(bb_ + 1) * PART],
                                    w_[:, :], eye_sb[:, :]), inc=PE)
                last_chain = toks[-1]
            cdn_rd[ring] = last_chain
            ftile_rd[flip] = last_chain
            em.need("scalar", em.now(PE), tstage_rd[0])
            tok_h = em.emit("scalar", lambda E: E.copy(
                out=tstage[:, 0:512], in_=ps[5][:, :]), inc=AC)
            em.need("vector", tok_h, tstage_rd[0])
            tok_l = em.emit("vector", lambda E: E.tensor_tensor(
                out=tstage[:, 512:1024], in0=ps[5][:, :],
                in1=tstage[:, 0:512], op=OP.subtract), inc=DV)
            bank_rd[5] = tok_l
            em.need("scalar", tok_l)
            k = s // GS
            off = k * 2 * PART * QsS + (4 * s * PART - k * QsS)
            for plane in range(2):
                dst = RawAP(dsh, off + plane * PART * QsS,
                            [[QsS, PART], [1, 512]])
                dma("scalar", dst, tstage[:, plane * 512:plane * 512 + 512],
                    "st_scr")
            em.need("scalar", em.now("st_scr"))
            tok = em.emit("scalar", lambda E: E.nop(), inc=AC)
            tstage_rd[0] = tok
            return tok

        ag_g = {}
        ag_u = {}
        ag_d = {}

        def emit_ag(k, shard, full, tokname, lad_tok):
            em.need("gpsimd", lad_tok)
            ins = RawAP(shard, k * 2 * PART * QsS, [[1, 2 * PART * QsS]])
            outs = RawAP(full, k * n_cores * 2 * PART * QsS,
                         [[1, n_cores * 2 * PART * QsS]])
            return em.emit("gpsimd", lambda E, i_=ins, o_=outs:
                           E.collective_compute(
                               "AllGather", OP.bypass,
                               replica_groups=[list(range(n_cores))],
                               ins=[i_.opt()], outs=[o_.opt()]), inc="cc")

        for s in range(QCs // 4):
            lad = emit_gu_stage(s)
            if (s + 1) % GS == 0:
                k = s // GS
                ag_g[k] = emit_ag(k, gsh, gate_scr, "g", lad)
                ag_u[k] = emit_ag(k, ush, up_scr, "u", lad)
        dn_todo = list(range(QCs // 4))

        def emit_dn_some(n):
            while n > 0 and dn_todo:
                s = dn_todo.pop(0)
                lad = emit_dn_stage(s)
                if (s + 1) % GS == 0:
                    k = s // GS
                    ag_d[k] = emit_ag(k, dsh, down_scr, "d", lad)
                n -= 1

        guwin_rd = [None, None]
        dwin_rd = [None]
        intu_rd = [None]
        inter_rd = [None, None]
        gu_ld = {}
        dn_ld = [None]
        x_wr = {}

        def emit_gu_load(ic):
            half = ic % 2
            qg = ic * HB * PART
            r = qg // Qs
            qloc = qg - r * Qs
            k = qloc // QsS
            q2 = qloc - k * QsS
            em.need("sync", ag_g[k], ag_u[k], guwin_rd[half])
            off = half * 4 * HB * PART
            base = k * n_cores * 2 * PART * QsS + r * 2 * PART * QsS + q2
            ch = "ld_guA" if half == 0 else "ld_guB"
            for pl, scr in enumerate((gate_scr, gate_scr, up_scr, up_scr)):
                plane = pl % 2
                sa = RawAP(scr, base + plane * PART * QsS,
                           [[QsS, PART], [1, HB * PART]])
                gu_ld[ic] = dma("sync",
                                guwin[:, off + pl * HB * PART:
                                      off + (pl + 1) * HB * PART], sa, ch)

        def emit_dn_load(g):
            r = (2 * g) // IBs
            ibloc = (2 * g) % IBs
            k = ibloc // (IBs // SUBK)
            q2 = (ibloc - k * (IBs // SUBK)) * HB * PART
            em.need("sync", ag_d[k], dwin_rd[0])
            base = k * n_cores * 2 * PART * QsS + r * 2 * PART * QsS + q2
            for plane in range(2):
                sa = RawAP(down_scr, base + plane * PART * QsS,
                           [[QsS, PART], [1, 2 * HB * PART]])
                dn_ld[0] = dma("sync",
                               dwin[:, plane * 2 * HB * PART:
                                    (plane + 1) * 2 * HB * PART], sa, "ld_dw")

        def emit_ic_unit(ic, lnx_tok):
            half = ic % 2
            off = half * 4 * HB * PART
            slot = ic % 2
            toks = {}
            for w in range(2):
                em.need("tensor", gu_ld[ic], lnx_tok, bank_rd[w])
                for kb in range(HB):
                    hi0 = off + 2 * w * HB * PART + kb
                    lo0 = hi0 + HB * PART
                    xh = lnx[:, kb * FT:(kb + 1) * FT]
                    xl = lnx[:, HB * FT + kb * FT:HB * FT + (kb + 1) * FT]
                    em.emit("tensor",
                            lambda E, s0=hi0, x_=xh, kb_=kb, w_=w: E.matmul(
                                ps[w_][:, 0:FT],
                                guwin[:, s0:s0 + (PART - 1) * HB + 1:HB],
                                x_, start=(kb_ == 0), stop=False))
                    em.emit("tensor",
                            lambda E, s0=hi0, x_=xl, w_=w: E.matmul(
                                ps[w_][:, 0:FT],
                                guwin[:, s0:s0 + (PART - 1) * HB + 1:HB],
                                x_, start=False, stop=False))
                    em.emit("tensor",
                            lambda E, s0=lo0, x_=xh, kb_=kb, w_=w: E.matmul(
                                ps[w_][:, 0:FT],
                                guwin[:, s0:s0 + (PART - 1) * HB + 1:HB],
                                x_, start=False, stop=(kb_ == HB - 1)),
                            inc=PE if kb == HB - 1 else None)
                toks[w] = em.now(PE)
            guwin_rd[half] = toks[1]
            sl = slot * 2 * FT
            em.need("scalar", toks[0], inter_rd[slot], intu_rd[0])
            tok_sg = em.emit("scalar", lambda E: E.activation(
                out=intu[:, 0:FT], in_=ps[0][:, 0:FT], func=AF.Silu),
                inc=AC)
            bank_rd[0] = tok_sg
            em.need("vector", tok_sg, toks[1])
            tok_mu = em.emit("vector", lambda E: E.tensor_tensor(
                out=intu[:, 0:FT], in0=intu[:, 0:FT],
                in1=ps[1][:, 0:FT], op=OP.mult), inc=DV)
            bank_rd[1] = tok_mu
            em.need("scalar", tok_mu)
            tok_h = em.emit("scalar", lambda E, s_=sl: E.copy(
                out=inter[:, s_:s_ + FT], in_=intu[:, 0:FT]), inc=AC)
            em.need("vector", tok_h)
            tok_in = em.emit("vector", lambda E, s_=sl: E.tensor_tensor(
                out=inter[:, s_ + FT:s_ + 2 * FT], in0=intu[:, 0:FT],
                in1=inter[:, s_:s_ + FT], op=OP.subtract), inc=DV)
            intu_rd[0] = tok_in
            return tok_in

        def emit_down_unit(tile, g, tok_in0, tok_in1, scale):
            last = None
            for hb in range(HB):
                bank = 2 + hb % 2
                em.need("tensor", dn_ld[0], tok_in0, tok_in1, bank_rd[bank])
                for k in range(2):
                    whi = (k * HB + hb) * PART
                    wlo = 2 * HB * PART + whi
                    xh = inter[:, k * 2 * FT:k * 2 * FT + FT]
                    xl = inter[:, k * 2 * FT + FT:k * 2 * FT + 2 * FT]
                    em.emit("tensor",
                            lambda E, w_=whi, x_=xh, k_=k, b=bank: E.matmul(
                                ps[b][:, 0:FT], dwin[:, w_:w_ + PART], x_,
                                start=(k_ == 0), stop=False))
                    em.emit("tensor",
                            lambda E, w_=whi, x_=xl, b=bank: E.matmul(
                                ps[b][:, 0:FT], dwin[:, w_:w_ + PART], x_,
                                start=False, stop=False))
                    em.emit("tensor",
                            lambda E, w_=wlo, x_=xh, k_=k, b=bank: E.matmul(
                                ps[b][:, 0:FT], dwin[:, w_:w_ + PART], x_,
                                start=False, stop=(k_ == 1)),
                            inc=PE if k == 1 else None)
                tok_mm = em.now(PE)
                em.need("vector", tok_mm, x_wr.get(hb))
                t = em.emit("vector", lambda E, hb_=hb, b=bank, t_=tile, s_=scale:
                            E.scalar_tensor_tensor(
                                out=x_sb[:, hb_ * TOK + t_ * FT:
                                         hb_ * TOK + t_ * FT + FT],
                                in0=ps[b][:, 0:FT], scalar=s_,
                                in1=x_sb[:, hb_ * TOK + t_ * FT:
                                         hb_ * TOK + t_ * FT + FT],
                                op0=OP.mult, op1=OP.add), inc=DV)
                x_wr[hb] = t
                bank_rd[bank] = t
                last = t
            dwin_rd[0] = em.now(PE)
            inter_rd[0] = dwin_rd[0]
            inter_rd[1] = dwin_rd[0]
            return last

        def emit_ffn(first=False):
            x_tok = None
            for tile in range(cfg.NFT):
                lnx_tok = emit_ln(tile * FT, FT, pair=True)
                for g in range(IB // 2):
                    emit_gu_load(2 * g)
                    emit_gu_load(2 * g + 1)
                    t0 = emit_ic_unit(2 * g, lnx_tok)
                    t1 = emit_ic_unit(2 * g + 1, lnx_tok)
                    if first and tile == 0:
                        need_done = cdiv((g % SUBK + 1) * (QCs // 4), SUBK)
                        emit_dn_some(max(need_done - (QCs // 4 - len(dn_todo)),
                                         2))
                    emit_dn_load(g)
                    x_tok = emit_down_unit(tile, g, t0, t1, 0.5)
            return x_tok

        emit_dn_some(10 ** 9)
        x1_tok = emit_ffn(first=False)

        def attn_proj(mb, wten, bank, lnx_tok, wreg, wreg_rd_tok, reg):
            em.need("sync", wreg_rd_tok)
            src = RawAP(wten, mb * HB * PART * PART,
                        [[HB * PART, PART], [1, HB * PART]])
            tw = dma("sync", wreg, src, "ld_w%d" % reg)
            em.need("tensor", tw, lnx_tok, bank_rd[bank])
            for kb in range(HB):
                em.emit("tensor",
                        lambda E, kb_=kb, r=wreg, b=bank, n_=AST: E.matmul(
                            ps[b][:, 0:n_],
                            r[:, kb_ * PART:(kb_ + 1) * PART],
                            lnxa[:, kb_ * AST:(kb_ + 1) * AST],
                            start=(kb_ == 0), stop=(kb_ == HB - 1)),
                        inc=PE if kb == HB - 1 else None)
            lnxa_rd[0] = em.now(PE)
            return lnxa_rd[0]

        def emit_av(mb, tok_i, tok_f, bi, bf):
            c = mb * AST
            em.need("scalar", tok_f, em.now(DV), em.now(PE))
            tok_a = em.emit("scalar", lambda E, c_=c, b_=bf: E.activation(
                out=atile[:, c_:c_ + AST], in_=ps[b_][:, 0:AST],
                func=AF.Sigmoid), inc=AC)
            bank_rd[bf] = tok_a
            em.need("scalar", tok_i)
            tok_si = em.emit("scalar", lambda E, c_=c, b_=bi: E.activation(
                out=vtile[:, c_:c_ + AST], in_=ps[b_][:, 0:AST],
                func=AF.Silu), inc=AC)
            bank_rd[bi] = tok_si
            em.need("vector", tok_a, tok_si)
            em.emit("vector", lambda E, c_=c: E.tensor_scalar(
                out=intu[:, 0:AST], in0=atile[:, c_:c_ + AST],
                scalar1=-1.0, scalar2=1.0, op0=OP.mult, op1=OP.add))
            em.drain("vector")
            return em.emit("vector", lambda E, c_=c: E.tensor_tensor(
                out=vtile[:, c_:c_ + AST], in0=vtile[:, c_:c_ + AST],
                in1=intu[:, 0:AST], op=OP.mult), inc=DV)

        def emit_attn():
            guw32 = guwin[:, :].bitcast(F32)
            WCOLS = HB * PART
            wext32 = (lnx[:, :].bitcast(F32) if HB * FT >= 4 * WCOLS
                      else wext)
            WREG = ([guw32[:, r * WCOLS:(r + 1) * WCOLS] for r in range(4)]
                    + [wext32[:, r * WCOLS:(r + 1) * WCOLS]
                       for r in range(4)])
            pe0 = em.now(PE)
            wreg_rd = [guwin_rd[0], guwin_rd[0], guwin_rd[1], guwin_rd[1],
                       pe0, pe0, pe0, pe0]
            for st in range(cfg.NAST):
                lnx_tok = emit_ln(st * AST, AST, pair=False)
                if st > 0:
                    em.need("scalar", em.now("st_sp"))
                for mb in range(HB):
                    r4 = mb % 4
                    bi = 0 if mb % 2 == 0 else 4
                    bf = 1 if mb % 2 == 0 else 5
                    tok_i = attn_proj(mb, wiT, bi, lnx_tok, WREG[r4],
                                      wreg_rd[r4], mb % 2)
                    wreg_rd[r4] = tok_i
                    tok_f = attn_proj(mb, wfT, bf, lnx_tok, WREG[4 + r4],
                                      wreg_rd[4 + r4], 2 + mb % 2)
                    wreg_rd[4 + r4] = tok_f
                    tok_v = emit_av(mb, tok_i, tok_f, bi, bf)
                    em.need("vector", tok_v)
                    c = mb * AST
                    init_o = 0.0 if st == 0 else sc_sb[:, mb:mb + 1]
                    t_o = em.emit("vector", lambda E, c_=c, i_=init_o:
                                  E.tensor_tensor_scan(
                                      out=otile[:, c_:c_ + AST],
                                      data0=atile[:, c_:c_ + AST],
                                      data1=vtile[:, c_:c_ + AST],
                                      initial=i_, op0=OP.mult, op1=OP.add),
                                  inc=DV)
                    em.need("vector", t_o)
                    t_oc = em.emit("vector", lambda E, c_=c, mb_=mb:
                                   E.tensor_copy(
                                       out=sc_sb[:, mb_:mb_ + 1],
                                       in_=otile[:, c_ + AST - 1:c_ + AST]),
                                   inc=DV)
                    init_p = 1.0 if st == 0 else sc_sb[:, HB + mb:HB + mb + 1]
                    em.need("vector", t_oc)
                    t_p = em.emit("vector", lambda E, c_=c, i_=init_p:
                                  E.tensor_tensor_scan(
                                      out=vtile[:, c_:c_ + AST],
                                      data0=atile[:, c_:c_ + AST],
                                      data1=atile[:, c_:c_ + AST],
                                      initial=i_, op0=OP.mult, op1=OP.bypass),
                                  inc=DV)
                    em.need("vector", t_p)
                    em.emit("vector", lambda E, c_=c, mb_=mb: E.tensor_copy(
                        out=sc_sb[:, HB + mb_:HB + mb_ + 1],
                        in_=vtile[:, c_ + AST - 1:c_ + AST]), inc=DV)
                em.need("gpsimd", em.now(DV))
                dma("gpsimd",
                    RawAP(o_scr, st * HB * AST,
                          [[HB * TOK, PART], [1, HB * AST]]),
                    otile[:, 0:HB * AST], "st_sp")
                dma("gpsimd",
                    RawAP(p_scr, st * HB * AST,
                          [[HB * TOK, PART], [1, HB * AST]]),
                    vtile[:, 0:HB * AST], "st_sp")

            em.need("vector", em.now(DV))
            tok_pack = em.emit("vector", lambda E: E.tensor_copy(
                out=carry[:, 0:2 * HB], in_=sc_sb[:, 0:2 * HB]), inc=DV)
            em.need("gpsimd", tok_pack)
            tok_cdma = dma("gpsimd", carry_src.ap(), carry[:, 0:2 * HB], "st_carry")
            em.need("gpsimd", tok_cdma)
            tok_cc = em.emit("gpsimd", lambda E: E.collective_compute(
                "AllGather", OP.bypass,
                replica_groups=[list(range(n_cores))],
                ins=[carry_src.ap().opt()],
                outs=[carry_gth.ap().opt()]), inc="cc")
            em.need("gpsimd", tok_cc)
            tok_gth = dma("gpsimd", cg_sb[:, 0:n_cores * 2 * HB],
                          RawAP(carry_gth, 0,
                                [[2 * HB, PART], [PART * 2 * HB, n_cores],
                                 [1, 2 * HB]]), "ld_gth")
            em.need("vector", tok_gth, tok_consts)
            em.emit("vector", lambda E: E.tensor_scalar(
                out=carry2[:, 0:HB], in0=cg_sb[:, 0:HB],
                scalar1=0.0, scalar2=None, op0=OP.mult))
            em.drain("vector")
            for k in range(n_cores):
                base = k * 2 * HB
                if k == 0:
                    em.emit("vector", lambda E, b=base: E.tensor_copy(
                        out=carry[:, 0:HB], in_=cg_sb[:, b:b + HB]))
                else:
                    em.emit("vector", lambda E, b=base: E.tensor_tensor(
                        out=carry[:, 0:HB], in0=carry[:, 0:HB],
                        in1=cg_sb[:, b + HB:b + 2 * HB], op=OP.mult))
                    em.drain("vector")
                    em.emit("vector", lambda E, b=base: E.tensor_tensor(
                        out=carry[:, 0:HB], in0=carry[:, 0:HB],
                        in1=cg_sb[:, b:b + HB], op=OP.add))
                em.drain("vector")
                em.emit("vector", lambda E, k_=k: E.tensor_tensor(
                    out=intu[:, 0:HB], in0=carry[:, 0:HB],
                    in1=csel_sb[:, k_ * HB:(k_ + 1) * HB], op=OP.mult))
                em.drain("vector")
                em.emit("vector", lambda E: E.tensor_tensor(
                    out=carry2[:, 0:HB], in0=carry2[:, 0:HB],
                    in1=intu[:, 0:HB], op=OP.add))
                em.drain("vector")
            tok_oin = em.emit("vector", lambda E: E.nop(), inc=DV)

            last_x = None
            prev_wo = None
            for st in range(cfg.NAST):
                lnx_tok = emit_ln(st * AST, AST, pair=False)
                em.need("gpsimd", prev_wo,
                        tok_oin if st == 0 else None,
                        em.now("st_sp") if st == 0 else None)
                dma("gpsimd", otile[:, 0:HB * AST],
                    RawAP(o_scr, st * HB * AST,
                          [[HB * TOK, PART], [1, HB * AST]]), "ld_ob")
                t_lp = dma("gpsimd", atile[:, 0:HB * AST],
                           RawAP(p_scr, st * HB * AST,
                                 [[HB * TOK, PART], [1, HB * AST]]), "ld_ob")
                rr_toks = []
                for mb in range(HB):
                    em.need("vector", t_lp, tok_oin, prev_wo)
                    t_r = em.emit("vector", lambda E, mb_=mb:
                                  E.scalar_tensor_tensor(
                                      out=otile[:, mb_ * AST:(mb_ + 1) * AST],
                                      in0=atile[:, mb_ * AST:(mb_ + 1) * AST],
                                      scalar=carry2[:, mb_:mb_ + 1],
                                      in1=otile[:, mb_ * AST:(mb_ + 1) * AST],
                                      op0=OP.mult, op1=OP.add), inc=DV)
                    rr_toks.append(t_r)
                pe_bar = em.now(PE)
                sq_war = None
                for mb in range(HB):
                    em.need("scalar", rr_toks[-1], sq_war)
                    t = em.emit("scalar", lambda E, mb_=mb: E.activation(
                        out=intu[:, 0:AST],
                        in_=otile[:, mb_ * AST:(mb_ + 1) * AST],
                        func=AF.Square), inc=AC)
                    em.need("tensor", t, bank_rd[7] if mb == 0 else None,
                            pe_bar if mb == 0 else None)
                    sq_war = em.emit("tensor", lambda E, mb_=mb: E.matmul(
                        ps[7][0:1, 0:AST], onec_sb[:, :], intu[:, 0:AST],
                        start=(mb_ == 0), stop=(mb_ == HB - 1)), inc=PE)
                em.need("vector", em.now(PE))
                em.emit("vector", lambda E: E.tensor_scalar(
                    out=lnrowA[0:1, 0:AST], in0=ps[7][0:1, 0:AST],
                    scalar1=1.0 / H, scalar2=1e-5, op0=OP.mult, op1=OP.add))
                em.drain("vector")
                tok_r_ = em.emit("vector", lambda E: E.nop(), inc=DV)
                em.need("scalar", tok_r_)
                tok_rs0 = em.emit("scalar", lambda E: E.activation(
                    out=lnrowA[0:1, 0:AST], in_=lnrowA[0:1, 0:AST],
                    func=AF.Sqrt), inc=AC)
                em.need("vector", tok_rs0)
                tok_rs = em.emit("vector", lambda E: E.reciprocal(
                    out=lnrowA[0:1, 0:AST], in_=lnrowA[0:1, 0:AST]), inc=DV)
                em.need("tensor", tok_rs)
                em.emit("tensor", lambda E: E.matmul(
                    ps[7][:, 0:AST], oner_sb[:, :], lnrowA[0:1, 0:AST],
                    start=True, stop=True), inc=PE)
                em.need("vector", em.now(PE))
                tok_R = em.emit("vector", lambda E: E.tensor_copy(
                    out=bcA[:, 0:AST], in_=ps[7][:, 0:AST]), inc=DV)
                bank_rd[7] = tok_R
                sg_toks = []
                for mb in range(HB):
                    r4 = mb % 4
                    bg = 2 if mb % 2 == 0 else 6
                    tok_g = attn_proj(mb, wgT, bg, lnx_tok, WREG[r4],
                                      wreg_rd[r4], mb % 2)
                    wreg_rd[r4] = tok_g
                    em.need("scalar", tok_g, rr_toks[-1])
                    t_sg = em.emit("scalar", lambda E, c_=mb * AST, b_=bg:
                                   E.activation(
                                       out=vtile[:, c_:c_ + AST],
                                       in_=ps[b_][:, 0:AST], func=AF.Silu),
                                   inc=AC)
                    bank_rd[bg] = t_sg
                    sg_toks.append(t_sg)
                gtoks = []
                for mb in range(HB):
                    c = mb * AST
                    em.need("vector", tok_R, sg_toks[mb])
                    em.emit("vector", lambda E, mb_=mb: E.tensor_tensor(
                        out=otile[:, mb_ * AST:(mb_ + 1) * AST],
                        in0=otile[:, mb_ * AST:(mb_ + 1) * AST],
                        in1=bcA[:, 0:AST], op=OP.mult))
                    em.drain("vector")
                    t = em.emit("vector", lambda E, mb_=mb, c_=c:
                                E.scalar_tensor_tensor(
                                    out=otile[:, mb_ * AST:(mb_ + 1) * AST],
                                    in0=otile[:, mb_ * AST:(mb_ + 1) * AST],
                                    scalar=gnw_sb[:, mb_:mb_ + 1],
                                    in1=vtile[:, c_:c_ + AST],
                                    op0=OP.mult, op1=OP.mult), inc=DV)
                    gtoks.append(t)
                for mb in range(HB):
                    ri = 4 + mb % 4
                    bw = 3 if mb % 2 == 0 else 4
                    em.need("sync", wreg_rd[ri])
                    src = RawAP(woT, mb * HB * PART * PART,
                                [[HB * PART, PART], [1, HB * PART]])
                    tk = dma("sync", WREG[ri], src, "ld_w%d" % (2 + mb % 2))
                    em.need("tensor", tk, gtoks[-1], bank_rd[bw])
                    for kb in range(HB):
                        em.emit("tensor", lambda E, kb_=kb, r_=ri, b_=bw:
                                E.matmul(
                                    ps[b_][:, 0:AST],
                                    WREG[r_][:, kb_ * PART:(kb_ + 1) * PART],
                                    otile[:, kb_ * AST:(kb_ + 1) * AST],
                                    start=(kb_ == 0), stop=(kb_ == HB - 1)),
                                inc=PE if kb == HB - 1 else None)
                    tok_wo = em.now(PE)
                    wreg_rd[ri] = tok_wo
                    prev_wo = tok_wo
                    em.need("vector", tok_wo, x_wr.get(mb))
                    last_x = em.emit("vector", lambda E, mb_=mb, s_=st, b_=bw:
                                     E.tensor_tensor(
                                         out=x_sb[:, mb_ * TOK + s_ * AST:
                                                  mb_ * TOK + s_ * AST + AST],
                                         in0=x_sb[:, mb_ * TOK + s_ * AST:
                                                  mb_ * TOK + s_ * AST + AST],
                                         in1=ps[b_][:, 0:AST], op=OP.add),
                                     inc=DV)
                    x_wr[mb] = last_x
                    bank_rd[bw] = last_x
            guwin_rd[0] = guwin_rd[1] = em.now(PE)
            dwin_rd[0] = em.now(PE)
            em.need("vector", em.now(PE))
            em.need("scalar", em.now(PE))
            return last_x

        if cfg.stop_after == "ffn1":
            x_final = x1_tok
        else:
            x2_tok = emit_attn()
            if cfg.stop_after == "attn":
                x_final = x2_tok
            else:
                x_final = emit_ffn()

        em.need("gpsimd", x_final, em.now(DV))
        dst = RawAP(y_out, 0, [[TOK, PART], [PART * TOK, HB], [1, TOK]])
        tok_out = dma("gpsimd", dst, x_sb[:, 0:HB * TOK], "st_y")
        em.need("gpsimd", tok_out)

        with nc.Block() as block:
            @block.sync
            def _(E):
                for f in em.ops["sync"]:
                    f(E)

            @block.tensor
            def _(E):
                for f in em.ops["tensor"]:
                    f(E)

            @block.vector
            def _(E):
                for f in em.ops["vector"]:
                    f(E)

            @block.scalar
            def _(E):
                for f in em.ops["scalar"]:
                    f(E)

            @block.gpsimd
            def _(E):
                for f in em.ops["gpsimd"]:
                    f(E)

    return nc


def make_inputs(cfg, L0_attn_hidden_states, L0_attn_Wi, L0_attn_Wf,
                L0_attn_Wg, L0_attn_Wo, L0_attn_gnorm_w,
                L0_ffn_gate_cp, L0_ffn_up_cp, L0_ffn_down_cp):
    grid_f, grid_f2, grid_f3, jq, sq = host_grid(cfg)
    perm = down_perm(cfg)

    def tab4(a, core):
        sh = a[core * cfg.Qs:(core + 1) * cfg.Qs]
        g = sh.reshape(cfg.QCs // 4, 4, PART, PART).transpose(0, 2, 1, 3)
        return np.ascontiguousarray(g.reshape(cfg.QCs // 4, PART, 512))
    f32 = np.float32
    M = host_M() / cfg.n_cores
    assert cfg.n_cores in (1, 2, 4, 8, 16)
    M_gu = np.zeros((10, 16), f32)
    M_gu[0:5, 0:8] = M
    M_gu[5:10, 8:16] = M
    common = {
        "wiT": tile_lhsT(np.ascontiguousarray(np.asarray(L0_attn_Wi, f32).T)),
        "wfT": tile_lhsT(np.ascontiguousarray(np.asarray(L0_attn_Wf, f32).T)),
        "wgT": tile_lhsT(np.ascontiguousarray(np.asarray(L0_attn_Wg, f32).T)),
        "woT": tile_lhsT(np.ascontiguousarray(np.asarray(L0_attn_Wo, f32).T)),
        "gnw": np.ascontiguousarray(np.asarray(L0_attn_gnorm_w, f32).reshape(cfg.HB, PART).T),
        "cp_g": np.ascontiguousarray(L0_ffn_gate_cp, dtype=f32),
        "cp_u": np.ascontiguousarray(L0_ffn_up_cp, dtype=f32),
        "cp_d": np.ascontiguousarray(L0_ffn_down_cp, dtype=f32),
        "iota": np.ascontiguousarray(
            np.broadcast_to(np.arange(PART, dtype=f32), (PART, PART))),
        "eye": np.eye(PART, dtype=f32),
        "ones_col": np.ones((PART, 1), dtype=f32),
        "ones_row": np.ones((1, PART), dtype=f32),
        "M_gu": M_gu,
        "M_d": np.ascontiguousarray(M),
    }
    x = np.asarray(L0_attn_hidden_states, f32).reshape(cfg.T, cfg.H)
    in_maps = []
    for c in range(cfg.n_cores):
        sl = slice(c * cfg.TOK, (c + 1) * cfg.TOK)
        csel = np.zeros((PART, cfg.n_cores * cfg.HB), dtype=f32)
        if c > 0:
            csel[:, (c - 1) * cfg.HB:c * cfg.HB] = 1.0
        m = dict(common)
        m["x_in"] = np.ascontiguousarray(x[sl].T)
        m["csel"] = csel
        m["ftab"] = tab4(grid_f, c)
        m["f2tab"] = tab4(grid_f2, c)
        m["f3tab"] = tab4(grid_f3, c)
        m["fdtab"] = tab4(grid_f[perm], c)
        m["f2dtab"] = tab4(grid_f2[perm], c)
        m["f3dtab"] = tab4(grid_f3[perm], c)
        m["stab"] = np.ascontiguousarray(
            sq[c * cfg.Qs:(c + 1) * cfg.Qs].reshape(cfg.QCs, PART).T)
        m["sdtab"] = np.ascontiguousarray(
            sq[perm][c * cfg.Qs:(c + 1) * cfg.Qs].reshape(cfg.QCs, PART).T)
        in_maps.append(m)
    return in_maps


def run(cfg, **inputs):
    nc = build_program(cfg)
    in_maps = make_inputs(cfg, **inputs)
    res = run_bass_kernel_spmd(nc, in_maps, core_ids=list(range(cfg.n_cores)))
    shards = [np.asarray(r["y_out"]) for r in res.results]
    out = np.concatenate([s.T for s in shards], axis=0)
    return np.ascontiguousarray(out.reshape(1, cfg.T, cfg.H))


def kernel(**inputs):
    cfg = full_cfg()
    return run(cfg, **inputs)


# version 27 (final)
# speedup vs baseline: 1.0586x; 1.0008x over previous
import numpy as np
from contextlib import ExitStack

import concourse.bass as bass
import concourse.mybir as mybir
from concourse.bass_utils import run_bass_kernel_spmd
from bass_rust import AP as RawAP

F32 = mybir.dt.float32
BF16 = mybir.dt.bfloat16
OP = mybir.AluOpType
AF = mybir.ActivationFunctionType

PART = 128


def cdiv(a, b):
    return -(-a // b)


class Cfg:
    def __init__(self, H, I, T, n_cores, NCTRL, FT, AST, WQ, stop_after=None):
        self.H, self.I, self.T = H, I, T
        self.n_cores = n_cores
        self.NCTRL = NCTRL
        self.TOK = T // n_cores
        self.FT = FT
        self.AST = AST
        self.HB = H // PART
        self.IB = I // PART
        self.N = I * H
        self.Q = self.N // PART
        self.QC = self.Q // PART
        self.WQ = WQ
        self.NFT = self.TOK // FT
        self.NAST = self.TOK // AST
        self.QCs = self.QC // n_cores
        self.Qs = self.Q // n_cores
        self.SUBK = min(4, self.QCs // 4)
        self.stop_after = stop_after
        assert self.N % PART == 0 and self.Q % PART == 0
        assert self.N / (NCTRL - 1) >= PART, "a 128-tile may span >2 segments"
        assert T % n_cores == 0 and self.TOK % FT == 0 and self.TOK % AST == 0
        assert self.QC % 4 == 0 and self.Q % WQ == 0 and WQ % PART == 0
        assert self.QC % (4 * n_cores) == 0
        assert (self.QCs // 4) % self.SUBK == 0
        assert self.IB % n_cores == 0 and (self.IB // n_cores) % 2 == 0
        assert self.IB % 2 == 0


def full_cfg(stop_after=None):
    return Cfg(H=2048, I=8192, T=8192, n_cores=8, NCTRL=130156,
               FT=512, AST=128, WQ=512, stop_after=stop_after)


def host_grid(cfg):
    t = np.linspace(0.0, cfg.NCTRL - 1.0, cfg.N, dtype=np.float64)
    i = np.clip(np.floor(t).astype(np.int64), 0, cfg.NCTRL - 2)
    f = (t - i).astype(np.float32)
    f2 = f * f
    f3 = f2 * f
    it = i.reshape(cfg.Q, PART)
    jq = it[:, 0].copy()
    ok = (it == jq[:, None]) | (it == jq[:, None] + 1)
    assert ok.all(), "tile spans more than two segments"
    sq = (it == jq[:, None]).sum(1).astype(np.float32)
    return (f.reshape(cfg.Q, PART), f2.reshape(cfg.Q, PART),
            f3.reshape(cfg.Q, PART), jq, sq)


def host_pieces(jq):
    Q = len(jq)
    pieces = []
    q0 = 0
    for q in range(1, Q + 1):
        if q == Q or jq[q] != jq[q - 1] + 1:
            pieces.append((q0, q, int(jq[q0])))
            q0 = q
    return pieces


def piece_dmas(pieces, NCTRL, WQ):
    out = []
    for (q0, q1, j0) in pieces:
        q = q0
        while q < q1:
            win = q // WQ
            qe = min(q1, (win + 1) * WQ)
            L = qe - q
            j = j0 + (q - q0)
            if (j - 1) >= 0 and (j + L - 1 + 3) <= NCTRL - 1:
                out.append((win, q - win * WQ, L, ("full", j - 1)))
            else:
                rows = []
                for k in range(5):
                    runs = []
                    c = 0
                    while c < L:
                        idx = j + c + k - 1
                        if idx < 0:
                            ce = min(L, c + (0 - idx))
                            runs.append((c, ce - c, 0, 0))
                            c = ce
                        elif idx > NCTRL - 1:
                            runs.append((c, L - c, NCTRL - 1, 0))
                            c = L
                        else:
                            ce = min(L, c + ((NCTRL - 1) - idx) + 1)
                            runs.append((c, ce - c, idx, 1))
                            c = ce
                    rows.append((k, runs))
                out.append((win, q - win * WQ, L, ("rows", rows)))
            q = qe
    return out


def host_M():
    A = np.zeros((5, 4), dtype=np.float64)
    A[1, 0] = 1.0
    A[0, 1] = -0.5; A[2, 1] = 0.5
    A[0, 2] = 1.0; A[1, 2] = -2.5; A[2, 2] = 2.0; A[3, 2] = -0.5
    A[0, 3] = -0.5; A[1, 3] = 1.5; A[2, 3] = -1.5; A[3, 3] = 0.5
    B = np.zeros((5, 4), dtype=np.float64)
    B[1:5, :] = A[0:4, :]
    return np.concatenate([A, B - A], axis=1).astype(np.float32)


def down_perm(cfg):
    hb = np.arange(cfg.HB)
    ib = np.arange(cfg.IB)
    h_off = np.arange(PART)
    p = ((hb[None, :, None] * PART + h_off[None, None, :]) * cfg.IB
         + ib[:, None, None])
    return p.reshape(-1)


def tile_lhsT(Wt):
    K, M = Wt.shape
    a = Wt.reshape(K // PART, PART, M // PART, PART)
    a = a.transpose(2, 1, 0, 3)
    return np.ascontiguousarray(a.reshape(M // PART, PART, K))


ENGS = ("sync", "tensor", "vector", "scalar", "gpsimd")


class Chan:
    def __init__(self, stack, nc, name, dma=False):
        self.sem = stack.enter_context(nc.semaphore(name))
        self.count = 0
        self.dma = dma


class Emitter:
    def __init__(self, stack, nc):
        self.nc = nc
        self.ops = {e: [] for e in ENGS}
        self.seq = {e: 0 for e in ENGS}
        self.last_drain = {e: -1 for e in ENGS}
        self.waited = {e: {} for e in ENGS}
        self.chans = {}
        for name in ("ld_x", "ld_ftA", "ld_ftB", "ld_phA", "ld_phB",
                     "ld_pdA", "ld_pdB", "ld_phAs", "ld_phBs",
                     "ld_cdA", "ld_cdB",
                     "ld_cgA", "ld_cgB",
                     "ld_guA", "ld_guB", "ld_dw",
                     "ld_w0", "ld_w1", "ld_w2", "ld_w3", "ld_gth", "ld_ob",
                     "st_scr", "st_cimg", "st_prm", "st_carry", "st_y",
                     "st_sp",
                     "pe", "dv", "ac", "po", "cc"):
            self.chans[name] = Chan(stack, nc, name,
                                    dma=name.startswith(("ld_", "st_")))

    def ch(self, name):
        return self.chans[name]

    def now(self, chan_name):
        c = self.chans[chan_name]
        return ("sem", c, c.count)

    def need(self, eng, *tokens):
        drains = False
        waits = []
        for tok in tokens:
            if tok is None:
                continue
            if tok[0] == "sem":
                _, chan, val = tok
                if chan.dma and val > 0:
                    val = chan.count
                if val > 0 and self.waited[eng].get(id(chan), 0) < val:
                    self.waited[eng][id(chan)] = val
                    waits.append((chan.sem, val))
            elif tok[0] == "eng":
                _, e2, seq = tok
                assert e2 == eng, f"cross-engine eng token {e2}->{eng}"
                if seq > self.last_drain[eng]:
                    drains = True
            else:
                raise RuntimeError(f"bad token {tok}")
        if drains:
            self.last_drain[eng] = self.seq[eng]
            self.ops[eng].append(lambda E: E.drain())
            self.seq[eng] += 1
        for sem, val in waits:
            self.ops[eng].append(lambda E, s=sem, v=val: E.wait_ge(s, v))

    def drain(self, eng):
        self.last_drain[eng] = self.seq[eng]
        self.ops[eng].append(lambda E: E.drain())
        self.seq[eng] += 1

    def emit(self, eng, fn, inc=None, dinc=1):
        if inc is not None:
            chan = self.chans[inc]
            chan.count += dinc
            val = chan.count
            self.ops[eng].append(
                lambda E, f=fn, c=chan, n=dinc: f(E).then_inc(c.sem, n))
            self.seq[eng] += 1
            return ("sem", chan, val)
        self.ops[eng].append(lambda E, f=fn: f(E))
        self.seq[eng] += 1
        return ("eng", eng, self.seq[eng] - 1)


def build_program(cfg):
    nc = bass.Bass("TRN2", target_bir_lowering=False)
    H, I, TOK, HB, IB = cfg.H, cfg.I, cfg.TOK, cfg.HB, cfg.IB
    FT, AST, Q, QC, WQ, NCTRL = cfg.FT, cfg.AST, cfg.Q, cfg.QC, cfg.WQ, cfg.NCTRL
    n_cores = cfg.n_cores

    _, _, _, jq, _ = host_grid(cfg)
    pdmas = piece_dmas(host_pieces(jq), NCTRL, WQ)
    pdmas_by_win = {}
    for pd in pdmas:
        pdmas_by_win.setdefault(pd[0], []).append(pd)

    def din(name, shape):
        return nc.dram_tensor(name, list(shape), F32, kind="ExternalInput")

    x_in = din("x_in", (H, TOK))
    wiT = din("wiT", (HB, PART, HB * PART))
    wfT = din("wfT", (HB, PART, HB * PART))
    wgT = din("wgT", (HB, PART, HB * PART))
    woT = din("woT", (HB, PART, HB * PART))
    gnw_in = din("gnw", (PART, H // PART))
    cp_g = din("cp_g", (NCTRL,))
    cp_u = din("cp_u", (NCTRL,))
    cp_d = din("cp_d", (NCTRL,))
    ftab = din("ftab", (cfg.QCs // 4, PART, 512))
    f2tab = din("f2tab", (cfg.QCs // 4, PART, 512))
    f3tab = din("f3tab", (cfg.QCs // 4, PART, 512))
    fdtab = din("fdtab", (cfg.QCs // 4, PART, 512))
    f2dtab = din("f2dtab", (cfg.QCs // 4, PART, 512))
    f3dtab = din("f3dtab", (cfg.QCs // 4, PART, 512))
    stab = din("stab", (PART, cfg.QCs))
    sdtab = din("sdtab", (PART, cfg.QCs))
    iota_in = din("iota", (PART, PART))
    eye_in = din("eye", (PART, PART))
    ones_col = din("ones_col", (PART, 1))
    ones_row = din("ones_row", (1, PART))
    M_gu_in = din("M_gu", (10, 16))
    M_d_in = din("M_d", (5, 8))
    csel_in = din("csel", (PART, n_cores * HB))

    y_out = nc.dram_tensor("y_out", [H, TOK], F32, kind="ExternalOutput")

    _aspace = {"addr_space": "Shared"} if n_cores > 4 else {}
    gate_scr = nc.dram_tensor("gate_scr", [2 * QC * PART * PART], BF16,
                              **_aspace)
    up_scr = nc.dram_tensor("up_scr", [2 * QC * PART * PART], BF16,
                            **_aspace)
    down_scr = nc.dram_tensor("down_scr", [2 * QC * PART * PART], BF16,
                              **_aspace)
    Qs, QCs, SUBK = cfg.Qs, cfg.QCs, cfg.SUBK
    QsS = Qs // SUBK
    IBs = IB // n_cores
    gsh = nc.dram_tensor("gsh", [2 * Qs * PART], BF16)
    ush = nc.dram_tensor("ush", [2 * Qs * PART], BF16)
    dsh = nc.dram_tensor("dsh", [2 * Qs * PART], BF16)
    cimg_gu = nc.dram_tensor("cimg_gu", [Q * 16], F32)
    cimg_d2 = nc.dram_tensor("cimg_d2", [Q * 8], F32)
    cglo = nc.dram_tensor("cglo", [Qs * 16], F32)
    cdlo = nc.dram_tensor("cdlo", [Qs * 8], F32)
    cimg = nc.dram_tensor("cimg", [Q * 8], F32)
    carry_src = nc.dram_tensor("carry_src", [PART, 2 * HB], F32)
    carry_gth = nc.dram_tensor("carry_gth", [n_cores * PART, 2 * HB], F32)
    o_scr = nc.dram_tensor("o_scr", [PART, HB * TOK], F32)
    p_scr = nc.dram_tensor("p_scr", [PART, HB * TOK], F32)

    stack = ExitStack()
    with stack:
        def sb(name, p, n):
            return stack.enter_context(nc.sbuf_tensor(name, [p, n], F32))

        x_sb = sb("x_sb", PART, HB * TOK)
        iota_sb = sb("iota_sb", PART, PART)
        eye_sb = sb("eye_sb", PART, PART)
        onec_sb = sb("onec_sb", PART, 1)
        oner_sb = sb("oner_sb", 1, PART)
        gnw_sb = sb("gnw_sb", PART, HB)
        mgu_sb = sb("mgu_sb", 10, 16)
        md_sb = sb("md_sb", 5, 8)
        csel_sb = sb("csel_sb", PART, n_cores * HB)
        def sbb(name, p, n):
            return stack.enter_context(nc.sbuf_tensor(name, [p, n], BF16))

        lnx = sbb("lnx", PART, 2 * HB * FT)
        inter = sbb("inter", PART, 2 * 2 * FT)
        intu = sb("intu", PART, FT)
        lnu = intu[:, 0:FT]
        guwin = sbb("guwin", PART, 2 * 4 * HB * PART)
        arena = sb("arena", PART, max(4 * HB * AST, 2 * HB * PART))
        wext = (sb("wext", PART, 4 * HB * PART)
                if HB * FT < 4 * HB * PART else None)
        atile = arena[:, 0:HB * AST]
        vtile = arena[:, HB * AST:2 * HB * AST]
        otile = arena[:, 2 * HB * AST:3 * HB * AST]
        lnxa = arena[:, 3 * HB * AST:4 * HB * AST]
        dwin = arena[:, 0:2 * HB * PART].bitcast(BF16)
        phat = sb("phat", 10, 2 * WQ)
        phd = sb("phd", 5, 2 * WQ)
        ftileA = sb("ftileA", PART, 3 * 512)
        ftileB = sb("ftileB", PART, 3 * 512)
        sringA = sb("sringA", PART, 4)
        sringB = sb("sringB", PART, 4)
        htile = [[sb(f"htile{e}{k}", PART, 4 * PART) for k in range(2)]
                 for e in range(1)]
        accT = [[sb(f"acc{e}_{k}", PART, PART) for k in range(4)]
                for e in range(1)]
        wqT = [[sb(f"wq{e}_{k}", PART, PART) for k in range(6)]
               for e in range(1)]
        cgu = sb("cgu", PART, 2 * 64)
        cdn = sb("cdn", PART, 2 * 32)
        tstage = sbb("tstage", PART, 4 * 512)
        pstage = sb("pstage", PART, 64)
        pbuf = intu[:, FT - AST:FT]
        carry = sb("carry", PART, 2 * HB)
        carry2 = sb("carry2", PART, HB)
        cg_sb = sb("cg_sb", PART, n_cores * 2 * HB)
        sc_sb = sb("sc_sb", PART, 3 * HB)
        bcA = sb("bcA", PART, max(FT, AST))
        bcB = sb("bcB", PART, max(FT, AST))
        lnrowA = bcB[0:1, :]
        lnrowB = bcA[0:1, :]

        ps = [stack.enter_context(nc.psum_tensor(f"ps{k}", [PART, 512], F32))
              for k in range(8)]

        em = Emitter(stack, nc)
        PE, DV, AC = "pe", "dv", "ac"

        bank_rd = {k: None for k in range(8)}

        def dma(eng_name, out_ap, in_ap, ch):
            return em.emit(eng_name,
                           lambda E, o=out_ap, i=in_ap: E.dma_start(out=o, in_=i),
                           inc=ch, dinc=16)

        tok_x = dma("sync", x_sb[:, :],
                    RawAP(x_in, 0, [[TOK, PART], [PART * TOK, HB], [1, TOK]]),
                    "ld_x")
        dma("sync", iota_sb[:, :], iota_in.ap(), "ld_x")
        dma("sync", eye_sb[:, :], eye_in.ap(), "ld_x")
        dma("sync", onec_sb[:, :], ones_col.ap(), "ld_x")
        dma("sync", oner_sb[:, :], ones_row.ap(), "ld_x")
        dma("sync", gnw_sb[:, :], gnw_in.ap(), "ld_x")
        dma("sync", mgu_sb[:, :], M_gu_in.ap(), "ld_x")
        dma("sync", md_sb[:, :], M_d_in.ap(), "ld_x")
        tok_consts = dma("sync", csel_sb[:, :], csel_in.ap(), "ld_x")

        PIECE_Q = ("gpsimd", "sync")

        def emit_phat_window(win, dest_sb, half, cps, war_tok, ch):
            toks = []
            base = half * WQ
            for wi, cph in enumerate(cps):
                q = PIECE_Q[wi % len(PIECE_Q)]
                em.need(q, war_tok)
                r0 = wi * 5
                chq = ch + ("", "s")[wi % len(PIECE_Q)]
                tok = None
                for (_, col0, L, spec) in pdmas_by_win.get(win, []):
                    if spec[0] == "full":
                        sap = RawAP(cph, spec[1], [[1, 5], [1, L]])
                        tok = dma(q,
                                  dest_sb[r0:r0 + 5,
                                          base + col0:base + col0 + L],
                                  sap, chq)
                    else:
                        for (k, runs) in spec[1]:
                            for (c, ln, off, step) in runs:
                                dims = ([[step, ln], [1, 1]] if step == 0
                                        else [[step, ln]])
                                sap = RawAP(cph, off, dims)
                                tok = dma(
                                    q,
                                    dest_sb[r0 + k:r0 + k + 1,
                                            base + col0 + c:
                                            base + col0 + c + ln],
                                    sap, chq)
                toks.append(tok)
            return toks

        NW = Q // WQ
        CPW = WQ // PART
        BATCH = min(16, CPW)
        GS = (QCs // 4) // SUBK

        pp_done = [None]

        def emit_prepass(ph_sb, cps, chA, chB, msb, width, dst_dram, blocked,
                         interleave=False, other=None):
            ph_tok = [None, None]
            ph_rd = [None, None]
            for win in range(NW):
                half = win % 2
                ph_tok[half] = emit_phat_window(
                    win, ph_sb, half, cps, ph_rd[half],
                    chA if half == 0 else chB)
                for b0 in range(0, CPW, BATCH):
                    nb = min(BATCH, CPW - b0)
                    done = pp_done[0]
                    em.need("tensor", *ph_tok[half])
                    em.need("tensor", tok_consts, bank_rd[4])
                    for b in range(nb):
                        cq = b0 + b
                        lhs = ph_sb[:, half * WQ + cq * PART:
                                    half * WQ + (cq + 1) * PART]
                        em.emit("tensor",
                                lambda E, l=lhs, b_=b, w_=width, m_=msb:
                                E.matmul(ps[4][:, b_ * w_:(b_ + 1) * w_],
                                         l, m_[:, :], start=True, stop=True),
                                inc=PE if b == nb - 1 else None)
                    tok_mm = em.now(PE)
                    ph_rd[half] = tok_mm
                    em.need("scalar", tok_mm, done)
                    tok_ev = em.emit("scalar",
                                     lambda E, n_=nb * width: E.copy(
                                         out=pstage[:, 0:n_],
                                         in_=ps[4][:, 0:n_]),
                                     inc=AC)
                    bank_rd[4] = tok_ev
                    em.need("scalar", tok_ev)
                    cq0 = win * CPW + b0
                    if blocked:
                        r = cq0 // QCs
                        cqloc = cq0 % QCs
                        dst = RawAP(dst_dram,
                                    r * PART * QCs * width + cqloc * width,
                                    [[QCs * width, PART], [1, nb * width]])
                    else:
                        dst = RawAP(dst_dram, cq0 * width,
                                    [[QC * width, PART], [1, nb * width]])
                    pp_done[0] = dma("scalar", dst, pstage[:, 0:nb * width],
                                     "st_cimg")
                if interleave:
                    if other is not None:
                        next(other, None)
                    yield pp_done[0]
            if other is not None:
                for _ in other:
                    yield pp_done[0]
            yield pp_done[0]

        pp_gu = emit_prepass(phat, [cp_g, cp_u], "ld_phA", "ld_phB",
                             mgu_sb, 16, cimg_gu, True, interleave=True)
        pp_d = emit_prepass(phd, [cp_d], "ld_pdA", "ld_pdB",
                            md_sb, 8, cimg, False, interleave=True,
                            other=pp_gu)
        for _ in pp_d:
            pass
        pp1 = pp2 = pp_done[0]

        Pc = 128 // IB if IB <= 128 else 1
        assert IB <= 128 and 128 % IB == 0
        em.need("gpsimd", pp1, pp2)
        prm = None
        for ib in range(IB):
            r = ib // IBs
            dbase = r * PART * QCs * 8 + (ib % IBs) * HB * 8
            for B in range(Pc):
                dst = RawAP(cimg_d2, dbase + B * QCs * 8,
                            [[Pc * QCs * 8, IB], [1, HB * 8]])
                sa = RawAP(cimg, (ib + IB * B) * QC * 8,
                           [[8, IB], [IB * 8, HB], [1, 8]])
                prm = dma("gpsimd", dst, sa, "st_prm")
        em.need("gpsimd", prm)
        tok_rs_gu = em.emit("gpsimd", lambda E: E.collective_compute(
            "ReduceScatter", OP.add, replica_groups=[list(range(n_cores))],
            ins=[cimg_gu.ap().opt()], outs=[cglo.ap().opt()]), inc="cc")
        tok_rs_d = em.emit("gpsimd", lambda E: E.collective_compute(
            "ReduceScatter", OP.add, replica_groups=[list(range(n_cores))],
            ins=[cimg_d2.ap().opt()], outs=[cdlo.ap().opt()]), inc="cc")

        lnxa_rd = [None]

        def emit_ln(src_off, ntok, pair):
            pe_bar = em.now(PE) if pair else lnxa_rd[0]
            dv_bar = em.now(DV)
            st_bar = em.now("st_cimg")
            for c in range(HB):
                em.need("tensor", tok_x, tok_consts,
                        bank_rd[7] if c == 0 else None,
                        pe_bar if c == 0 else None)
                em.emit("tensor",
                        lambda E, c_=c, n_=ntok, o_=src_off: E.matmul(
                            ps[7][0:1, 0:n_], onec_sb[:, :],
                            x_sb[:, c_ * TOK + o_:c_ * TOK + o_ + n_],
                            start=(c_ == 0), stop=(c_ == HB - 1)),
                        inc=PE if c == HB - 1 else None)
            em.need("vector", em.now(PE))
            em.drain("vector")
            tok_mean = em.emit("vector", lambda E, n_=ntok: E.tensor_scalar(
                out=lnrowA[0:1, 0:n_], in0=ps[7][0:1, 0:n_],
                scalar1=1.0 / H, scalar2=None, op0=OP.mult), inc=DV)
            sq_war = None
            for c in range(HB):
                em.need("scalar", tok_x, sq_war,
                        dv_bar if c == 0 else None,
                        st_bar if c == 0 else None)
                t = em.emit("scalar",
                            lambda E, c_=c, n_=ntok, o_=src_off: E.activation(
                                out=intu[:, 0:n_],
                                in_=x_sb[:, c_ * TOK + o_:c_ * TOK + o_ + n_],
                                func=AF.Square),
                            inc=AC)
                em.need("tensor", t, tok_mean if c == 0 else None)
                sq_war = em.emit("tensor",
                                 lambda E, n_=ntok, first=(c == 0),
                                 last=(c == HB - 1): E.matmul(
                                     ps[7][0:1, 0:n_], onec_sb[:, :],
                                     intu[:, 0:n_], start=first, stop=last),
                                 inc=PE)
            em.need("vector", em.now(PE))
            em.emit("vector", lambda E, n_=ntok: E.tensor_scalar(
                out=lnrowB[0:1, 0:n_], in0=ps[7][0:1, 0:n_],
                scalar1=1.0 / H, scalar2=None, op0=OP.mult))
            em.drain("vector")
            em.emit("vector", lambda E, n_=ntok: E.tensor_tensor(
                out=intu[0:1, 0:n_], in0=lnrowA[0:1, 0:n_],
                in1=lnrowA[0:1, 0:n_], op=OP.mult))
            em.drain("vector")
            em.emit("vector", lambda E, n_=ntok: E.tensor_tensor(
                out=lnrowB[0:1, 0:n_], in0=lnrowB[0:1, 0:n_],
                in1=intu[0:1, 0:n_], op=OP.subtract))
            em.drain("vector")
            tok_var = em.emit("vector", lambda E, n_=ntok: E.tensor_scalar(
                out=lnrowB[0:1, 0:n_], in0=lnrowB[0:1, 0:n_],
                scalar1=1.0, scalar2=1e-5, op0=OP.mult, op1=OP.add), inc=DV)
            em.need("scalar", tok_var)
            tok_sq = em.emit("scalar", lambda E, n_=ntok: E.activation(
                out=lnrowB[0:1, 0:n_], in_=lnrowB[0:1, 0:n_], func=AF.Sqrt),
                inc=AC)
            em.need("vector", tok_sq)
            tok_a = em.emit("vector", lambda E, n_=ntok: E.reciprocal(
                out=lnrowB[0:1, 0:n_], in_=lnrowB[0:1, 0:n_]), inc=DV)
            em.drain("vector")
            em.emit("vector", lambda E, n_=ntok: E.tensor_tensor(
                out=lnrowA[0:1, 0:n_], in0=lnrowA[0:1, 0:n_],
                in1=lnrowB[0:1, 0:n_], op=OP.mult))
            em.drain("vector")
            tok_b = em.emit("vector", lambda E, n_=ntok: E.tensor_scalar(
                out=lnrowA[0:1, 0:n_], in0=lnrowA[0:1, 0:n_],
                scalar1=-1.0, scalar2=None, op0=OP.mult), inc=DV)
            em.need("tensor", tok_a, tok_b)
            em.emit("tensor", lambda E, n_=ntok: E.matmul(
                ps[7][:, 0:n_], oner_sb[:, :], lnrowB[0:1, 0:n_],
                start=True, stop=True), inc=PE)
            em.need("vector", em.now(PE))
            tok_evA = em.emit("vector", lambda E, n_=ntok: E.tensor_copy(
                out=bcA[:, 0:n_], in_=ps[7][:, 0:n_]), inc=DV)
            em.need("tensor", tok_evA)
            em.emit("tensor", lambda E, n_=ntok: E.matmul(
                ps[7][:, 0:n_], oner_sb[:, :], lnrowA[0:1, 0:n_],
                start=True, stop=True), inc=PE)
            em.need("vector", em.now(PE))
            tok_evB = em.emit("vector", lambda E, n_=ntok: E.tensor_copy(
                out=bcB[:, 0:n_], in_=ps[7][:, 0:n_]), inc=DV)
            bank_rd[7] = tok_evB
            last = None
            if pair:
                for c in range(HB):
                    em.need("vector", tok_evA, tok_evB, last)
                    em.emit("vector", lambda E, c_=c, n_=ntok, o_=src_off:
                            E.tensor_tensor(
                                out=lnu[:, 0:n_],
                                in0=x_sb[:, c_ * TOK + o_:c_ * TOK + o_ + n_],
                                in1=bcA[:, 0:n_], op=OP.mult))
                    em.drain("vector")
                    tok_u = em.emit("vector", lambda E, n_=ntok:
                                    E.tensor_tensor(
                                        out=lnu[:, 0:n_], in0=lnu[:, 0:n_],
                                        in1=bcB[:, 0:n_], op=OP.add), inc=DV)
                    em.need("scalar", tok_u)
                    tok_h = em.emit("scalar", lambda E, c_=c, n_=ntok:
                                    E.copy(out=lnx[:, c_ * n_:(c_ + 1) * n_],
                                           in_=lnu[:, 0:n_]), inc=AC)
                    em.need("vector", tok_h)
                    last = em.emit("vector", lambda E, c_=c, n_=ntok:
                                   E.tensor_tensor(
                                       out=lnx[:, HB * FT + c_ * n_:
                                               HB * FT + (c_ + 1) * n_],
                                       in0=lnu[:, 0:n_],
                                       in1=lnx[:, c_ * n_:(c_ + 1) * n_],
                                       op=OP.subtract), inc=DV)
            else:
                for c in range(HB):
                    em.need("vector", tok_evA, tok_evB)
                    em.emit("vector", lambda E, c_=c, n_=ntok, o_=src_off:
                            E.tensor_tensor(
                                out=lnxa[:, c_ * n_:(c_ + 1) * n_],
                                in0=x_sb[:, c_ * TOK + o_:c_ * TOK + o_ + n_],
                                in1=bcA[:, 0:n_], op=OP.mult))
                    em.drain("vector")
                    last = em.emit("vector", lambda E, c_=c, n_=ntok:
                                   E.tensor_tensor(
                                       out=lnxa[:, c_ * n_:(c_ + 1) * n_],
                                       in0=lnxa[:, c_ * n_:(c_ + 1) * n_],
                                       in1=bcB[:, 0:n_], op=OP.add),
                                   inc=DV if c == HB - 1 else None)
            return last

        def emit_hprep(eng, fsb, foff, srng, sidx, ht):
            em.emit(eng, lambda E: E.tensor_scalar(
                out=ht[:, 0:PART], in0=iota_sb[:, :],
                scalar1=srng[:, sidx:sidx + 1], scalar2=None, op0=OP.is_ge))

        def emit_hprep2(eng, fsb, foff, ht):
            f_ap = fsb[:, foff:foff + PART]
            f2_ap = fsb[:, 512 + foff:512 + foff + PART]
            f3_ap = fsb[:, 1024 + foff:1024 + foff + PART]
            em.emit(eng, lambda E: E.tensor_tensor(
                out=ht[:, PART:2 * PART], in0=ht[:, 0:PART],
                in1=f_ap, op=OP.mult))
            em.emit(eng, lambda E: E.tensor_tensor(
                out=ht[:, 2 * PART:3 * PART], in0=ht[:, 0:PART],
                in1=f2_ap, op=OP.mult))
            em.emit(eng, lambda E: E.tensor_tensor(
                out=ht[:, 3 * PART:4 * PART], in0=ht[:, 0:PART],
                in1=f3_ap, op=OP.mult))

        def emit_chain_group(eng, ch_inc, fsb, chains):
            def co(csb, coff, x):
                return csb[:, coff + x:coff + x + 1]

            for (foff, ht, csb, coff, acc, wq) in chains:
                f_ap = fsb[:, foff:foff + PART]
                em.emit(eng, lambda E, f=f_ap, c1=co(csb, coff, 1),
                        c0=co(csb, coff, 0), a=acc: E.tensor_scalar(
                            out=a[:, :], in0=f, scalar1=c1, scalar2=c0,
                            op0=OP.mult, op1=OP.add))
            for step, (sel, cx) in enumerate(
                    (("f2", 2), ("f3", 3), ("H", 4), ("fH", 5), ("f2H", 6))):
                em.drain(eng)
                for (foff, ht, csb, coff, acc, wq) in chains:
                    if sel == "f2":
                        tbl = fsb[:, 512 + foff:512 + foff + PART]
                    elif sel == "f3":
                        tbl = fsb[:, 1024 + foff:1024 + foff + PART]
                    elif sel == "H":
                        tbl = ht[:, 0:PART]
                    elif sel == "fH":
                        tbl = ht[:, PART:2 * PART]
                    else:
                        tbl = ht[:, 2 * PART:3 * PART]
                    em.emit(eng, lambda E, t=tbl, c_=co(csb, coff, cx),
                            a=acc: E.scalar_tensor_tensor(
                                out=a[:, :], in0=t, scalar=c_, in1=a[:, :],
                                op0=OP.mult, op1=OP.add))
            em.drain(eng)
            toks = []
            for (foff, ht, csb, coff, acc, wq) in chains:
                toks.append(em.emit(eng, lambda E, t=ht[:, 3 * PART:4 * PART],
                                    c_=co(csb, coff, 7), a=acc, w=wq:
                                    E.scalar_tensor_tensor(
                                        out=w[:, :], in0=t, scalar=c_,
                                        in1=a[:, :], op0=OP.mult, op1=OP.add),
                                    inc=ch_inc))
            return toks

        gu_win_state = {"win": -1, "tok": [None, None], "rd": [None, None]}
        stage_scr_gu = {}
        stage_scr_dn = {}
        ftile_rd = [None, None]
        cgu_rd = [None, None]
        cdn_rd = [None, None]
        wq_tr = [[None] * 6 for _ in range(1)]
        wq_ring = [0]
        tstage_rd = [None]

        def load_ftile(kind, grp, flip):
            buf = ftileA if flip == 0 else ftileB
            srng = sringA if flip == 0 else sringB
            srcs = ((ftab, f2tab, f3tab) if kind == "gu"
                    else (fdtab, f2dtab, f3dtab))
            stsrc = stab if kind == "gu" else sdtab
            ch = "ld_ftA" if flip == 0 else "ld_ftB"
            em.need("sync", ftile_rd[flip])
            for ti, tsrc in enumerate(srcs):
                src = RawAP(tsrc, (grp // 4) * PART * 512,
                            [[512, PART], [1, 512]])
                dma("sync", buf[:, ti * 512:(ti + 1) * 512], src, ch)
            tok = dma("sync", srng[:, 0:4],
                      RawAP(stsrc, grp, [[QCs, PART], [1, 4]]), ch)
            return buf, srng, tok

        def emit_gu_stage(s):
            ring = s % 2
            ch = "ld_cgA" if ring == 0 else "ld_cgB"
            em.need("scalar", tok_rs_gu, cgu_rd[ring])
            tok_cg = dma("scalar", cgu[:, ring * 64:(ring + 1) * 64],
                         RawAP(cglo, 4 * s * 16, [[QCs * 16, PART], [1, 64]]),
                         ch)
            flip = s % 2
            buf, srng, tok_ft = load_ftile("gu", 4 * s, flip)
            ei = 0
            eng = "vector"
            ch_inc = DV
            last_chain = None
            for pair in range(2):
                b0, b1 = 2 * pair, 2 * pair + 1
                r0 = wq_ring[ei]
                wq_ring[ei] = (r0 + 4) % 6
                em.need(eng, tok_ft, tok_cg,
                        *[wq_tr[ei][(r0 + k) % 6] for k in range(4)])
                em.drain(eng)
                emit_hprep(eng, buf, b0 * PART, srng, b0, htile[ei][0])
                emit_hprep(eng, buf, b1 * PART, srng, b1, htile[ei][1])
                em.drain(eng)
                emit_hprep2(eng, buf, b0 * PART, htile[ei][0])
                emit_hprep2(eng, buf, b1 * PART, htile[ei][1])
                em.drain(eng)
                chains = [
                    (b0 * PART, htile[ei][0], cgu, ring * 64 + b0 * 16 + 0,
                     accT[ei][0], wqT[ei][r0]),
                    (b0 * PART, htile[ei][0], cgu, ring * 64 + b0 * 16 + 8,
                     accT[ei][1], wqT[ei][(r0 + 1) % 6]),
                    (b1 * PART, htile[ei][1], cgu, ring * 64 + b1 * 16 + 0,
                     accT[ei][2], wqT[ei][(r0 + 2) % 6]),
                    (b1 * PART, htile[ei][1], cgu, ring * 64 + b1 * 16 + 8,
                     accT[ei][3], wqT[ei][(r0 + 3) % 6]),
                ]
                toks = emit_chain_group(eng, ch_inc, buf, chains)
                for ci, (b_, wi) in enumerate(((b0, 0), (b0, 1), (b1, 0), (b1, 1))):
                    bank_id = 5 if wi == 0 else 6
                    em.need("tensor", toks[ci],
                            bank_rd[bank_id] if (pair == 0 and b_ == b0) else None)
                    tr = em.emit(
                        "tensor", lambda E, w_=chains[ci][5], bi=bank_id, bb=b_:
                        E.transpose(ps[bi][:, bb * PART:(bb + 1) * PART],
                                    w_[:, :], eye_sb[:, :]), inc=PE)
                    wq_tr[ei][(r0 + ci) % 6] = tr
                last_chain = toks[-1]
            cgu_rd[ring] = last_chain
            ftile_rd[flip] = last_chain
            tok_tr = em.now(PE)
            em.need("scalar", tok_tr, tstage_rd[0])
            tok_h5 = em.emit("scalar", lambda E: E.copy(
                out=tstage[:, 0:512], in_=ps[5][:, :]), inc=AC)
            tok_h6 = em.emit("scalar", lambda E: E.copy(
                out=tstage[:, 1024:1536], in_=ps[6][:, :]), inc=AC)
            em.need("vector", tok_h5, tok_h6, tstage_rd[0])
            tok_l5 = em.emit("vector", lambda E: E.tensor_tensor(
                out=tstage[:, 512:1024], in0=ps[5][:, :],
                in1=tstage[:, 0:512], op=OP.subtract), inc=DV)
            tok_l6 = em.emit("vector", lambda E: E.tensor_tensor(
                out=tstage[:, 1536:2048], in0=ps[6][:, :],
                in1=tstage[:, 1024:1536], op=OP.subtract), inc=DV)
            bank_rd[5] = tok_l5
            bank_rd[6] = tok_l6
            em.need("scalar", tok_l5, tok_l6)
            k = s // GS
            off = k * 2 * PART * QsS + (4 * s * PART - k * QsS)
            for pl, (sh_, t0) in enumerate(((gsh, 0), (gsh, 512),
                                            (ush, 1024), (ush, 1536))):
                plane = pl % 2
                dst = RawAP(sh_, off + plane * PART * QsS,
                            [[QsS, PART], [1, 512]])
                dma("scalar", dst, tstage[:, t0:t0 + 512], "st_scr")
            em.need("scalar", em.now("st_scr"))
            tok = em.emit("scalar", lambda E: E.nop(), inc=AC)
            tstage_rd[0] = tok
            return tok

        def emit_dn_stage(s):
            ring = s % 2
            ch = "ld_cdA" if ring == 0 else "ld_cdB"
            em.need("scalar", tok_rs_d, cdn_rd[ring])
            tok_cd = dma("scalar", cdn[:, ring * 32:(ring + 1) * 32],
                         RawAP(cdlo, 4 * s * 8, [[QCs * 8, PART], [1, 32]]),
                         ch)
            flip = s % 2
            buf, srng, tok_ft = load_ftile("dn", 4 * s, flip)
            ei = 0
            eng = "vector"
            ch_inc = DV
            last_chain = None
            for pair in range(2):
                b0, b1 = 2 * pair, 2 * pair + 1
                r0 = wq_ring[ei]
                wq_ring[ei] = (r0 + 2) % 6
                em.need(eng, tok_ft, tok_cd,
                        wq_tr[ei][r0], wq_tr[ei][(r0 + 1) % 6])
                em.drain(eng)
                emit_hprep(eng, buf, b0 * PART, srng, b0, htile[ei][0])
                emit_hprep(eng, buf, b1 * PART, srng, b1, htile[ei][1])
                em.drain(eng)
                emit_hprep2(eng, buf, b0 * PART, htile[ei][0])
                emit_hprep2(eng, buf, b1 * PART, htile[ei][1])
                em.drain(eng)
                chains = [
                    (b0 * PART, htile[ei][0], cdn, ring * 32 + b0 * 8,
                     accT[ei][0], wqT[ei][r0]),
                    (b1 * PART, htile[ei][1], cdn, ring * 32 + b1 * 8,
                     accT[ei][1], wqT[ei][(r0 + 1) % 6]),
                ]
                toks = emit_chain_group(eng, ch_inc, buf, chains)
                for ci, bb in enumerate((b0, b1)):
                    em.need("tensor", toks[ci],
                            bank_rd[5] if (pair == 0 and ci == 0) else None)
                    wq_tr[ei][(r0 + ci) % 6] = em.emit(
                        "tensor", lambda E, w_=chains[ci][5], bb_=bb:
                        E.transpose(ps[5][:, bb_ * PART:(bb_ + 1) * PART],
                                    w_[:, :], eye_sb[:, :]), inc=PE)
                last_chain = toks[-1]
            cdn_rd[ring] = last_chain
            ftile_rd[flip] = last_chain
            em.need("scalar", em.now(PE), tstage_rd[0])
            tok_h = em.emit("scalar", lambda E: E.copy(
                out=tstage[:, 0:512], in_=ps[5][:, :]), inc=AC)
            em.need("vector", tok_h, tstage_rd[0])
            tok_l = em.emit("vector", lambda E: E.tensor_tensor(
                out=tstage[:, 512:1024], in0=ps[5][:, :],
                in1=tstage[:, 0:512], op=OP.subtract), inc=DV)
            bank_rd[5] = tok_l
            em.need("scalar", tok_l)
            k = s // GS
            off = k * 2 * PART * QsS + (4 * s * PART - k * QsS)
            for plane in range(2):
                dst = RawAP(dsh, off + plane * PART * QsS,
                            [[QsS, PART], [1, 512]])
                dma("scalar", dst, tstage[:, plane * 512:plane * 512 + 512],
                    "st_scr")
            em.need("scalar", em.now("st_scr"))
            tok = em.emit("scalar", lambda E: E.nop(), inc=AC)
            tstage_rd[0] = tok
            return tok

        ag_g = {}
        ag_u = {}
        ag_d = {}

        def emit_ag(k, shard, full, tokname, lad_tok):
            em.need("gpsimd", lad_tok)
            ins = RawAP(shard, k * 2 * PART * QsS, [[1, 2 * PART * QsS]])
            outs = RawAP(full, k * n_cores * 2 * PART * QsS,
                         [[1, n_cores * 2 * PART * QsS]])
            return em.emit("gpsimd", lambda E, i_=ins, o_=outs:
                           E.collective_compute(
                               "AllGather", OP.bypass,
                               replica_groups=[list(range(n_cores))],
                               ins=[i_.opt()], outs=[o_.opt()]), inc="cc")

        for s in range(QCs // 4):
            lad = emit_gu_stage(s)
            if (s + 1) % GS == 0:
                k = s // GS
                ag_g[k] = emit_ag(k, gsh, gate_scr, "g", lad)
                ag_u[k] = emit_ag(k, ush, up_scr, "u", lad)
        dn_todo = list(range(QCs // 4))

        def emit_dn_some(n):
            while n > 0 and dn_todo:
                s = dn_todo.pop(0)
                lad = emit_dn_stage(s)
                if (s + 1) % GS == 0:
                    k = s // GS
                    ag_d[k] = emit_ag(k, dsh, down_scr, "d", lad)
                n -= 1

        guwin_rd = [None, None]
        dwin_rd = [None]
        intu_rd = [None]
        inter_rd = [None, None]
        gu_ld = {}
        dn_ld = [None]
        x_wr = {}

        def emit_gu_load(ic):
            half = ic % 2
            qg = ic * HB * PART
            r = qg // Qs
            qloc = qg - r * Qs
            k = qloc // QsS
            q2 = qloc - k * QsS
            em.need("sync", ag_g[k], ag_u[k], guwin_rd[half])
            off = half * 4 * HB * PART
            base = k * n_cores * 2 * PART * QsS + r * 2 * PART * QsS + q2
            ch = "ld_guA" if half == 0 else "ld_guB"
            for pl, scr in enumerate((gate_scr, gate_scr, up_scr, up_scr)):
                plane = pl % 2
                sa = RawAP(scr, base + plane * PART * QsS,
                           [[QsS, PART], [1, HB * PART]])
                gu_ld[ic] = dma("sync",
                                guwin[:, off + pl * HB * PART:
                                      off + (pl + 1) * HB * PART], sa, ch)

        def emit_dn_load(g):
            r = (2 * g) // IBs
            ibloc = (2 * g) % IBs
            k = ibloc // (IBs // SUBK)
            q2 = (ibloc - k * (IBs // SUBK)) * HB * PART
            em.need("sync", ag_d[k], dwin_rd[0])
            base = k * n_cores * 2 * PART * QsS + r * 2 * PART * QsS + q2
            for plane in range(2):
                sa = RawAP(down_scr, base + plane * PART * QsS,
                           [[QsS, PART], [1, 2 * HB * PART]])
                dn_ld[0] = dma("sync",
                               dwin[:, plane * 2 * HB * PART:
                                    (plane + 1) * 2 * HB * PART], sa, "ld_dw")

        def emit_ic_unit(ic, lnx_tok):
            half = ic % 2
            off = half * 4 * HB * PART
            slot = ic % 2
            toks = {}
            icb = (0, 1) if ic % 2 == 0 else (4, 5)
            for w in range(2):
                em.need("tensor", gu_ld[ic], lnx_tok, bank_rd[icb[w]])
                for kb in range(HB):
                    hi0 = off + 2 * w * HB * PART + kb
                    lo0 = hi0 + HB * PART
                    xh = lnx[:, kb * FT:(kb + 1) * FT]
                    xl = lnx[:, HB * FT + kb * FT:HB * FT + (kb + 1) * FT]
                    em.emit("tensor",
                            lambda E, s0=hi0, x_=xh, kb_=kb, w_=icb[w]:
                            E.matmul(
                                ps[w_][:, 0:FT],
                                guwin[:, s0:s0 + (PART - 1) * HB + 1:HB],
                                x_, start=(kb_ == 0), stop=False))
                    em.emit("tensor",
                            lambda E, s0=hi0, x_=xl, w_=icb[w]: E.matmul(
                                ps[w_][:, 0:FT],
                                guwin[:, s0:s0 + (PART - 1) * HB + 1:HB],
                                x_, start=False, stop=False))
                    em.emit("tensor",
                            lambda E, s0=lo0, x_=xh, kb_=kb, w_=icb[w]:
                            E.matmul(
                                ps[w_][:, 0:FT],
                                guwin[:, s0:s0 + (PART - 1) * HB + 1:HB],
                                x_, start=False, stop=(kb_ == HB - 1)),
                            inc=PE if kb == HB - 1 else None)
                toks[w] = em.now(PE)
            guwin_rd[half] = toks[1]
            sl = slot * 2 * FT
            em.need("scalar", toks[0], inter_rd[slot], intu_rd[0])
            tok_sg = em.emit("scalar", lambda E, b_=icb[0]: E.activation(
                out=intu[:, 0:FT], in_=ps[b_][:, 0:FT], func=AF.Silu),
                inc=AC)
            bank_rd[icb[0]] = tok_sg
            em.need("vector", tok_sg, toks[1])
            tok_mu = em.emit("vector", lambda E, b_=icb[1]: E.tensor_tensor(
                out=intu[:, 0:FT], in0=intu[:, 0:FT],
                in1=ps[b_][:, 0:FT], op=OP.mult), inc=DV)
            bank_rd[icb[1]] = tok_mu
            em.need("scalar", tok_mu)
            tok_h = em.emit("scalar", lambda E, s_=sl: E.copy(
                out=inter[:, s_:s_ + FT], in_=intu[:, 0:FT]), inc=AC)
            em.need("vector", tok_h)
            tok_in = em.emit("vector", lambda E, s_=sl: E.tensor_tensor(
                out=inter[:, s_ + FT:s_ + 2 * FT], in0=intu[:, 0:FT],
                in1=inter[:, s_:s_ + FT], op=OP.subtract), inc=DV)
            intu_rd[0] = tok_in
            return tok_in

        def emit_down_unit(tile, g, tok_in0, tok_in1, scale):
            last = None
            for hb in range(HB):
                bank = (2, 3, 6)[hb % 3]
                em.need("tensor", dn_ld[0], tok_in0, tok_in1, bank_rd[bank])
                for k in range(2):
                    whi = (k * HB + hb) * PART
                    wlo = 2 * HB * PART + whi
                    xh = inter[:, k * 2 * FT:k * 2 * FT + FT]
                    xl = inter[:, k * 2 * FT + FT:k * 2 * FT + 2 * FT]
                    em.emit("tensor",
                            lambda E, w_=whi, x_=xh, k_=k, b=bank: E.matmul(
                                ps[b][:, 0:FT], dwin[:, w_:w_ + PART], x_,
                                start=(k_ == 0), stop=False))
                    em.emit("tensor",
                            lambda E, w_=whi, x_=xl, b=bank: E.matmul(
                                ps[b][:, 0:FT], dwin[:, w_:w_ + PART], x_,
                                start=False, stop=False))
                    em.emit("tensor",
                            lambda E, w_=wlo, x_=xh, k_=k, b=bank: E.matmul(
                                ps[b][:, 0:FT], dwin[:, w_:w_ + PART], x_,
                                start=False, stop=(k_ == 1)),
                            inc=PE if k == 1 else None)
                tok_mm = em.now(PE)
                em.need("vector", tok_mm, x_wr.get(hb))
                t = em.emit("vector", lambda E, hb_=hb, b=bank, t_=tile, s_=scale:
                            E.scalar_tensor_tensor(
                                out=x_sb[:, hb_ * TOK + t_ * FT:
                                         hb_ * TOK + t_ * FT + FT],
                                in0=ps[b][:, 0:FT], scalar=s_,
                                in1=x_sb[:, hb_ * TOK + t_ * FT:
                                         hb_ * TOK + t_ * FT + FT],
                                op0=OP.mult, op1=OP.add), inc=DV)
                x_wr[hb] = t
                bank_rd[bank] = t
                last = t
            dwin_rd[0] = em.now(PE)
            inter_rd[0] = dwin_rd[0]
            inter_rd[1] = dwin_rd[0]
            return last

        def emit_ffn(first=False):
            x_tok = None
            for tile in range(cfg.NFT):
                lnx_tok = emit_ln(tile * FT, FT, pair=True)
                for g in range(IB // 2):
                    emit_gu_load(2 * g)
                    emit_gu_load(2 * g + 1)
                    t0 = emit_ic_unit(2 * g, lnx_tok)
                    t1 = emit_ic_unit(2 * g + 1, lnx_tok)
                    if first and tile == 0:
                        need_done = cdiv((g % SUBK + 1) * (QCs // 4), SUBK)
                        emit_dn_some(max(need_done - (QCs // 4 - len(dn_todo)),
                                         2))
                    emit_dn_load(g)
                    x_tok = emit_down_unit(tile, g, t0, t1, 0.5)
            return x_tok

        emit_dn_some(10 ** 9)
        em.need("tensor", em.now(AC), em.now(DV))
        x1_tok = emit_ffn(first=False)

        def attn_proj(mb, wten, bank, lnx_tok, wreg, wreg_rd_tok, reg):
            em.need("sync", wreg_rd_tok)
            src = RawAP(wten, mb * HB * PART * PART,
                        [[HB * PART, PART], [1, HB * PART]])
            tw = dma("sync", wreg, src, "ld_w%d" % reg)
            em.need("tensor", tw, lnx_tok, bank_rd[bank])
            for kb in range(HB):
                em.emit("tensor",
                        lambda E, kb_=kb, r=wreg, b=bank, n_=AST: E.matmul(
                            ps[b][:, 0:n_],
                            r[:, kb_ * PART:(kb_ + 1) * PART],
                            lnxa[:, kb_ * AST:(kb_ + 1) * AST],
                            start=(kb_ == 0), stop=(kb_ == HB - 1)),
                        inc=PE if kb == HB - 1 else None)
            lnxa_rd[0] = em.now(PE)
            return lnxa_rd[0]

        def emit_av(mb, tok_i, tok_f, bi, bf):
            c = mb * AST
            em.need("scalar", tok_f, em.now(DV), em.now(PE))
            tok_a = em.emit("scalar", lambda E, c_=c, b_=bf: E.activation(
                out=atile[:, c_:c_ + AST], in_=ps[b_][:, 0:AST],
                func=AF.Sigmoid), inc=AC)
            bank_rd[bf] = tok_a
            em.need("scalar", tok_i)
            tok_si = em.emit("scalar", lambda E, c_=c, b_=bi: E.activation(
                out=vtile[:, c_:c_ + AST], in_=ps[b_][:, 0:AST],
                func=AF.Silu), inc=AC)
            bank_rd[bi] = tok_si
            em.need("vector", tok_a, tok_si)
            em.emit("vector", lambda E, c_=c: E.tensor_scalar(
                out=intu[:, 0:AST], in0=atile[:, c_:c_ + AST],
                scalar1=-1.0, scalar2=1.0, op0=OP.mult, op1=OP.add))
            em.drain("vector")
            return em.emit("vector", lambda E, c_=c: E.tensor_tensor(
                out=vtile[:, c_:c_ + AST], in0=vtile[:, c_:c_ + AST],
                in1=intu[:, 0:AST], op=OP.mult), inc=DV)

        def emit_attn():
            guw32 = guwin[:, :].bitcast(F32)
            WCOLS = HB * PART
            wext32 = (lnx[:, :].bitcast(F32) if HB * FT >= 4 * WCOLS
                      else wext)
            WREG = ([guw32[:, r * WCOLS:(r + 1) * WCOLS] for r in range(4)]
                    + [wext32[:, r * WCOLS:(r + 1) * WCOLS]
                       for r in range(4)])
            pe0 = em.now(PE)
            wreg_rd = [guwin_rd[0], guwin_rd[0], guwin_rd[1], guwin_rd[1],
                       pe0, pe0, pe0, pe0]
            for st in range(cfg.NAST):
                lnx_tok = emit_ln(st * AST, AST, pair=False)
                if st > 0:
                    em.need("scalar", em.now("st_sp"))
                for mb in range(HB):
                    r4 = mb % 4
                    bi = 0 if mb % 2 == 0 else 4
                    bf = 1 if mb % 2 == 0 else 5
                    tok_i = attn_proj(mb, wiT, bi, lnx_tok, WREG[r4],
                                      wreg_rd[r4], mb % 2)
                    wreg_rd[r4] = tok_i
                    tok_f = attn_proj(mb, wfT, bf, lnx_tok, WREG[4 + r4],
                                      wreg_rd[4 + r4], 2 + mb % 2)
                    wreg_rd[4 + r4] = tok_f
                    tok_v = emit_av(mb, tok_i, tok_f, bi, bf)
                    em.need("vector", tok_v)
                    c = mb * AST
                    init_o = 0.0 if st == 0 else sc_sb[:, mb:mb + 1]
                    t_o = em.emit("vector", lambda E, c_=c, i_=init_o:
                                  E.tensor_tensor_scan(
                                      out=otile[:, c_:c_ + AST],
                                      data0=atile[:, c_:c_ + AST],
                                      data1=vtile[:, c_:c_ + AST],
                                      initial=i_, op0=OP.mult, op1=OP.add),
                                  inc=DV)
                    em.need("vector", t_o)
                    t_oc = em.emit("vector", lambda E, c_=c, mb_=mb:
                                   E.tensor_copy(
                                       out=sc_sb[:, mb_:mb_ + 1],
                                       in_=otile[:, c_ + AST - 1:c_ + AST]),
                                   inc=DV)
                    init_p = 1.0 if st == 0 else sc_sb[:, HB + mb:HB + mb + 1]
                    em.need("vector", t_oc)
                    t_p = em.emit("vector", lambda E, c_=c, i_=init_p:
                                  E.tensor_tensor_scan(
                                      out=vtile[:, c_:c_ + AST],
                                      data0=atile[:, c_:c_ + AST],
                                      data1=atile[:, c_:c_ + AST],
                                      initial=i_, op0=OP.mult, op1=OP.bypass),
                                  inc=DV)
                    em.need("vector", t_p)
                    em.emit("vector", lambda E, c_=c, mb_=mb: E.tensor_copy(
                        out=sc_sb[:, HB + mb_:HB + mb_ + 1],
                        in_=vtile[:, c_ + AST - 1:c_ + AST]), inc=DV)
                em.need("gpsimd", em.now(DV))
                dma("gpsimd",
                    RawAP(o_scr, st * HB * AST,
                          [[HB * TOK, PART], [1, HB * AST]]),
                    otile[:, 0:HB * AST], "st_sp")
                dma("gpsimd",
                    RawAP(p_scr, st * HB * AST,
                          [[HB * TOK, PART], [1, HB * AST]]),
                    vtile[:, 0:HB * AST], "st_sp")

            em.need("vector", em.now(DV))
            tok_pack = em.emit("vector", lambda E: E.tensor_copy(
                out=carry[:, 0:2 * HB], in_=sc_sb[:, 0:2 * HB]), inc=DV)
            em.need("gpsimd", tok_pack)
            tok_cdma = dma("gpsimd", carry_src.ap(), carry[:, 0:2 * HB], "st_carry")
            em.need("gpsimd", tok_cdma)
            tok_cc = em.emit("gpsimd", lambda E: E.collective_compute(
                "AllGather", OP.bypass,
                replica_groups=[list(range(n_cores))],
                ins=[carry_src.ap().opt()],
                outs=[carry_gth.ap().opt()]), inc="cc")
            em.need("gpsimd", tok_cc)
            tok_gth = dma("gpsimd", cg_sb[:, 0:n_cores * 2 * HB],
                          RawAP(carry_gth, 0,
                                [[2 * HB, PART], [PART * 2 * HB, n_cores],
                                 [1, 2 * HB]]), "ld_gth")
            em.need("vector", tok_gth, tok_consts)
            em.emit("vector", lambda E: E.tensor_scalar(
                out=carry2[:, 0:HB], in0=cg_sb[:, 0:HB],
                scalar1=0.0, scalar2=None, op0=OP.mult))
            em.drain("vector")
            for k in range(n_cores):
                base = k * 2 * HB
                if k == 0:
                    em.emit("vector", lambda E, b=base: E.tensor_copy(
                        out=carry[:, 0:HB], in_=cg_sb[:, b:b + HB]))
                else:
                    em.emit("vector", lambda E, b=base: E.tensor_tensor(
                        out=carry[:, 0:HB], in0=carry[:, 0:HB],
                        in1=cg_sb[:, b + HB:b + 2 * HB], op=OP.mult))
                    em.drain("vector")
                    em.emit("vector", lambda E, b=base: E.tensor_tensor(
                        out=carry[:, 0:HB], in0=carry[:, 0:HB],
                        in1=cg_sb[:, b:b + HB], op=OP.add))
                em.drain("vector")
                em.emit("vector", lambda E, k_=k: E.tensor_tensor(
                    out=intu[:, 0:HB], in0=carry[:, 0:HB],
                    in1=csel_sb[:, k_ * HB:(k_ + 1) * HB], op=OP.mult))
                em.drain("vector")
                em.emit("vector", lambda E: E.tensor_tensor(
                    out=carry2[:, 0:HB], in0=carry2[:, 0:HB],
                    in1=intu[:, 0:HB], op=OP.add))
                em.drain("vector")
            tok_oin = em.emit("vector", lambda E: E.nop(), inc=DV)

            last_x = None
            prev_wo = None
            for st in range(cfg.NAST):
                lnx_tok = emit_ln(st * AST, AST, pair=False)
                em.need("gpsimd", prev_wo,
                        tok_oin if st == 0 else None,
                        em.now("st_sp") if st == 0 else None)
                dma("gpsimd", otile[:, 0:HB * AST],
                    RawAP(o_scr, st * HB * AST,
                          [[HB * TOK, PART], [1, HB * AST]]), "ld_ob")
                t_lp = dma("gpsimd", atile[:, 0:HB * AST],
                           RawAP(p_scr, st * HB * AST,
                                 [[HB * TOK, PART], [1, HB * AST]]), "ld_ob")
                rr_toks = []
                for mb in range(HB):
                    em.need("vector", t_lp, tok_oin, prev_wo)
                    t_r = em.emit("vector", lambda E, mb_=mb:
                                  E.scalar_tensor_tensor(
                                      out=otile[:, mb_ * AST:(mb_ + 1) * AST],
                                      in0=atile[:, mb_ * AST:(mb_ + 1) * AST],
                                      scalar=carry2[:, mb_:mb_ + 1],
                                      in1=otile[:, mb_ * AST:(mb_ + 1) * AST],
                                      op0=OP.mult, op1=OP.add), inc=DV)
                    rr_toks.append(t_r)
                pe_bar = em.now(PE)
                sq_war = None
                for mb in range(HB):
                    em.need("scalar", rr_toks[-1], sq_war)
                    t = em.emit("scalar", lambda E, mb_=mb: E.activation(
                        out=intu[:, 0:AST],
                        in_=otile[:, mb_ * AST:(mb_ + 1) * AST],
                        func=AF.Square), inc=AC)
                    em.need("tensor", t, bank_rd[7] if mb == 0 else None,
                            pe_bar if mb == 0 else None)
                    sq_war = em.emit("tensor", lambda E, mb_=mb: E.matmul(
                        ps[7][0:1, 0:AST], onec_sb[:, :], intu[:, 0:AST],
                        start=(mb_ == 0), stop=(mb_ == HB - 1)), inc=PE)
                em.need("vector", em.now(PE))
                em.emit("vector", lambda E: E.tensor_scalar(
                    out=lnrowA[0:1, 0:AST], in0=ps[7][0:1, 0:AST],
                    scalar1=1.0 / H, scalar2=1e-5, op0=OP.mult, op1=OP.add))
                em.drain("vector")
                tok_r_ = em.emit("vector", lambda E: E.nop(), inc=DV)
                em.need("scalar", tok_r_)
                tok_rs0 = em.emit("scalar", lambda E: E.activation(
                    out=lnrowA[0:1, 0:AST], in_=lnrowA[0:1, 0:AST],
                    func=AF.Sqrt), inc=AC)
                em.need("vector", tok_rs0)
                tok_rs = em.emit("vector", lambda E: E.reciprocal(
                    out=lnrowA[0:1, 0:AST], in_=lnrowA[0:1, 0:AST]), inc=DV)
                em.need("tensor", tok_rs)
                em.emit("tensor", lambda E: E.matmul(
                    ps[7][:, 0:AST], oner_sb[:, :], lnrowA[0:1, 0:AST],
                    start=True, stop=True), inc=PE)
                em.need("vector", em.now(PE))
                tok_R = em.emit("vector", lambda E: E.tensor_copy(
                    out=bcA[:, 0:AST], in_=ps[7][:, 0:AST]), inc=DV)
                bank_rd[7] = tok_R
                sg_toks = []
                for mb in range(HB):
                    r4 = mb % 4
                    bg = 2 if mb % 2 == 0 else 6
                    tok_g = attn_proj(mb, wgT, bg, lnx_tok, WREG[r4],
                                      wreg_rd[r4], mb % 2)
                    wreg_rd[r4] = tok_g
                    em.need("scalar", tok_g, rr_toks[-1])
                    t_sg = em.emit("scalar", lambda E, c_=mb * AST, b_=bg:
                                   E.activation(
                                       out=vtile[:, c_:c_ + AST],
                                       in_=ps[b_][:, 0:AST], func=AF.Silu),
                                   inc=AC)
                    bank_rd[bg] = t_sg
                    sg_toks.append(t_sg)
                gtoks = []
                for mb in range(HB):
                    c = mb * AST
                    em.need("vector", tok_R, sg_toks[mb])
                    em.emit("vector", lambda E, mb_=mb: E.tensor_tensor(
                        out=otile[:, mb_ * AST:(mb_ + 1) * AST],
                        in0=otile[:, mb_ * AST:(mb_ + 1) * AST],
                        in1=bcA[:, 0:AST], op=OP.mult))
                    em.drain("vector")
                    t = em.emit("vector", lambda E, mb_=mb, c_=c:
                                E.scalar_tensor_tensor(
                                    out=otile[:, mb_ * AST:(mb_ + 1) * AST],
                                    in0=otile[:, mb_ * AST:(mb_ + 1) * AST],
                                    scalar=gnw_sb[:, mb_:mb_ + 1],
                                    in1=vtile[:, c_:c_ + AST],
                                    op0=OP.mult, op1=OP.mult), inc=DV)
                    gtoks.append(t)
                for mb in range(HB):
                    ri = 4 + mb % 4
                    bw = 3 if mb % 2 == 0 else 4
                    em.need("sync", wreg_rd[ri])
                    src = RawAP(woT, mb * HB * PART * PART,
                                [[HB * PART, PART], [1, HB * PART]])
                    tk = dma("sync", WREG[ri], src, "ld_w%d" % (2 + mb % 2))
                    em.need("tensor", tk, gtoks[-1], bank_rd[bw])
                    for kb in range(HB):
                        em.emit("tensor", lambda E, kb_=kb, r_=ri, b_=bw:
                                E.matmul(
                                    ps[b_][:, 0:AST],
                                    WREG[r_][:, kb_ * PART:(kb_ + 1) * PART],
                                    otile[:, kb_ * AST:(kb_ + 1) * AST],
                                    start=(kb_ == 0), stop=(kb_ == HB - 1)),
                                inc=PE if kb == HB - 1 else None)
                    tok_wo = em.now(PE)
                    wreg_rd[ri] = tok_wo
                    prev_wo = tok_wo
                    em.need("vector", tok_wo, x_wr.get(mb))
                    last_x = em.emit("vector", lambda E, mb_=mb, s_=st, b_=bw:
                                     E.tensor_tensor(
                                         out=x_sb[:, mb_ * TOK + s_ * AST:
                                                  mb_ * TOK + s_ * AST + AST],
                                         in0=x_sb[:, mb_ * TOK + s_ * AST:
                                                  mb_ * TOK + s_ * AST + AST],
                                         in1=ps[b_][:, 0:AST], op=OP.add),
                                     inc=DV)
                    x_wr[mb] = last_x
                    bank_rd[bw] = last_x
            guwin_rd[0] = guwin_rd[1] = em.now(PE)
            dwin_rd[0] = em.now(PE)
            em.need("vector", em.now(PE))
            em.need("scalar", em.now(PE))
            return last_x

        if cfg.stop_after == "ffn1":
            x_final = x1_tok
        else:
            x2_tok = emit_attn()
            if cfg.stop_after == "attn":
                x_final = x2_tok
            else:
                x_final = emit_ffn()

        em.need("gpsimd", x_final, em.now(DV))
        dst = RawAP(y_out, 0, [[TOK, PART], [PART * TOK, HB], [1, TOK]])
        tok_out = dma("gpsimd", dst, x_sb[:, 0:HB * TOK], "st_y")
        em.need("gpsimd", tok_out)

        with nc.Block() as block:
            @block.sync
            def _(E):
                for f in em.ops["sync"]:
                    f(E)

            @block.tensor
            def _(E):
                for f in em.ops["tensor"]:
                    f(E)

            @block.vector
            def _(E):
                for f in em.ops["vector"]:
                    f(E)

            @block.scalar
            def _(E):
                for f in em.ops["scalar"]:
                    f(E)

            @block.gpsimd
            def _(E):
                for f in em.ops["gpsimd"]:
                    f(E)

    return nc


def make_inputs(cfg, L0_attn_hidden_states, L0_attn_Wi, L0_attn_Wf,
                L0_attn_Wg, L0_attn_Wo, L0_attn_gnorm_w,
                L0_ffn_gate_cp, L0_ffn_up_cp, L0_ffn_down_cp):
    grid_f, grid_f2, grid_f3, jq, sq = host_grid(cfg)
    perm = down_perm(cfg)

    def tab4(a, core):
        sh = a[core * cfg.Qs:(core + 1) * cfg.Qs]
        g = sh.reshape(cfg.QCs // 4, 4, PART, PART).transpose(0, 2, 1, 3)
        return np.ascontiguousarray(g.reshape(cfg.QCs // 4, PART, 512))
    f32 = np.float32
    M = host_M() / cfg.n_cores
    assert cfg.n_cores in (1, 2, 4, 8, 16)
    M_gu = np.zeros((10, 16), f32)
    M_gu[0:5, 0:8] = M
    M_gu[5:10, 8:16] = M
    common = {
        "wiT": tile_lhsT(np.ascontiguousarray(np.asarray(L0_attn_Wi, f32).T)),
        "wfT": tile_lhsT(np.ascontiguousarray(np.asarray(L0_attn_Wf, f32).T)),
        "wgT": tile_lhsT(np.ascontiguousarray(np.asarray(L0_attn_Wg, f32).T)),
        "woT": tile_lhsT(np.ascontiguousarray(np.asarray(L0_attn_Wo, f32).T)),
        "gnw": np.ascontiguousarray(np.asarray(L0_attn_gnorm_w, f32).reshape(cfg.HB, PART).T),
        "cp_g": np.ascontiguousarray(L0_ffn_gate_cp, dtype=f32),
        "cp_u": np.ascontiguousarray(L0_ffn_up_cp, dtype=f32),
        "cp_d": np.ascontiguousarray(L0_ffn_down_cp, dtype=f32),
        "iota": np.ascontiguousarray(
            np.broadcast_to(np.arange(PART, dtype=f32), (PART, PART))),
        "eye": np.eye(PART, dtype=f32),
        "ones_col": np.ones((PART, 1), dtype=f32),
        "ones_row": np.ones((1, PART), dtype=f32),
        "M_gu": M_gu,
        "M_d": np.ascontiguousarray(M),
    }
    x = np.asarray(L0_attn_hidden_states, f32).reshape(cfg.T, cfg.H)
    in_maps = []
    for c in range(cfg.n_cores):
        sl = slice(c * cfg.TOK, (c + 1) * cfg.TOK)
        csel = np.zeros((PART, cfg.n_cores * cfg.HB), dtype=f32)
        if c > 0:
            csel[:, (c - 1) * cfg.HB:c * cfg.HB] = 1.0
        m = dict(common)
        m["x_in"] = np.ascontiguousarray(x[sl].T)
        m["csel"] = csel
        m["ftab"] = tab4(grid_f, c)
        m["f2tab"] = tab4(grid_f2, c)
        m["f3tab"] = tab4(grid_f3, c)
        m["fdtab"] = tab4(grid_f[perm], c)
        m["f2dtab"] = tab4(grid_f2[perm], c)
        m["f3dtab"] = tab4(grid_f3[perm], c)
        m["stab"] = np.ascontiguousarray(
            sq[c * cfg.Qs:(c + 1) * cfg.Qs].reshape(cfg.QCs, PART).T)
        m["sdtab"] = np.ascontiguousarray(
            sq[perm][c * cfg.Qs:(c + 1) * cfg.Qs].reshape(cfg.QCs, PART).T)
        in_maps.append(m)
    return in_maps


def run(cfg, **inputs):
    nc = build_program(cfg)
    in_maps = make_inputs(cfg, **inputs)
    res = run_bass_kernel_spmd(nc, in_maps, core_ids=list(range(cfg.n_cores)))
    shards = [np.asarray(r["y_out"]) for r in res.results]
    out = np.concatenate([s.T for s in shards], axis=0)
    return np.ascontiguousarray(out.reshape(1, cfg.T, cfg.H))


def kernel(**inputs):
    cfg = full_cfg()
    return run(cfg, **inputs)
